# version 02
# baseline (speedup 1.0000x reference)
import os
import numpy as np
import ml_dtypes
from contextlib import ExitStack

PHLIM = float(os.environ.get("PHLIM", "99"))

import concourse.bass as bass
import concourse.mybir as mybir
from concourse.bass_utils import run_bass_kernel_spmd

F32 = mybir.dt.float32
BF16 = mybir.dt.bfloat16
I32 = mybir.dt.int32
AF = mybir.ActivationFunctionType
ALU = mybir.AluOpType
AX = mybir.AxisListType.X

T, D, O3, H = 4096, 1024, 3072, 4096
NT, ND, NH = 32, 8, 32
EPS = np.float32(1e-5)
INV1024 = np.float32(1.0 / 1024.0)
SCL = np.float32(0.015625)
NEGSCL = np.float32(-0.015625)
INV255 = np.float32(1.0 / 255.0)
INV127 = np.float32(1.0 / 127.0)
NEG1 = np.float32(-1.0)
HALFNEG = np.float32(-0.5)
ONEP5 = np.float32(1.5)


def build_nc():
    nc = bass.Bass("TRN2", target_bir_lowering=False)

    x_hbm = nc.dram_tensor("x", [T, D], F32, kind="ExternalInput").ap()
    nw_hbm = nc.dram_tensor("norm_w", [1, D], F32, kind="ExternalInput").ap()
    nb_hbm = nc.dram_tensor("norm_b", [1, D], F32, kind="ExternalInput").ap()
    wq_hbm = nc.dram_tensor("qkv_w", [O3, D], F32, kind="ExternalInput").ap()
    bq_hbm = nc.dram_tensor("qkv_b", [1, O3], F32, kind="ExternalInput").ap()
    pw_hbm = nc.dram_tensor("proj_w", [D, D], F32, kind="ExternalInput").ap()
    pb_hbm = nc.dram_tensor("proj_b", [1, D], F32, kind="ExternalInput").ap()
    w1_hbm = nc.dram_tensor("w1", [H, D], F32, kind="ExternalInput").ap()
    b1_hbm = nc.dram_tensor("b1", [1, H], F32, kind="ExternalInput").ap()
    w2_hbm = nc.dram_tensor("w2", [D, H], F32, kind="ExternalInput").ap()
    b2_hbm = nc.dram_tensor("b2", [1, D], F32, kind="ExternalInput").ap()
    idf_hbm = nc.dram_tensor("identf", [128, 128], F32, kind="ExternalInput").ap()
    out_hbm = nc.dram_tensor("out", [T, D], F32, kind="ExternalOutput").ap()

    g_xb = nc.dram_tensor("g_xb", [T, D], BF16).ap()
    g_xn = nc.dram_tensor("g_xn", [T, D], F32).ap()
    g_xqz = nc.dram_tensor("g_xqz", [T, D], BF16).ap()
    g_q = nc.dram_tensor("g_q", [T, D], BF16).ap()
    g_k = nc.dram_tensor("g_k", [T, D], BF16).ap()
    g_vT = nc.dram_tensor("g_vT", [D, T], BF16).ap()
    g_w = nc.dram_tensor("g_w", [D, D], BF16).ap()
    g_s = nc.dram_tensor("g_s", [T, D], F32).ap()
    g_gh = nc.dram_tensor("g_gh", [H, T], F32).ap()
    g_wq = nc.dram_tensor("g_wq", [O3, D], BF16).ap()
    g_pw = nc.dram_tensor("g_pw", [D, D], BF16).ap()
    g_w1 = nc.dram_tensor("g_w1", [H, D], BF16).ap()
    g_w2 = nc.dram_tensor("g_w2", [D, H], BF16).ap()
    g_bg = nc.dram_tensor("g_bg", [64, 128], BF16).ap()
    cc1_in = nc.dram_tensor("cc1_in", [1, 2], F32).ap()
    cc1_out = nc.dram_tensor("cc1_out", [1, 2], F32).ap()
    cc2_in = nc.dram_tensor("cc2_in", [1, 2], F32).ap()
    cc2_out = nc.dram_tensor("cc2_out", [1, 2], F32).ap()

    C = {}

    def cinc(n, k=1):
        C[n] = C.get(n, 0) + k
        return C[n]

    def cget(n):
        return C.get(n, 0)

    with ExitStack() as top:
        sb = lambda n, s, d: top.enter_context(nc.sbuf_tensor(n, s, d))
        sem = lambda n: top.enter_context(nc.semaphore(n))

        c_idf = sb("c_idf", [128, 128], F32)
        ones_f = sb("ones_f", [1, 128], F32)
        ones_b = sb("ones_b", [1, 128], BF16)
        c_nwnb = sb("c_nwnb", [128, 16], BF16)
        c_b1c = sb("c_b1c", [128, NH], F32)
        r_pb16 = sb("r_pb16", [1, D], BF16)
        r_b2 = sb("r_b2", [1, D], F32)
        r_b2r = sb("r_b2r", [1, D], F32)
        r_b2r16 = sb("r_b2r16", [1, D], BF16)
        bg16 = sb("bg16", [2, T], BF16)
        grow16 = sb("grow16", [1, T], BF16)
        st_sum = sb("st_sum", [128, NT], F32)
        st_ssq = sb("st_ssq", [128, NT], F32)
        st_mnx = sb("st_mnx", [128, NT], F32)
        st_mxx = sb("st_mxx", [128, NT], F32)
        st_mu = sb("st_mu", [128, NT], F32)
        st_mu2 = sb("st_mu2", [128, NT], F32)
        st_r1 = sb("st_r1", [128, NT], F32)
        st_r2 = sb("st_r2", [128, NT], F32)
        st_al = sb("st_al", [128, NT], F32)
        st_bg = sb("st_bg", [128, 2 * NT], F32)
        st_t = sb("st_t", [128, 256], F32)
        st_rr = sb("st_rr", [128, ND], F32)
        nm_col = sb("nm_col", [128, ND], F32)
        gmn = sb("gmn", [128, 64], F32)
        gmx = sb("gmx", [128, 64], F32)
        scal = sb("scal", [1, 32], F32)
        scl_b = sb("scl_b", [128, 8], F32)
        cc_sb = sb("cc_sb", [1, 2], F32)
        crow = sb("crow", [2, O3], F32)
        crow16 = sb("crow16", [2, O3], BF16)
        r_qkvb = sb("r_qkvb", [2, O3], F32)
        bgst16 = sb("bgst16", [64, 128], BF16)

        VS = sem("VS")
        SS = sem("SS")
        MMC = sem("MMC")
        MMQ = sem("MMQ")
        MMV = sem("MMV")
        MMS = sem("MMS")
        MM5 = sem("MM5")
        MM8 = sem("MM8")
        MM9 = sem("MM9")
        MM0 = sem("MM0")
        FR3 = sem("FR3")
        FR3V = sem("FR3V")
        CCS = sem("CCS")

        _streams = {}

        def _stream(name):
            if name not in _streams:
                _streams[name] = [top.enter_context(nc.semaphore(name)), 0]
            return _streams[name]

        def dput(name, eng, out, in_, **k):
            s = _stream(name)
            eng.dma_start(out=out, in_=in_, **k).then_inc(s[0], 16)
            s[1] += 16
            return s[1]

        def dtot(name):
            return _stream(name)[1]

        def dwait(eng, name, tok):
            if tok:
                eng.wait_ge(_stream(name)[0], tok)

        from concourse.bass_types import SBTensorHandle, PSumTensorHandle

        def _ap(x):
            if isinstance(x, (SBTensorHandle, PSumTensorHandle)):
                return x[:, :]
            return x

        def vi(op, *a, **k):
            a = tuple(_ap(x) for x in a)
            inst = getattr(nc.vector, op)(*a, **k)
            prev = cget("VS")
            if prev:
                inst._wait_ge(VS, prev)
            inst.then_inc(VS, 1)
            cinc("VS")
            return inst

        def si(op, *a, **k):
            a = tuple(_ap(x) for x in a)
            inst = getattr(nc.scalar, op)(*a, **k)
            prev = cget("SS")
            if prev:
                inst._wait_ge(SS, prev)
            inst.then_inc(SS, 1)
            cinc("SS")
            return inst

        def ts(out, in0, s1, s2=None, op0=ALU.mult, op1=None):
            if isinstance(s1, np.floating):
                s1 = float(s1)
            if isinstance(s2, np.floating):
                s2 = float(s2)
            if op1 is None:
                return vi("tensor_scalar", out, in0, s1, None, op0)
            return vi("tensor_scalar", out, in0, s1, s2, op0, op1)

        def mm(*a, **k):
            return nc.tensor.matmul(*a, **k)

        LD = ["LA", "LB"]
        WS = ["WA", "WB"]
        VSL = ["VA", "VB"]
        SG = ["SA", "SB"]
        S4 = ["S4A", "S4B", "S4C", "S4D"]
        S2 = ["S2A", "S2B"]
        GH = ["GA", "GB"]
        OUTS = ["OA", "OB"]

        if PHLIM < 0.5:
            with nc.Block("pnull") as bk:
                def fv(v):
                    v.memset(scal[:, :], 0.0)
                bk.vector(fv)
            return nc
        with ExitStack() as ps:
            p_tr = ps.enter_context(nc.psum_tensor("p_tr", [128, 64], F32))
            p_bc = ps.enter_context(nc.psum_tensor("p_bc", [128, 1024], F32))
            nwrow = ps.enter_context(nc.sbuf_tensor("nwrow", [1, D], F32))
            pbrow = ps.enter_context(nc.sbuf_tensor("pbrow", [1, D], F32))
            c_nwb = ps.enter_context(nc.sbuf_tensor("c_nwb", [128, D], F32))
            xld = ps.enter_context(nc.sbuf_tensor("xld", [128, 2 * D], F32))
            strash = ps.enter_context(nc.sbuf_tensor("strash", [128, D], F32))
            wq16 = ps.enter_context(nc.sbuf_tensor("wq16", [128, 2 * 2048], BF16))
            tbi = ps.enter_context(nc.sbuf_tensor("tbi", [128, 2048], I32))
            w2ld = ps.enter_context(nc.sbuf_tensor("w2ld", [128, 2 * 2048], F32))
            xnst = ps.enter_context(nc.sbuf_tensor("xnst", [128, 2 * D], F32))
            xbst = ps.enter_context(nc.sbuf_tensor("xbst", [128, 2 * D], BF16))

            with nc.Block("p0m") as bk:
                def fv(v):
                    v.memset(ones_f[:, :], 1.0)
                    v.memset(ones_b[:, :], 1.0)
                    v.memset(r_qkvb[0:1, :], 0.0)
                    v.memset(scal[:, :], 0.0)
                bk.vector(fv)

            if PHLIM < 0.7:
                return nc
            with nc.Block("p0") as bk:
                def fsy(sy):
                    dput("TX", sy, c_idf[:, :], idf_hbm)
                    dput("TX", sy, r_qkvb[1:2, :], bq_hbm)
                    dput("TX", sy, r_b2[:, :], b2_hbm)
                    dput("TX", sy, nwrow[:, :], nw_hbm)
                    dput("TX", sy, pbrow[:, :], pb_hbm)
                    dput("TX", sy, strash[:, 0:8],
                         nw_hbm.rearrange("o (c p) -> (o p) c", p=128),
                         allow_slow_non_contiguous=True)
                    dput("TX", sy, strash[:, 8:16],
                         nb_hbm.rearrange("o (c p) -> (o p) c", p=128),
                         allow_slow_non_contiguous=True)
                    dput("TX", sy, c_b1c[:, :],
                         b1_hbm.rearrange("o (c p) -> (o p) c", p=128),
                         allow_slow_non_contiguous=True)
                bk.sync(fsy)
                tx_p0 = dtot("TX")

                def ft(t):
                    dwait(t, "TX", tx_p0)
                    mm(p_bc[:, 0:512], ones_f[:, :], nwrow[0:1, 0:512],
                       start=True, stop=True)
                    mm(p_bc[:, 512:1024], ones_f[:, :], nwrow[0:1, 512:1024],
                       start=True, stop=True).then_inc(MM0, 1)
                    cinc("MM0")
                bk.tensor(ft)
                mm0_a = cget("MM0")

            if PHLIM < 0.9:
                return nc
            with nc.Block("p0b") as bk:
                def fv(v):
                    v.wait_ge(MM0, mm0_a)
                    dwait(v, "TX", tx_p0)
                    il = c_nwnb[:, :].rearrange("p (a two) -> p two a", two=2)
                    vi("tensor_copy", il[:, 0, :], strash[:, 0:8])
                    vi("tensor_copy", il[:, 1, :], strash[:, 8:16])
                    vi("tensor_copy", c_nwb[:, :], p_bc[:, :])
                    vi("tensor_copy", r_pb16[:, :], pbrow[:, :])
                bk.vector(fv)

            if PHLIM < 1:
                return nc
            with nc.Block("p1a") as bk:
                vs1 = cget("VS")
                ss1 = cget("SS")
                toks = []

                def fsy(sy):
                    for i in range(NT):
                        if i >= 2:
                            sy.wait_ge(VS, vs1 + 2 * (i - 1))
                            sy.wait_ge(SS, ss1 + 2 * (i - 1))
                        toks.append(dput(LD[i % 2], sy,
                                         xld[:, (i % 2) * D:(i % 2 + 1) * D],
                                         x_hbm[i * 128:(i + 1) * 128, :]))
                bk.sync(fsy)

                def fv(v):
                    for i in range(NT):
                        dwait(v, LD[i % 2], toks[i])
                        sl = xld[:, (i % 2) * D:(i % 2 + 1) * D]
                        vi("tensor_reduce", st_mnx[:, i:i + 1], sl, AX, ALU.min)
                        vi("reduce_max", st_mxx[:, i:i + 1], sl, AX)
                bk.vector(fv)

                def fs(s):
                    for i in range(NT):
                        dwait(s, LD[i % 2], toks[i])
                        sl = xld[:, (i % 2) * D:(i % 2 + 1) * D]
                        si("activation", strash[:, :], sl, AF.Copy,
                           accum_out=st_sum[:, i:i + 1])
                        si("activation", strash[:, :], sl, AF.Square,
                           accum_out=st_ssq[:, i:i + 1])
                bk.scalar(fs)

            if PHLIM < 2:
                return nc
            with nc.Block("p2a") as bk:
                def fv(v):
                    ts(st_mu, st_sum, INV1024)
                    ts(st_t[:, 0:32], st_ssq, INV1024)
                    vi("tensor_mul", st_t[:, 32:64], st_mu, st_mu)
                    vi("tensor_sub", st_t[:, 0:32], st_t[:, 0:32], st_t[:, 32:64])
                    ts(st_t[:, 32:64], st_t[:, 0:32], EPS, op0=ALU.add)
                bk.vector(fv)

            with nc.Block("p2b") as bk:
                def fs(s):
                    si("activation", st_t[:, 64:96], st_t[:, 32:64], AF.Sqrt)
                bk.scalar(fs)

            with nc.Block("p2c") as bk:
                def fv(v):
                    vi("reciprocal", st_t[:, 96:128], st_t[:, 64:96])
                    vi("tensor_mul", st_t[:, 128:160], st_t[:, 96:128],
                       st_t[:, 96:128])
                    vi("tensor_mul", st_t[:, 128:160], st_t[:, 128:160],
                       st_t[:, 32:64])
                    ts(st_t[:, 128:160], st_t[:, 128:160], HALFNEG, ONEP5,
                       ALU.mult, ALU.add)
                    vi("tensor_mul", st_r1, st_t[:, 96:128], st_t[:, 128:160])
                    ts(st_t[:, 160:192], st_mu, np.float32(-1024.0))
                    vi("tensor_add", st_t[:, 160:192], st_t[:, 160:192], st_sum)
                    ts(st_t[:, 160:192], st_t[:, 160:192], INV1024)
                    vi("tensor_mul", st_mu2, st_t[:, 160:192], st_r1)
                    vi("tensor_mul", st_t[:, 160:192], st_r1, st_r1)
                    vi("tensor_mul", st_t[:, 160:192], st_t[:, 160:192],
                       st_t[:, 0:32])
                    vi("tensor_mul", st_t[:, 192:224], st_mu2, st_mu2)
                    vi("tensor_sub", st_t[:, 160:192], st_t[:, 160:192],
                       st_t[:, 192:224])
                    ts(st_t[:, 160:192], st_t[:, 160:192], EPS, op0=ALU.add)
                bk.vector(fv)

            with nc.Block("p2d") as bk:
                def fs(s):
                    si("activation", st_t[:, 192:224], st_t[:, 160:192], AF.Sqrt)
                bk.scalar(fs)

            with nc.Block("p2e") as bk:
                def fv(v):
                    vi("reciprocal", st_t[:, 224:256], st_t[:, 192:224])
                    vi("tensor_mul", st_t[:, 64:96], st_t[:, 224:256],
                       st_t[:, 224:256])
                    vi("tensor_mul", st_t[:, 64:96], st_t[:, 64:96],
                       st_t[:, 160:192])
                    ts(st_t[:, 64:96], st_t[:, 64:96], HALFNEG, ONEP5,
                       ALU.mult, ALU.add)
                    vi("tensor_mul", st_r2, st_t[:, 224:256], st_t[:, 64:96])
                    vi("tensor_mul", st_al, st_r1, st_r2)
                    vi("reciprocal", st_t[:, 96:128], st_al)
                    vi("tensor_mul", st_t[:, 128:160], st_al, st_mu)
                    vi("tensor_mul", st_t[:, 160:192], st_r2, st_mu2)
                    vi("tensor_add", st_t[:, 128:160], st_t[:, 128:160],
                       st_t[:, 160:192])
                    ts(st_t[:, 128:160], st_t[:, 128:160], NEG1)
                    vi("tensor_mul", st_t[:, 128:160], st_t[:, 128:160],
                       st_t[:, 96:128])
                    il = st_bg[:, :].rearrange("p (a two) -> p two a", two=2)
                    vi("tensor_copy", il[:, 0, :], st_t[:, 128:160])
                    vi("tensor_copy", il[:, 1, :], st_t[:, 96:128])
                    vi("tensor_sub", st_t[:, 160:192], st_mnx[:, :], st_mu[:, :])
                    vi("tensor_mul", st_t[:, 160:192], st_t[:, 160:192],
                       st_r1[:, :])
                    vi("tensor_sub", st_t[:, 192:224], st_mxx[:, :], st_mu[:, :])
                    vi("tensor_mul", st_t[:, 192:224], st_t[:, 192:224],
                       st_r1[:, :])
                    vi("tensor_reduce", st_t[:, 224:225], st_t[:, 160:192], AX,
                       ALU.min)
                    vi("reduce_max", st_t[:, 225:226], st_t[:, 192:224], AX)
                    ts(st_t[:, 224:225], st_t[:, 224:225], NEG1)
                bk.vector(fv)
            vs_p2e = cget("VS")

            with nc.Block("p2f") as bk:
                def ft(t):
                    t.wait_ge(VS, vs_p2e)
                    t.transpose(p_bc[0:64, 0:128], st_bg[:, :], c_idf[:, :])
                    t.transpose(p_bc[0:1, 128:256], st_t[:, 225:226], c_idf[:, :])
                    t.transpose(p_bc[0:1, 256:384], st_t[:, 224:225],
                                c_idf[:, :]).then_inc(MM0, 1)
                    cinc("MM0")
                bk.tensor(ft)
                mm0_f = cget("MM0")

                def fv(v):
                    v.wait_ge(MM0, mm0_f)
                    vi("tensor_copy", bgst16[:, :], p_bc[0:64, 0:128])
                    vi("reduce_max", cc_sb[0:1, 0:1], p_bc[0:1, 128:256], AX)
                    vi("reduce_max", cc_sb[0:1, 1:2], p_bc[0:1, 256:384], AX)
                bk.vector(fv)
            vs_p2f = cget("VS")

            with nc.Block("p2g") as bk:
                def fg(g):
                    g.wait_ge(VS, vs_p2f)
                    dput("CC", g, g_bg, bgst16[:, :])
                    dput("CC", g, cc1_in, cc_sb[0:1, 0:2])
                    dwait(g, "CC", dtot("CC"))
                    g.collective_compute(
                        "AllReduce", ALU.max, replica_groups=[list(range(8))],
                        ins=[cc1_in], outs=[cc1_out]).then_inc(CCS, 1)
                    cinc("CCS")
                bk.gpsimd(fg)
                cc_bg = dtot("CC")

                def fsy(sy):
                    dwait(sy, "CC", cc_bg)
                    bgr = g_bg.rearrange("(a two) j -> two a j", two=2)
                    dput("BG", sy, bg16[:, :], bgr)
                    dput("BG", sy, grow16[:, :], bgr[1:2, :, :])
                bk.sync(fsy)
            bg_tok = dtot("BG")

            if PHLIM < 3:
                return nc
            def cast_pipe(bk, src, dst, ntiles):
                vsb = cget("VS")
                ltoks, stoks = [], []
                sg_base = [dtot(SG[0]), dtot(SG[1])]

                def fsy(sy):
                    for i in range(ntiles):
                        if i >= 2:
                            sy.wait_ge(VS, vsb + (i - 1))
                        ltoks.append(dput(LD[i % 2], sy,
                                          xld[:, (i % 2) * D:(i % 2 + 1) * D],
                                          src[i * 128:(i + 1) * 128, :]))
                bk.sync(fsy)

                def fg(g):
                    for i in range(ntiles):
                        g.wait_ge(VS, vsb + (i + 1))
                        stoks.append(dput(SG[i % 2], g,
                                          dst[i * 128:(i + 1) * 128, :],
                                          wq16[:, (i % 2) * 2048:
                                               (i % 2) * 2048 + 1024]))
                bk.gpsimd(fg)

                def fv(v):
                    for i in range(ntiles):
                        dwait(v, LD[i % 2], ltoks[i])
                        if i >= 2:
                            dwait(v, SG[i % 2], stoks[i - 2])
                        else:
                            dwait(v, SG[i % 2], sg_base[i % 2])
                        vi("tensor_copy",
                           wq16[:, (i % 2) * 2048:(i % 2) * 2048 + 1024],
                           xld[:, (i % 2) * D:(i % 2 + 1) * D])
                bk.vector(fv)

            with nc.Block("p2wq") as bk:
                cast_pipe(bk, wq_hbm, g_wq, O3 // 128)
            with nc.Block("p2wp") as bk:
                cast_pipe(bk, pw_hbm, g_pw, D // 128)

            with nc.Block("p2wm") as bk:
                vsb = cget("VS")
                ltoks = []

                def fsy(sy):
                    for i in range(NH):
                        if i >= 2:
                            sy.wait_ge(VS, vsb + (i - 1))
                        ltoks.append(dput(LD[i % 2], sy,
                                          xld[:, (i % 2) * D:(i % 2 + 1) * D],
                                          w1_hbm[i * 128:(i + 1) * 128, :]))
                    for i in range(16):
                        if i >= 2:
                            sy.wait_ge(VS, vsb + NH + (i - 1))
                        else:
                            sy.wait_ge(VS, vsb + NH)
                        ltoks.append(dput(
                            LD[i % 2], sy,
                            w2ld[:, (i % 2) * 2048:(i % 2 + 1) * 2048],
                            w2_hbm[(i // 2) * 128:(i // 2 + 1) * 128,
                                   (i % 2) * 2048:(i % 2 + 1) * 2048]))
                bk.sync(fsy)

                def fv(v):
                    for i in range(NH):
                        dwait(v, LD[i % 2], ltoks[i])
                        vi("reduce_max", st_t[:, 64 + i:65 + i],
                           xld[:, (i % 2) * D:(i % 2 + 1) * D],
                           AX, apply_absolute_value=True)
                    for i in range(16):
                        dwait(v, LD[i % 2], ltoks[NH + i])
                        vi("reduce_max", st_t[:, 96 + i:97 + i],
                           w2ld[:, (i % 2) * 2048:(i % 2 + 1) * 2048],
                           AX, apply_absolute_value=True)
                    vi("reduce_max", st_t[:, 112:113], st_t[:, 64:96], AX)
                    vi("reduce_max", st_t[:, 113:114], st_t[:, 96:112], AX)
                bk.vector(fv)
            vs_wm = cget("VS")

            with nc.Block("p2wn") as bk:
                def ft(t):
                    t.wait_ge(VS, vs_wm)
                    t.transpose(p_bc[0:1, 0:128], st_t[:, 112:113], c_idf[:, :])
                    t.transpose(p_bc[0:1, 128:256], st_t[:, 113:114],
                                c_idf[:, :]).then_inc(MM0, 1)
                    cinc("MM0")
                bk.tensor(ft)
                mm0_w = cget("MM0")

                def fv(v):
                    v.wait_ge(MM0, mm0_w)
                    vi("reduce_max", scal[0:1, 4:5], p_bc[0:1, 0:128], AX)
                    vi("reduce_max", scal[0:1, 5:6], p_bc[0:1, 128:256], AX)
                    ts(scal[0:1, 0:1], scal[0:1, 4:5], INV127)
                    ts(scal[0:1, 0:1], scal[0:1, 0:1], np.float32(1e-8),
                       op0=ALU.max)
                    vi("reciprocal", scal[0:1, 1:2], scal[0:1, 0:1])
                    ts(scal[0:1, 2:3], scal[0:1, 5:6], INV127)
                    ts(scal[0:1, 2:3], scal[0:1, 2:3], np.float32(1e-8),
                       op0=ALU.max)
                    vi("reciprocal", scal[0:1, 3:4], scal[0:1, 2:3])
                bk.vector(fv)
            vs_wn = cget("VS")

            with nc.Block("p2wo") as bk:
                def ft(t):
                    t.wait_ge(VS, vs_wn)
                    mm(p_bc[:, 0:1], ones_f[:, :], scal[0:1, 1:2],
                       start=True, stop=True)
                    mm(p_bc[:, 2:3], ones_f[:, :], scal[0:1, 3:4],
                       start=True, stop=True).then_inc(MM0, 1)
                    cinc("MM0")
                bk.tensor(ft)
                mm0_o = cget("MM0")

                def fv(v):
                    v.wait_ge(MM0, mm0_o)
                    vi("tensor_copy", scl_b[:, 0:1], p_bc[:, 0:1])
                    vi("tensor_copy", scl_b[:, 1:2], p_bc[:, 2:3])
                bk.vector(fv)

            def quant_pipe(bk, src, dst, ntiles, wide, rcol):
                vsb = cget("VS")
                ltoks, stoks = [], []
                sg_base = [dtot(SG[0]), dtot(SG[1])]

                def fsy(sy):
                    for i in range(ntiles):
                        if i >= 2:
                            sy.wait_ge(VS, vsb + 2 * (i - 1))
                        if not wide:
                            ltoks.append(dput(
                                LD[i % 2], sy,
                                xld[:, (i % 2) * D:(i % 2 + 1) * D],
                                src[i * 128:(i + 1) * 128, :]))
                        else:
                            ltoks.append(dput(
                                LD[i % 2], sy,
                                w2ld[:, (i % 2) * 2048:(i % 2 + 1) * 2048],
                                src[(i // 2) * 128:(i // 2 + 1) * 128,
                                    (i % 2) * 2048:(i % 2 + 1) * 2048]))
                bk.sync(fsy)

                def fg(g):
                    for i in range(ntiles):
                        g.wait_ge(VS, vsb + 2 * (i + 1))
                        if not wide:
                            stoks.append(dput(SG[i % 2], g,
                                              dst[i * 128:(i + 1) * 128, :],
                                              wq16[:, (i % 2) * 2048:
                                                   (i % 2) * 2048 + 1024]))
                        else:
                            stoks.append(dput(
                                SG[i % 2], g,
                                dst[(i // 2) * 128:(i // 2 + 1) * 128,
                                    (i % 2) * 2048:(i % 2 + 1) * 2048],
                                wq16[:, (i % 2) * 2048:(i % 2 + 1) * 2048]))
                bk.gpsimd(fg)

                def fv(v):
                    for i in range(ntiles):
                        dwait(v, LD[i % 2], ltoks[i])
                        if i >= 2:
                            dwait(v, SG[i % 2], stoks[i - 2])
                        else:
                            dwait(v, SG[i % 2], sg_base[i % 2])
                        if not wide:
                            srcsl = xld[:, (i % 2) * D:(i % 2 + 1) * D]
                            isl = tbi[:, 0:1024]
                            osl = wq16[:, (i % 2) * 2048:(i % 2) * 2048 + 1024]
                        else:
                            srcsl = w2ld[:, (i % 2) * 2048:(i % 2 + 1) * 2048]
                            isl = tbi[:, 0:2048]
                            osl = wq16[:, (i % 2) * 2048:(i % 2 + 1) * 2048]
                        ts(isl, srcsl, rcol, np.float32(-127.0), ALU.mult,
                           ALU.max)
                        ts(osl, isl, np.float32(127.0), op0=ALU.min)
                bk.vector(fv)

            with nc.Block("p2q1") as bk:
                quant_pipe(bk, w1_hbm, g_w1, NH, False, scl_b[:, 0:1])
            with nc.Block("p2q2") as bk:
                quant_pipe(bk, w2_hbm, g_w2, 16, True, scl_b[:, 1:2])

            if PHLIM < 4:
                return nc
            with nc.Block("p1b") as bk:
                vsb = cget("VS")
                ltoks, stoks = [], []
                sg_base = [dtot(SG[0]), dtot(SG[1])]

                def fsy(sy):
                    for i in range(NT):
                        if i >= 2:
                            sy.wait_ge(VS, vsb + 2 * (i - 1))
                        ltoks.append(dput(LD[i % 2], sy,
                                          xld[:, (i % 2) * D:(i % 2 + 1) * D],
                                          x_hbm[i * 128:(i + 1) * 128, :]))
                bk.sync(fsy)

                def fg(g):
                    for i in range(NT):
                        g.wait_ge(VS, vsb + 2 * (i + 1))
                        dput(SG[i % 2], g, g_xn[i * 128:(i + 1) * 128, :],
                             xnst[:, (i % 2) * D:(i % 2 + 1) * D])
                        stoks.append(dput(SG[i % 2], g,
                                          g_xb[i * 128:(i + 1) * 128, :],
                                          xbst[:, (i % 2) * D:(i % 2 + 1) * D]))
                bk.gpsimd(fg)

                def fv(v):
                    for i in range(NT):
                        dwait(v, LD[i % 2], ltoks[i])
                        if i >= 2:
                            dwait(v, SG[i % 2], stoks[i - 2])
                        else:
                            dwait(v, SG[i % 2], sg_base[i % 2])
                        sl = xld[:, (i % 2) * D:(i % 2 + 1) * D]
                        ts(xnst[:, (i % 2) * D:(i % 2 + 1) * D], sl,
                           st_mu[:, i:i + 1], st_r1[:, i:i + 1],
                           ALU.subtract, ALU.mult)
                        vi("tensor_mul", xbst[:, (i % 2) * D:(i % 2 + 1) * D],
                           sl, c_nwb[:, :])
                bk.vector(fv)
            sg_p1b = [dtot(SG[0]), dtot(SG[1])]

        if PHLIM < 5:
            return nc
        with ExitStack() as ps:
            xbT = ps.enter_context(nc.sbuf_tensor("xbT", [128, ND * T], BF16))
            wbuf = ps.enter_context(nc.sbuf_tensor("wbuf", [128, 2 * 4096], BF16))
            vwb = ps.enter_context(nc.sbuf_tensor("vwb", [128, 2 * 1024], BF16))
            qkst = ps.enter_context(nc.sbuf_tensor("qkst", [128, 4 * 512], BF16))
            vst = ps.enter_context(nc.sbuf_tensor("vst", [128, 2 * 512], BF16))
            p_qk = ps.enter_context(nc.psum_tensor("p_qk", [128, 2 * 512], F32))
            p_v = ps.enter_context(nc.psum_tensor("p_v", [128, 2 * 512], F32))
            p_c = ps.enter_context(nc.psum_tensor("p_c", [2, 512], F32))

            with nc.Block("p3") as bk:
                vs3, ss3 = cget("VS"), cget("SS")
                txk, wtoks, vtoks = [], [], []
                qkstoks, vstoks = [], []

                def fsy(sy):
                    dwait(sy, SG[0], sg_p1b[0])
                    dwait(sy, SG[1], sg_p1b[1])
                    for ic in range(ND):
                        txk.append(dput("TX", sy, xbT[:, ic * T:(ic + 1) * T],
                                        g_xb[0:T, ic * 128:(ic + 1) * 128],
                                        transpose=True))
                    for oc in range(4):
                        if oc >= 2:
                            sy.wait_ge(FR3, oc - 1)
                        for ic in range(ND):
                            wtoks.append(dput(
                                WS[oc % 2], sy,
                                wbuf[:, (oc % 2) * 4096 + ic * 512:
                                     (oc % 2) * 4096 + (ic + 1) * 512],
                                g_wq[oc * 512:(oc + 1) * 512,
                                     ic * 128:(ic + 1) * 128], transpose=True))
                    for ot in range(ND):
                        if ot >= 2:
                            sy.wait_ge(FR3V, ot - 1)
                        for ic in range(ND):
                            vtoks.append(dput(
                                VSL[ot % 2], sy,
                                vwb[:, (ot % 2) * 1024 + ic * 128:
                                    (ot % 2) * 1024 + (ic + 1) * 128],
                                g_wq[2048 + ot * 128:2048 + (ot + 1) * 128,
                                     ic * 128:(ic + 1) * 128], transpose=True))
                bk.sync(fsy)

                def fg(g_):
                    for g in range(128):
                        oc, i = divmod(g, NT)
                        g_.wait_ge(SS, ss3 + g + 1)
                        dstt = g_q if oc < 2 else g_k
                        col = (oc % 2) * 512
                        qkstoks.append(dput(
                            S4[g % 4], g_,
                            dstt[i * 128:(i + 1) * 128, col:col + 512],
                            qkst[:, (g % 4) * 512:(g % 4 + 1) * 512]))
                    for g in range(64):
                        ot, tc = divmod(g, ND)
                        g_.wait_ge(VS, vs3 + 10 * ot + tc + 11)
                        vstoks.append(dput(
                            S2[g % 2], g_,
                            g_vT[ot * 128:(ot + 1) * 128,
                                 tc * 512:(tc + 1) * 512],
                            vst[:, (g % 2) * 512:(g % 2 + 1) * 512]))
                bk.gpsimd(fg)

                def ft(t):
                    dwait(t, "TX", txk[-1])
                    dwait(t, "BG", bg_tok)
                    for oc in range(4):
                        dwait(t, WS[oc % 2], wtoks[oc * ND + ND - 1])
                        if oc >= 1:
                            t.wait_ge(VS, vs3 + 2 * oc)
                        inst = None
                        for ic in range(ND):
                            inst = mm(p_c[:, :], c_nwnb[:, 2 * ic:2 * ic + 2],
                                      wbuf[:, (oc % 2) * 4096 + ic * 512:
                                           (oc % 2) * 4096 + (ic + 1) * 512],
                                      start=(ic == 0), stop=(ic == ND - 1))
                        inst.then_inc(MMC, 1)
                        cinc("MMC")
                        for i in range(NT):
                            g = oc * NT + i
                            if g >= 2:
                                t.wait_ge(SS, ss3 + (g - 1))
                            sl = p_qk[:, (g % 2) * 512:(g % 2 + 1) * 512]
                            for ic in range(ND):
                                wmm = mm(
                                    sl,
                                    xbT[:, ic * T + i * 128:ic * T + (i + 1) * 128],
                                    wbuf[:, (oc % 2) * 4096 + ic * 512:
                                         (oc % 2) * 4096 + (ic + 1) * 512],
                                    start=(ic == 0), stop=False)
                            if i == NT - 1:
                                wmm.then_inc(FR3, 1)
                                cinc("FR3")
                            inst = mm(sl, bg16[:, i * 128:(i + 1) * 128],
                                      crow16[:, oc * 512:(oc + 1) * 512],
                                      start=False, stop=True)
                            inst._wait_ge(VS, vs3 + 2 * (oc + 1))
                            inst.then_inc(MMQ, 1)
                            cinc("MMQ")
                    for ot in range(ND):
                        dwait(t, VSL[ot % 2], vtoks[ot * ND + ND - 1])
                        t.wait_ge(VS, (vs3 + 8) if ot == 0 else (vs3 + 10 * ot))
                        inst = None
                        for ic in range(ND):
                            inst = mm(p_c[:, 0:128], c_nwnb[:, 2 * ic:2 * ic + 2],
                                      vwb[:, (ot % 2) * 1024 + ic * 128:
                                          (ot % 2) * 1024 + (ic + 1) * 128],
                                      start=(ic == 0), stop=(ic == ND - 1))
                        inst.then_inc(MMC, 1)
                        cinc("MMC")
                        for tc in range(ND):
                            g = ot * ND + tc
                            if g >= 2:
                                otp, tcp = divmod(g - 2, ND)
                                t.wait_ge(VS, vs3 + 10 * otp + tcp + 11)
                            sl = p_v[:, (g % 2) * 512:(g % 2 + 1) * 512]
                            for ic in range(ND):
                                wmm = mm(
                                    sl,
                                    vwb[:, (ot % 2) * 1024 + ic * 128:
                                        (ot % 2) * 1024 + (ic + 1) * 128],
                                    xbT[:, ic * T + tc * 512:
                                        ic * T + (tc + 1) * 512],
                                    start=(ic == 0), stop=False)
                            if tc == ND - 1:
                                wmm.then_inc(FR3V, 1)
                                cinc("FR3V")
                            inst = mm(sl,
                                      crow16[:, 2048 + ot * 128:
                                             2048 + (ot + 1) * 128],
                                      bg16[:, tc * 512:(tc + 1) * 512],
                                      start=False, stop=True)
                            inst._wait_ge(VS, vs3 + 10 * ot + 10)
                            inst.then_inc(MMV, 1)
                            cinc("MMV")
                bk.tensor(ft)

                def fs(s):
                    for g in range(128):
                        oc, i = divmod(g, NT)
                        s.wait_ge(MMQ, g + 1)
                        if g >= 4:
                            dwait(s, S4[g % 4], qkstoks[g - 4])
                        si("activation", qkst[:, (g % 4) * 512:(g % 4 + 1) * 512],
                           p_qk[:, (g % 2) * 512:(g % 2 + 1) * 512], AF.Copy,
                           scale=st_al[:, i:i + 1])
                bk.scalar(fs)

                def fv(v):
                    for oc in range(4):
                        v.wait_ge(MMC, oc + 1)
                        vi("tensor_add", crow[:, oc * 512:(oc + 1) * 512],
                           p_c[:, :], r_qkvb[:, oc * 512:(oc + 1) * 512])
                        vi("tensor_copy", crow16[:, oc * 512:(oc + 1) * 512],
                           crow[:, oc * 512:(oc + 1) * 512])
                    for ot in range(ND):
                        v.wait_ge(MMC, 4 + ot + 1)
                        vi("tensor_add", crow[:, 2048 + ot * 128:
                                              2048 + (ot + 1) * 128],
                           p_c[:, 0:128], r_qkvb[:, 2048 + ot * 128:
                                                 2048 + (ot + 1) * 128])
                        vi("tensor_copy", crow16[:, 2048 + ot * 128:
                                                 2048 + (ot + 1) * 128],
                           crow[:, 2048 + ot * 128:2048 + (ot + 1) * 128])
                        for tc in range(ND):
                            g = ot * ND + tc
                            v.wait_ge(MMV, g + 1)
                            if g >= 2:
                                dwait(v, S2[g % 2], vstoks[g - 2])
                            vi("tensor_copy",
                               vst[:, (g % 2) * 512:(g % 2 + 1) * 512],
                               p_v[:, (g % 2) * 512:(g % 2 + 1) * 512])
                bk.vector(fv)
            s4_p3 = [dtot(x) for x in S4]
            s2_p3 = [dtot(x) for x in S2]

        if PHLIM < 6:
            return nc
        with ExitStack() as ps:
            xnld = ps.enter_context(nc.sbuf_tensor("xnld", [128, 2 * D], F32))
            xqst = ps.enter_context(nc.sbuf_tensor("xqst", [128, 2 * D], BF16))
            tbi4 = ps.enter_context(nc.sbuf_tensor("tbi4", [128, 1024], I32))
            tb1 = ps.enter_context(nc.sbuf_tensor("tb1", [1, 1], I32))
            p_b4 = ps.enter_context(nc.psum_tensor("p_b4", [128, 8], F32))

            with nc.Block("p4a") as bk:
                def fg(g):
                    g.wait_ge(CCS, 1)
                    dput("CC", g, cc_sb[0:1, 0:2], cc1_out)
                bk.gpsimd(fg)
                cc_r1 = dtot("CC")

                def fv(v):
                    dwait(v, "CC", cc_r1)
                    vi("tensor_add", scal[0:1, 6:7], cc_sb[0:1, 0:1],
                       cc_sb[0:1, 1:2])
                    ts(scal[0:1, 7:8], scal[0:1, 6:7], INV255)
                    ts(scal[0:1, 7:8], scal[0:1, 7:8], np.float32(1e-8),
                       op0=ALU.max)
                    vi("reciprocal", scal[0:1, 8:9], scal[0:1, 7:8])
                    vi("tensor_mul", scal[0:1, 9:10], cc_sb[0:1, 1:2],
                       scal[0:1, 8:9])
                    vi("tensor_copy", tb1[0:1, 0:1], scal[0:1, 9:10])
                    vi("tensor_copy", scal[0:1, 10:11], tb1[0:1, 0:1])
                    ts(scal[0:1, 10:11], scal[0:1, 10:11], np.float32(-128.0),
                       op0=ALU.add)
                    ts(scal[0:1, 10:11], scal[0:1, 10:11], np.float32(-128.0),
                       op0=ALU.max)
                    ts(scal[0:1, 10:11], scal[0:1, 10:11], np.float32(127.0),
                       op0=ALU.min)
                    ts(scal[0:1, 11:12], scal[0:1, 10:11], NEG1,
                       np.float32(-128.0), ALU.mult, ALU.add)
                    ts(scal[0:1, 12:13], scal[0:1, 10:11], NEG1,
                       np.float32(127.0), ALU.mult, ALU.add)
                    vi("tensor_mul", scal[0:1, 13:14], scal[0:1, 7:8],
                       scal[0:1, 0:1])
                bk.vector(fv)
            vs_4a = cget("VS")

            with nc.Block("p4b") as bk:
                def ft(t):
                    t.wait_ge(VS, vs_4a)
                    mm(p_b4[:, 0:4], ones_f[:, :], scal[0:1, 8:12],
                       start=True, stop=True)
                    mm(p_b4[:, 4:8], ones_f[:, :], scal[0:1, 10:14],
                       start=True, stop=True).then_inc(MM0, 1)
                    cinc("MM0")
                bk.tensor(ft)
                mm0_4 = cget("MM0")

                def fv(v):
                    v.wait_ge(MM0, mm0_4)
                    vi("tensor_copy", scl_b[:, 2:3], p_b4[:, 0:1])
                    vi("tensor_copy", scl_b[:, 3:4], p_b4[:, 5:6])
                    vi("tensor_copy", scl_b[:, 4:5], p_b4[:, 6:7])
                    vi("tensor_copy", scl_b[:, 5:6], p_b4[:, 7:8])
                bk.vector(fv)

            with nc.Block("p4c") as bk:
                vsb = cget("VS")
                ltoks, stoks = [], []
                sg_base = [dtot(SG[0]), dtot(SG[1])]
                s4b = [dtot(x) for x in S4]
                s2b = [dtot(x) for x in S2]

                def fsy(sy):
                    for ix in range(4):
                        dwait(sy, S4[ix], s4b[ix])
                    dwait(sy, S2[0], s2b[0])
                    dwait(sy, S2[1], s2b[1])
                    for i in range(NT):
                        if i >= 2:
                            sy.wait_ge(VS, vsb + 2 * (i - 1))
                        ltoks.append(dput(LD[i % 2], sy,
                                          xnld[:, (i % 2) * D:(i % 2 + 1) * D],
                                          g_xn[i * 128:(i + 1) * 128, :]))
                bk.sync(fsy)

                def fg(g):
                    for i in range(NT):
                        g.wait_ge(VS, vsb + 2 * (i + 1))
                        stoks.append(dput(SG[i % 2], g,
                                          g_xqz[i * 128:(i + 1) * 128, :],
                                          xqst[:, (i % 2) * D:(i % 2 + 1) * D]))
                bk.gpsimd(fg)

                def fv(v):
                    for i in range(NT):
                        dwait(v, LD[i % 2], ltoks[i])
                        if i >= 2:
                            dwait(v, SG[i % 2], stoks[i - 2])
                        else:
                            dwait(v, SG[i % 2], sg_base[i % 2])
                        ts(tbi4[:, :], xnld[:, (i % 2) * D:(i % 2 + 1) * D],
                           scl_b[:, 2:3], scl_b[:, 3:4], ALU.mult, ALU.max)
                        ts(xqst[:, (i % 2) * D:(i % 2 + 1) * D], tbi4[:, :],
                           scl_b[:, 4:5], op0=ALU.min)
                bk.vector(fv)
            sg_p4c = [dtot(SG[0]), dtot(SG[1])]

        if PHLIM < 7:
            return nc
        with ExitStack() as ps:
            xqzT = ps.enter_context(nc.sbuf_tensor("xqzT", [128, ND * T], BF16))
            w1h = ps.enter_context(nc.sbuf_tensor("w1h", [128, 2 * 16384], BF16))
            ghst = ps.enter_context(nc.sbuf_tensor("ghst", [128, 2 * 2048], F32))
            p_f = ps.enter_context(nc.psum_tensor("p_f", [128, 8 * 512], F32))

            with nc.Block("p5") as bk:
                vs5, ss5 = cget("VS"), cget("SS")
                txk, w1t, stoks = [], [], []
                sg_base = [dtot(SG[0]), dtot(SG[1])]

                def fsy(sy):
                    dwait(sy, SG[0], sg_p4c[0])
                    dwait(sy, SG[1], sg_p4c[1])
                    for ix in range(4):
                        dwait(sy, S4[ix], s4_p3[ix])
                    dwait(sy, S2[0], s2_p3[0])
                    dwait(sy, S2[1], s2_p3[1])
                    for ic in range(ND):
                        txk.append(dput("TX", sy, xqzT[:, ic * T:(ic + 1) * T],
                                        g_xqz[0:T, ic * 128:(ic + 1) * 128],
                                        transpose=True))
                    for hh in range(2):
                        for ic in range(ND):
                            w1t.append(dput(
                                WS[hh], sy,
                                w1h[:, hh * 16384 + ic * 2048:
                                    hh * 16384 + (ic + 1) * 2048],
                                g_w1[hh * 2048:(hh + 1) * 2048,
                                     ic * 128:(ic + 1) * 128], transpose=True))
                bk.sync(fsy)

                def fg(g):
                    for n in range(64):
                        ot, gg = divmod(n, 2)
                        g.wait_ge(SS, ss5 + 4 * (n + 1))
                        stoks.append(dput(SG[n % 2], g,
                                          g_gh[ot * 128:(ot + 1) * 128,
                                               gg * 2048:(gg + 1) * 2048],
                                          ghst[:, (n % 2) * 2048:
                                               (n % 2 + 1) * 2048]))
                bk.gpsimd(fg)

                def ft(t):
                    dwait(t, "TX", txk[-1])
                    for ot in range(NH):
                        hh = ot // 16
                        if ot % 16 == 0:
                            dwait(t, WS[hh], w1t[hh * ND + ND - 1])
                        base = hh * 16384
                        for gg in range(2):
                            n = 2 * ot + gg
                            if n >= 2:
                                t.wait_ge(SS, ss5 + 4 * (n - 1))
                            inst = None
                            for ic in range(ND):
                                for tc in range(4 * gg, 4 * gg + 4):
                                    inst = mm(
                                        p_f[:, tc * 512:(tc + 1) * 512],
                                        w1h[:, base + ic * 2048 + (ot % 16) * 128:
                                            base + ic * 2048 + (ot % 16) * 128 + 128],
                                        xqzT[:, ic * T + tc * 512:
                                             ic * T + (tc + 1) * 512],
                                        start=(ic == 0), stop=(ic == ND - 1))
                            inst.then_inc(MM5, 1)
                            cinc("MM5")
                bk.tensor(ft)

                def fs(s):
                    for n in range(64):
                        ot, gg = divmod(n, 2)
                        s.wait_ge(MM5, n + 1)
                        if n >= 2:
                            dwait(s, SG[n % 2], stoks[n - 2])
                            s.wait_ge(VS, vs5 + 2 * (n - 1))
                        else:
                            dwait(s, SG[n % 2], sg_base[n % 2])
                        for j in range(4):
                            tc = 4 * gg + j
                            si("activation",
                               ghst[:, (n % 2) * 2048 + j * 512:
                                    (n % 2) * 2048 + (j + 1) * 512],
                               p_f[:, tc * 512:(tc + 1) * 512], AF.Gelu,
                               scale=scl_b[:, 5:6], bias=c_b1c[:, ot:ot + 1])
                bk.scalar(fs)

                def fv(v):
                    for n in range(64):
                        v.wait_ge(SS, ss5 + 4 * (n + 1))
                        sl = ghst[:, (n % 2) * 2048:(n % 2 + 1) * 2048]
                        vi("tensor_reduce", gmn[:, n:n + 1], sl, AX, ALU.min)
                        vi("reduce_max", gmx[:, n:n + 1], sl, AX)
                bk.vector(fv)
            sg_p5 = [dtot(SG[0]), dtot(SG[1])]

        if PHLIM < 8:
            return nc
        with ExitStack() as ps:
            p_b6 = ps.enter_context(nc.psum_tensor("p_b6", [128, 512], F32))

            with nc.Block("p6a") as bk:
                def fv(v):
                    vi("tensor_reduce", st_t[:, 0:1], gmn, AX, ALU.min)
                    vi("reduce_max", st_t[:, 1:2], gmx, AX)
                    ts(st_t[:, 0:1], st_t[:, 0:1], NEG1)
                bk.vector(fv)
            vs6 = cget("VS")

            with nc.Block("p6b") as bk:
                def ft(t):
                    t.wait_ge(VS, vs6)
                    t.transpose(p_b6[0:1, 0:128], st_t[:, 1:2], c_idf[:, :])
                    t.transpose(p_b6[0:1, 128:256], st_t[:, 0:1],
                                c_idf[:, :]).then_inc(MM0, 1)
                    cinc("MM0")
                bk.tensor(ft)
                mm06 = cget("MM0")

                def fv(v):
                    v.wait_ge(MM0, mm06)
                    vi("reduce_max", cc_sb[0:1, 0:1], p_b6[0:1, 0:128], AX)
                    vi("reduce_max", cc_sb[0:1, 1:2], p_b6[0:1, 128:256], AX)
                bk.vector(fv)
            vs6b = cget("VS")

            with nc.Block("p6c") as bk:
                def fg(g):
                    g.wait_ge(VS, vs6b)
                    dput("CC", g, cc2_in, cc_sb[0:1, 0:2])
                    dwait(g, "CC", dtot("CC"))
                    g.collective_compute(
                        "AllReduce", ALU.max, replica_groups=[list(range(8))],
                        ins=[cc2_in], outs=[cc2_out]).then_inc(CCS, 1)
                    cinc("CCS")
                bk.gpsimd(fg)

        if PHLIM < 9:
            return nc
        with ExitStack() as ps:
            qb = ps.enter_context(nc.sbuf_tensor("qb", [128, 2 * D], BF16))
            kb = ps.enter_context(nc.sbuf_tensor("kb", [128, 2 * D], BF16))
            wst = ps.enter_context(nc.sbuf_tensor("wst", [128, 2 * D], BF16))
            p_S = ps.enter_context(nc.psum_tensor("p_S", [128, 4 * 1024], F32))

            with nc.Block("p7") as bk:
                vs7, ss7 = cget("VS"), cget("SS")
                ltoks, stoks = [], []
                sg_base = [dtot(SG[0]), dtot(SG[1])]

                def fsy(sy):
                    for ix in range(4):
                        dwait(sy, S4[ix], s4_p3[ix])
                    dwait(sy, S2[0], s2_p3[0])
                    dwait(sy, S2[1], s2_p3[1])
                    dwait(sy, SG[0], sg_p5[0])
                    dwait(sy, SG[1], sg_p5[1])
                    for grp in range(2):
                        for i_ in range(NT):
                            i = grp * NT + i_
                            if i >= 2:
                                sy.wait_ge(MMS, i - 1)
                            dput(LD[i % 2], sy,
                                 qb[:, (i % 2) * D:(i % 2 + 1) * D],
                                 g_q[i_ * 128:(i_ + 1) * 128, :])
                            ltoks.append(dput(
                                LD[i % 2], sy,
                                kb[:, (i % 2) * D:(i % 2 + 1) * D],
                                g_k[i_ * 128:(i_ + 1) * 128, :]))
                bk.sync(fsy)

                def fg(g):
                    for f1 in range(ND):
                        g.wait_ge(SS, ss7 + f1 + 1)
                        stoks.append(dput(SG[f1 % 2], g,
                                          g_w[f1 * 128:(f1 + 1) * 128, :],
                                          wst[:, (f1 % 2) * D:(f1 % 2 + 1) * D]))
                bk.gpsimd(fg)

                def ft(t):
                    for grp in range(2):
                        if grp == 1:
                            t.wait_ge(SS, ss7 + 4)
                        for i_ in range(NT):
                            i = grp * NT + i_
                            dwait(t, LD[i % 2], ltoks[i])
                            inst = None
                            for j in range(4):
                                f1 = 4 * grp + j
                                for hh in range(2):
                                    inst = mm(
                                        p_S[:, j * 1024 + hh * 512:
                                            j * 1024 + (hh + 1) * 512],
                                        qb[:, (i % 2) * D + f1 * 128:
                                           (i % 2) * D + (f1 + 1) * 128],
                                        kb[:, (i % 2) * D + hh * 512:
                                           (i % 2) * D + (hh + 1) * 512],
                                        start=(i_ == 0), stop=(i_ == NT - 1))
                            inst.then_inc(MMS, 1)
                            cinc("MMS")
                bk.tensor(ft)

                def fs(s):
                    for grp in range(2):
                        for j in range(4):
                            f1 = 4 * grp + j
                            s.wait_ge(VS, vs7 + grp * 12 + 2 * (j + 1))
                            if f1 >= 2:
                                dwait(s, SG[f1 % 2], stoks[f1 - 2])
                            else:
                                dwait(s, SG[f1 % 2], sg_base[f1 % 2])
                            si("activation",
                               wst[:, (f1 % 2) * D:(f1 % 2 + 1) * D],
                               p_S[:, j * 1024:(j + 1) * 1024], AF.Exp,
                               scale=SCL, bias=nm_col[:, f1:f1 + 1],
                               accum_out=st_t[:, 16 + f1:17 + f1])
                bk.scalar(fs)

                def fv(v):
                    for grp in range(2):
                        v.wait_ge(MMS, NT * (grp + 1))
                        for j in range(4):
                            f1 = 4 * grp + j
                            vi("reduce_max", st_t[:, 8 + f1:9 + f1],
                               p_S[:, j * 1024:(j + 1) * 1024], AX)
                            ts(nm_col[:, f1:f1 + 1], st_t[:, 8 + f1:9 + f1],
                               NEGSCL)
                        for j in range(4):
                            f1 = 4 * grp + j
                            v.wait_ge(SS, ss7 + 4 * grp + j + 1)
                            vi("reciprocal", st_rr[:, f1:f1 + 1],
                               st_t[:, 16 + f1:17 + f1])
                bk.vector(fv)
            sg_p7 = [dtot(SG[0]), dtot(SG[1])]

        if PHLIM < 10:
            return nc
        with ExitStack() as ps:
            aT = ps.enter_context(nc.sbuf_tensor("aT", [128, ND * T], BF16))
            wT8 = ps.enter_context(nc.sbuf_tensor("wT8", [128, ND * D], BF16))
            vb = ps.enter_context(nc.sbuf_tensor("vb", [128, 2 * 4096], BF16))
            xn9 = ps.enter_context(nc.sbuf_tensor("xn9", [128, 2 * D], F32))
            sst9 = ps.enter_context(nc.sbuf_tensor("sst9", [128, 2 * D], F32))
            p_a = ps.enter_context(nc.psum_tensor("p_a", [128, 8 * 512], F32))

            with nc.Block("p8") as bk:
                ss8 = cget("SS")
                txk, vbt = [], []

                def fsy(sy):
                    dwait(sy, SG[0], sg_p7[0])
                    dwait(sy, SG[1], sg_p7[1])
                    for ix in range(4):
                        dwait(sy, S4[ix], s4_p3[ix])
                    dwait(sy, S2[0], s2_p3[0])
                    dwait(sy, S2[1], s2_p3[1])
                    for f2 in range(ND):
                        txk.append(dput("TX", sy, wT8[:, f2 * D:(f2 + 1) * D],
                                        g_w[0:D, f2 * 128:(f2 + 1) * 128],
                                        transpose=True))
                    for tc in range(ND):
                        if tc >= 2:
                            sy.wait_ge(MM8, 2 * (tc - 1))
                        for f2 in range(ND):
                            vbt.append(dput(
                                VSL[tc % 2], sy,
                                vb[:, (tc % 2) * 4096 + f2 * 512:
                                   (tc % 2) * 4096 + (f2 + 1) * 512],
                                g_vT[f2 * 128:(f2 + 1) * 128,
                                     tc * 512:(tc + 1) * 512]))
                bk.sync(fsy)

                def ft(t):
                    dwait(t, "TX", txk[-1])
                    for tc in range(ND):
                        dwait(t, VSL[tc % 2], vbt[tc * ND + ND - 1])
                        for gg in range(2):
                            n = 2 * tc + gg
                            if n >= 2:
                                t.wait_ge(SS, ss8 + 4 * (n - 1))
                            inst = None
                            for f2 in range(ND):
                                for jj in range(4):
                                    j = 4 * gg + jj
                                    inst = mm(
                                        p_a[:, j * 512:(j + 1) * 512],
                                        wT8[:, f2 * D + j * 128:
                                            f2 * D + (j + 1) * 128],
                                        vb[:, (tc % 2) * 4096 + f2 * 512:
                                           (tc % 2) * 4096 + (f2 + 1) * 512],
                                        start=(f2 == 0), stop=(f2 == ND - 1))
                            inst.then_inc(MM8, 1)
                            cinc("MM8")
                bk.tensor(ft)

                def fs(s):
                    for n in range(2 * ND):
                        tc, gg = divmod(n, 2)
                        s.wait_ge(MM8, n + 1)
                        for jj in range(4):
                            j = 4 * gg + jj
                            si("activation",
                               aT[:, j * T + tc * 512:j * T + (tc + 1) * 512],
                               p_a[:, j * 512:(j + 1) * 512], AF.Copy,
                               scale=st_rr[:, j:j + 1])
                bk.scalar(fs)

            with nc.Block("p9") as bk:
                vs9 = cget("VS")
                ss9 = cget("SS")
                txk9, ltoks, stoks = [], [], []
                sg_base = [dtot(SG[0]), dtot(SG[1])]

                def fsy(sy):
                    for jc in range(ND):
                        txk9.append(dput("TX", sy, wT8[:, jc * D:(jc + 1) * D],
                                         g_pw[0:D, jc * 128:(jc + 1) * 128],
                                         transpose=True))
                    for i in range(NT):
                        if i >= 2:
                            sy.wait_ge(VS, vs9 + (i - 1))
                        ltoks.append(dput(LD[i % 2], sy,
                                          xn9[:, (i % 2) * D:(i % 2 + 1) * D],
                                          g_xn[i * 128:(i + 1) * 128, :]))
                bk.sync(fsy)

                def fg(g):
                    for i in range(NT):
                        g.wait_ge(VS, vs9 + i + 1)
                        stoks.append(dput(SG[i % 2], g,
                                          g_s[i * 128:(i + 1) * 128, :],
                                          sst9[:, (i % 2) * D:(i % 2 + 1) * D]))
                bk.gpsimd(fg)

                def ft(t):
                    dwait(t, "TX", txk9[-1])
                    t.wait_ge(SS, ss9)
                    for i in range(NT):
                        if i >= 2:
                            t.wait_ge(VS, vs9 + (i - 1))
                        sl = p_a[:, (i % 2) * 1024:(i % 2 + 1) * 1024]
                        for jc in range(ND):
                            for hh in range(2):
                                mm(sl[:, hh * 512:(hh + 1) * 512],
                                   aT[:, jc * T + i * 128:jc * T + (i + 1) * 128],
                                   wT8[:, jc * D + hh * 512:
                                       jc * D + (hh + 1) * 512],
                                   start=(jc == 0), stop=False)
                        inst = None
                        for hh in range(2):
                            inst = mm(sl[:, hh * 512:(hh + 1) * 512],
                                      grow16[:, i * 128:(i + 1) * 128],
                                      r_pb16[:, hh * 512:(hh + 1) * 512],
                                      start=False, stop=True)
                        inst.then_inc(MM9, 1)
                        cinc("MM9")
                bk.tensor(ft)

                def fv(v):
                    for i in range(NT):
                        v.wait_ge(MM9, i + 1)
                        dwait(v, LD[i % 2], ltoks[i])
                        if i >= 2:
                            dwait(v, SG[i % 2], stoks[i - 2])
                        else:
                            dwait(v, SG[i % 2], sg_base[i % 2])
                        sl = p_a[:, (i % 2) * 1024:(i % 2 + 1) * 1024]
                        vi("scalar_tensor_tensor",
                           sst9[:, (i % 2) * D:(i % 2 + 1) * D], sl,
                           st_al[:, i:i + 1],
                           xn9[:, (i % 2) * D:(i % 2 + 1) * D],
                           ALU.mult, ALU.add)
                bk.vector(fv)
            sg_p9 = [dtot(SG[0]), dtot(SG[1])]

        if PHLIM < 11:
            return nc
        with ExitStack() as ps10a:
            tb1b = ps10a.enter_context(nc.sbuf_tensor("tb1b", [1, 1], I32))
            p_b10 = ps10a.enter_context(nc.psum_tensor("p_b10", [128, 8], F32))

            with nc.Block("p10a") as bk:
                def fg(g):
                    g.wait_ge(CCS, 2)
                    dput("CC", g, cc_sb[0:1, 0:2], cc2_out)
                bk.gpsimd(fg)
                cc_r2 = dtot("CC")

                def fv(v):
                    dwait(v, "CC", cc_r2)
                    vi("tensor_add", scal[0:1, 16:17], cc_sb[0:1, 0:1],
                       cc_sb[0:1, 1:2])
                    ts(scal[0:1, 17:18], scal[0:1, 16:17], INV255)
                    ts(scal[0:1, 17:18], scal[0:1, 17:18], np.float32(1e-8),
                       op0=ALU.max)
                    vi("reciprocal", scal[0:1, 18:19], scal[0:1, 17:18])
                    vi("tensor_mul", scal[0:1, 19:20], cc_sb[0:1, 1:2],
                       scal[0:1, 18:19])
                    vi("tensor_copy", tb1b[0:1, 0:1], scal[0:1, 19:20])
                    vi("tensor_copy", scal[0:1, 20:21], tb1b[0:1, 0:1])
                    ts(scal[0:1, 20:21], scal[0:1, 20:21], np.float32(-128.0),
                       op0=ALU.add)
                    ts(scal[0:1, 20:21], scal[0:1, 20:21], np.float32(-128.0),
                       op0=ALU.max)
                    ts(scal[0:1, 20:21], scal[0:1, 20:21], np.float32(127.0),
                       op0=ALU.min)
                    ts(scal[0:1, 21:22], scal[0:1, 20:21], NEG1,
                       np.float32(-128.0), ALU.mult, ALU.add)
                    ts(scal[0:1, 22:23], scal[0:1, 20:21], NEG1,
                       np.float32(127.0), ALU.mult, ALU.add)
                    vi("tensor_mul", scal[0:1, 23:24], scal[0:1, 17:18],
                       scal[0:1, 2:3])
                    vi("reciprocal", scal[0:1, 24:25], scal[0:1, 23:24])
                    ts(r_b2r[:, :], r_b2[:, :], scal[0:1, 24:25])
                    vi("tensor_copy", r_b2r16[:, :], r_b2r[:, :])
                bk.vector(fv)
            vs_10a = cget("VS")

            with nc.Block("p10c") as bk:
                def ft(t):
                    t.wait_ge(VS, vs_10a)
                    mm(p_b10[:, 0:4], ones_f[:, :], scal[0:1, 18:22],
                       start=True, stop=True)
                    mm(p_b10[:, 4:8], ones_f[:, :], scal[0:1, 22:26],
                       start=True, stop=True).then_inc(MM0, 1)
                    cinc("MM0")
                bk.tensor(ft)
                mm010 = cget("MM0")

                def fv(v):
                    v.wait_ge(MM0, mm010)
                    vi("tensor_copy", scl_b[:, 6:7], p_b10[:, 0:1])
                    vi("tensor_copy", scl_b[:, 7:8], p_b10[:, 3:4])
                    vi("tensor_copy", scl_b[:, 0:1], p_b10[:, 4:5])
                    vi("tensor_copy", scl_b[:, 1:2], p_b10[:, 5:6])
                bk.vector(fv)

        if PHLIM < 12:
            return nc
        with ExitStack() as ps:
            hqT = ps.enter_context(nc.sbuf_tensor("hqT", [128, NH * 512], BF16))
            w2T = ps.enter_context(nc.sbuf_tensor("w2T", [128, NH * D], BF16))
            ghld = ps.enter_context(nc.sbuf_tensor("ghld", [128, 2 * 512], F32))
            tbq = ps.enter_context(nc.sbuf_tensor("tbq", [128, 512], I32))
            xl10 = ps.enter_context(nc.sbuf_tensor("xl10", [128, 2 * D], F32))
            sl10 = ps.enter_context(nc.sbuf_tensor("sl10", [128, 2 * D], F32))
            gst = ps.enter_context(nc.sbuf_tensor("gst", [128, 2 * D], F32))
            ost = ps.enter_context(nc.sbuf_tensor("ost", [128, 2 * D], F32))
            p_o = ps.enter_context(nc.psum_tensor("p_o", [128, 4 * 1024], F32))

            with nc.Block("p10") as bk:
                vs10, ss10 = cget("VS"), cget("SS")
                mm0b = cget("MM0")
                NBLK = 8
                VPB = 72
                txk, lxs, ghtoks, otoks = [], [], [], []

                def q_done(i):
                    b_, o_ = divmod(i, NH)
                    return vs10 + b_ * VPB + 2 * (o_ + 1)

                def comb_done(i):
                    b_, t_ = divmod(i, 4)
                    return vs10 + b_ * VPB + 64 + 2 * (t_ + 1)

                def fsy(sy):
                    for oc in range(NH):
                        txk.append(dput("TX", sy, w2T[:, oc * D:(oc + 1) * D],
                                        g_w2[0:D, oc * 128:(oc + 1) * 128],
                                        transpose=True))
                    dwait(sy, SG[0], sg_p9[0])
                    dwait(sy, SG[1], sg_p9[1])
                    for i in range(NT):
                        if i >= 2:
                            sy.wait_ge(VS, comb_done(i - 2))
                            otoks.append(dput(
                                OUTS[i % 2], sy,
                                out_hbm[(i - 2) * 128:(i - 1) * 128, :],
                                ost[:, (i % 2) * D:(i % 2 + 1) * D]))
                        dput(LD[i % 2], sy, xl10[:, (i % 2) * D:(i % 2 + 1) * D],
                             x_hbm[i * 128:(i + 1) * 128, :])
                        lxs.append(dput(LD[i % 2], sy,
                                        sl10[:, (i % 2) * D:(i % 2 + 1) * D],
                                        g_s[i * 128:(i + 1) * 128, :]))
                    for i in range(NT - 2, NT):
                        sy.wait_ge(VS, comb_done(i))
                        otoks.append(dput(OUTS[i % 2], sy,
                                          out_hbm[i * 128:(i + 1) * 128, :],
                                          ost[:, (i % 2) * D:(i % 2 + 1) * D]))
                    dwait(sy, OUTS[0], dtot(OUTS[0]))
                    dwait(sy, OUTS[1], dtot(OUTS[1]))
                bk.sync(fsy)

                def fg(g):
                    dwait(g, SG[0], sg_p9[0])
                    dwait(g, SG[1], sg_p9[1])
                    for blk_ in range(NBLK):
                        for oc in range(NH):
                            i = blk_ * NH + oc
                            if i >= 2:
                                g.wait_ge(VS, q_done(i - 2))
                            ghtoks.append(dput(
                                GH[i % 2], g,
                                ghld[:, (i % 2) * 512:(i % 2 + 1) * 512],
                                g_gh[oc * 128:(oc + 1) * 128,
                                     blk_ * 512:(blk_ + 1) * 512]))
                bk.gpsimd(fg)

                def fv(v):
                    for blk_ in range(NBLK):
                        for oc in range(NH):
                            i = blk_ * NH + oc
                            dwait(v, GH[i % 2], ghtoks[i])
                            if blk_ >= 1 and oc == 0:
                                v.wait_ge(MM0, mm0b + 4 * blk_)
                            ts(tbq[:, :],
                               ghld[:, (i % 2) * 512:(i % 2 + 1) * 512],
                               scl_b[:, 6:7], scl_b[:, 7:8], ALU.mult, ALU.max)
                            ts(hqT[:, oc * 512:(oc + 1) * 512], tbq[:, :],
                               scl_b[:, 0:1], op0=ALU.min)
                        for tt in range(4):
                            i = blk_ * 4 + tt
                            v.wait_ge(SS, ss10 + i + 1)
                            dwait(v, LD[i % 2], lxs[i])
                            if i >= 2:
                                dwait(v, OUTS[i % 2], otoks[i - 2])
                            vi("tensor_mul",
                               gst[:, (i % 2) * D:(i % 2 + 1) * D],
                               gst[:, (i % 2) * D:(i % 2 + 1) * D],
                               sl10[:, (i % 2) * D:(i % 2 + 1) * D])
                            vi("tensor_add",
                               ost[:, (i % 2) * D:(i % 2 + 1) * D],
                               gst[:, (i % 2) * D:(i % 2 + 1) * D],
                               xl10[:, (i % 2) * D:(i % 2 + 1) * D])
                bk.vector(fv)

                def ft(t):
                    dwait(t, "TX", txk[-1])
                    for blk_ in range(NBLK):
                        if blk_ >= 1:
                            t.wait_ge(SS, ss10 + 4 * blk_)
                        for oc in range(NH):
                            first = True
                            for tt in range(4):
                                for hh in range(2):
                                    inst = mm(
                                        p_o[:, tt * 1024 + hh * 512:
                                            tt * 1024 + (hh + 1) * 512],
                                        hqT[:, oc * 512 + tt * 128:
                                            oc * 512 + (tt + 1) * 128],
                                        w2T[:, oc * D + hh * 512:
                                            oc * D + (hh + 1) * 512],
                                        start=(oc == 0), stop=False)
                                    if first:
                                        inst._wait_ge(VS,
                                                      q_done(blk_ * NH + oc))
                                        first = False
                        for tt in range(4):
                            inst = None
                            for hh in range(2):
                                inst = mm(
                                    p_o[:, tt * 1024 + hh * 512:
                                        tt * 1024 + (hh + 1) * 512],
                                    ones_b[:, :],
                                    r_b2r16[:, hh * 512:(hh + 1) * 512],
                                    start=False, stop=True)
                            inst.then_inc(MM0, 1)
                            cinc("MM0")
                bk.tensor(ft)

                def fs(s):
                    for blk_ in range(NBLK):
                        for tt in range(4):
                            i = blk_ * 4 + tt
                            s.wait_ge(MM0, mm0b + i + 1)
                            if i >= 2:
                                s.wait_ge(VS, comb_done(i - 2))
                            si("activation",
                               gst[:, (i % 2) * D:(i % 2 + 1) * D],
                               p_o[:, tt * 1024:(tt + 1) * 1024], AF.Silu,
                               scale=scl_b[:, 1:2])
                bk.scalar(fs)

    return nc


IDENT = np.eye(128, dtype=np.float32)


def kernel(L0_attn_x, L0_attn_norm_w, L0_attn_norm_b, L0_attn_qkv_w, L0_attn_qkv_b,
           L0_attn_proj_w, L0_attn_proj_b, L0_ffn_W1, L0_ffn_b1, L0_ffn_W2,
           L0_ffn_b2):
    x = np.ascontiguousarray(np.asarray(L0_attn_x, dtype=np.float32))
    common = {
        "norm_w": np.asarray(L0_attn_norm_w, np.float32).reshape(1, D),
        "norm_b": np.asarray(L0_attn_norm_b, np.float32).reshape(1, D),
        "qkv_w": np.ascontiguousarray(np.asarray(L0_attn_qkv_w, np.float32)),
        "qkv_b": np.asarray(L0_attn_qkv_b, np.float32).reshape(1, O3),
        "proj_w": np.ascontiguousarray(np.asarray(L0_attn_proj_w, np.float32)),
        "proj_b": np.asarray(L0_attn_proj_b, np.float32).reshape(1, D),
        "w1": np.ascontiguousarray(np.asarray(L0_ffn_W1, np.float32)),
        "b1": np.asarray(L0_ffn_b1, np.float32).reshape(1, H),
        "w2": np.ascontiguousarray(np.asarray(L0_ffn_W2, np.float32)),
        "b2": np.asarray(L0_ffn_b2, np.float32).reshape(1, D),
        "identf": IDENT,
    }
    in_maps = [dict(common, x=x[b]) for b in range(8)]
    nc = build_nc()
    res = run_bass_kernel_spmd(nc, in_maps, core_ids=list(range(8)))
    out = np.stack([np.asarray(res.results[b]["out"]) for b in range(8)], axis=0)
    return out.astype(np.float32)


if __name__ == "__main__":
    import reference as R
    ins = {k: np.asarray(v) for k, v in R.setup_inputs(0).items()}
    out = kernel(**ins)
    print(out.shape, out.dtype)


# version 03
# speedup vs baseline: 1.0250x; 1.0250x over previous
import os
import numpy as np
import ml_dtypes
from contextlib import ExitStack

PHLIM = float(os.environ.get("PHLIM", "99"))

import concourse.bass as bass
import concourse.mybir as mybir
from concourse.bass_utils import run_bass_kernel_spmd

F32 = mybir.dt.float32
BF16 = mybir.dt.bfloat16
I32 = mybir.dt.int32
AF = mybir.ActivationFunctionType
ALU = mybir.AluOpType
AX = mybir.AxisListType.X

T, D, O3, H = 4096, 1024, 3072, 4096
NT, ND, NH = 32, 8, 32
EPS = np.float32(1e-5)
INV1024 = np.float32(1.0 / 1024.0)
SCL = np.float32(0.015625)
NEGSCL = np.float32(-0.015625)
INV255 = np.float32(1.0 / 255.0)
INV127 = np.float32(1.0 / 127.0)
NEG1 = np.float32(-1.0)
HALFNEG = np.float32(-0.5)
ONEP5 = np.float32(1.5)


def build_nc():
    nc = bass.Bass("TRN2", target_bir_lowering=False)

    x_hbm = nc.dram_tensor("x", [T, D], F32, kind="ExternalInput").ap()
    nw_hbm = nc.dram_tensor("norm_w", [1, D], F32, kind="ExternalInput").ap()
    nb_hbm = nc.dram_tensor("norm_b", [1, D], F32, kind="ExternalInput").ap()
    wq_hbm = nc.dram_tensor("qkv_w", [O3, D], F32, kind="ExternalInput").ap()
    bq_hbm = nc.dram_tensor("qkv_b", [1, O3], F32, kind="ExternalInput").ap()
    pw_hbm = nc.dram_tensor("proj_w", [D, D], F32, kind="ExternalInput").ap()
    pb_hbm = nc.dram_tensor("proj_b", [1, D], F32, kind="ExternalInput").ap()
    w1_hbm = nc.dram_tensor("w1", [H, D], F32, kind="ExternalInput").ap()
    b1_hbm = nc.dram_tensor("b1", [1, H], F32, kind="ExternalInput").ap()
    w2_hbm = nc.dram_tensor("w2", [D, H], F32, kind="ExternalInput").ap()
    b2_hbm = nc.dram_tensor("b2", [1, D], F32, kind="ExternalInput").ap()
    idf_hbm = nc.dram_tensor("identf", [128, 128], F32, kind="ExternalInput").ap()
    out_hbm = nc.dram_tensor("out", [T, D], F32, kind="ExternalOutput").ap()

    g_xb = nc.dram_tensor("g_xb", [T, D], BF16).ap()
    g_xn = nc.dram_tensor("g_xn", [T, D], F32).ap()
    g_xqz = nc.dram_tensor("g_xqz", [T, D], BF16).ap()
    g_q = nc.dram_tensor("g_q", [T, D], BF16).ap()
    g_k = nc.dram_tensor("g_k", [T, D], BF16).ap()
    g_vT = nc.dram_tensor("g_vT", [D, T], BF16).ap()
    g_w = nc.dram_tensor("g_w", [D, D], BF16).ap()
    g_s = nc.dram_tensor("g_s", [T, D], F32).ap()
    g_gh = nc.dram_tensor("g_gh", [H, T], F32).ap()
    g_wq = nc.dram_tensor("g_wq", [O3, D], BF16).ap()
    g_pw = nc.dram_tensor("g_pw", [D, D], BF16).ap()
    g_w1 = nc.dram_tensor("g_w1", [H, D], BF16).ap()
    g_w2 = nc.dram_tensor("g_w2", [D, H], BF16).ap()
    g_bg = nc.dram_tensor("g_bg", [64, 128], BF16).ap()
    g_c2 = nc.dram_tensor("g_c2", [128, 24], F32).ap()
    cc1_in = nc.dram_tensor("cc1_in", [1, 2], F32).ap()
    cc1_out = nc.dram_tensor("cc1_out", [1, 2], F32).ap()
    cc2_in = nc.dram_tensor("cc2_in", [1, 2], F32).ap()
    cc2_out = nc.dram_tensor("cc2_out", [1, 2], F32).ap()

    C = {}

    def cinc(n, k=1):
        C[n] = C.get(n, 0) + k
        return C[n]

    def cget(n):
        return C.get(n, 0)

    with ExitStack() as top:
        sb = lambda n, s, d: top.enter_context(nc.sbuf_tensor(n, s, d))
        sem = lambda n: top.enter_context(nc.semaphore(n))

        c_idf = sb("c_idf", [128, 128], F32)
        ones_f = sb("ones_f", [1, 128], F32)
        ones_b = sb("ones_b", [1, 128], BF16)
        c_nwnb = sb("c_nwnb", [128, 16], BF16)
        c_b1c = sb("c_b1c", [128, NH], F32)
        c2bcol = sb("c2bcol", [128, 24], F32)
        c2b_bc = sb("c2b_bc", [128, 2048], F32)
        pb_bc = sb("pb_bc", [128, D], F32)
        r_b2 = sb("r_b2", [1, D], F32)
        r_b2r = sb("r_b2r", [1, D], F32)
        r_b2r16 = sb("r_b2r16", [1, D], BF16)
        bg16 = sb("bg16", [2, T], BF16)
        st_sum = sb("st_sum", [128, NT], F32)
        st_ssq = sb("st_ssq", [128, NT], F32)
        st_mnx = sb("st_mnx", [128, NT], F32)
        st_mxx = sb("st_mxx", [128, NT], F32)
        st_mu = sb("st_mu", [128, NT], F32)
        st_mu2 = sb("st_mu2", [128, NT], F32)
        st_r1 = sb("st_r1", [128, NT], F32)
        st_r2 = sb("st_r2", [128, NT], F32)
        st_al = sb("st_al", [128, NT], F32)
        st_bg = sb("st_bg", [128, 2 * NT], F32)
        st_t = sb("st_t", [128, 256], F32)
        st_rr = sb("st_rr", [128, ND], F32)
        nm_col = sb("nm_col", [128, ND], F32)
        gmn = sb("gmn", [128, 64], F32)
        gmx = sb("gmx", [128, 64], F32)
        scal = sb("scal", [1, 32], F32)
        scl_b = sb("scl_b", [128, 8], F32)
        cc_sb = sb("cc_sb", [1, 2], F32)
        crow = sb("crow", [2, O3], F32)
        crow16 = sb("crow16", [2, O3], BF16)
        r_qkvb = sb("r_qkvb", [2, O3], F32)
        bgst16 = sb("bgst16", [64, 128], BF16)

        VS = sem("VS")
        SS = sem("SS")
        MMC = sem("MMC")
        MMQ = sem("MMQ")
        MMV = sem("MMV")
        MMS = sem("MMS")
        MM5 = sem("MM5")
        MM8 = sem("MM8")
        MM9 = sem("MM9")
        MM0 = sem("MM0")
        FR3 = sem("FR3")
        FR3V = sem("FR3V")
        CCS = sem("CCS")

        _streams = {}

        def _stream(name):
            if name not in _streams:
                _streams[name] = [top.enter_context(nc.semaphore(name)), 0]
            return _streams[name]

        def dput(name, eng, out, in_, **k):
            s = _stream(name)
            eng.dma_start(out=out, in_=in_, **k).then_inc(s[0], 16)
            s[1] += 16
            return s[1]

        def dtot(name):
            return _stream(name)[1]

        def dwait(eng, name, tok):
            if tok:
                eng.wait_ge(_stream(name)[0], tok)

        from concourse.bass_types import SBTensorHandle, PSumTensorHandle

        def _ap(x):
            if isinstance(x, (SBTensorHandle, PSumTensorHandle)):
                return x[:, :]
            return x

        def vi(op, *a, **k):
            a = tuple(_ap(x) for x in a)
            inst = getattr(nc.vector, op)(*a, **k)
            prev = cget("VS")
            if prev:
                inst._wait_ge(VS, prev)
            inst.then_inc(VS, 1)
            cinc("VS")
            return inst

        def si(op, *a, **k):
            a = tuple(_ap(x) for x in a)
            inst = getattr(nc.scalar, op)(*a, **k)
            prev = cget("SS")
            if prev:
                inst._wait_ge(SS, prev)
            inst.then_inc(SS, 1)
            cinc("SS")
            return inst

        def ts(out, in0, s1, s2=None, op0=ALU.mult, op1=None):
            if isinstance(s1, np.floating):
                s1 = float(s1)
            if isinstance(s2, np.floating):
                s2 = float(s2)
            if op1 is None:
                return vi("tensor_scalar", out, in0, s1, None, op0)
            return vi("tensor_scalar", out, in0, s1, s2, op0, op1)

        def mm(*a, **k):
            return nc.tensor.matmul(*a, **k)

        LD = ["LA", "LB"]
        WS = ["WA", "WB"]
        VSL = ["VA", "VB"]
        SG = ["SA", "SB"]
        S4 = ["S4A", "S4B", "S4C", "S4D"]
        S2 = ["S2A", "S2B"]
        GH = ["GA", "GB"]
        OUTS = ["OA", "OB"]

        if PHLIM < 0.5:
            with nc.Block("pnull") as bk:
                def fv(v):
                    v.memset(scal[:, :], 0.0)
                bk.vector(fv)
            return nc
        with ExitStack() as ps:
            p_tr = ps.enter_context(nc.psum_tensor("p_tr", [128, 64], F32))
            p_bc = ps.enter_context(nc.psum_tensor("p_bc", [128, 1024], F32))
            nwrow = ps.enter_context(nc.sbuf_tensor("nwrow", [1, D], F32))
            nbrow = ps.enter_context(nc.sbuf_tensor("nbrow", [1, D], F32))
            nb_bc = ps.enter_context(nc.sbuf_tensor("nb_bc", [128, D], F32))
            bqcol = ps.enter_context(nc.sbuf_tensor("bqcol", [128, 24], F32))
            c2brow = ps.enter_context(nc.sbuf_tensor("c2brow", [1, O3], F32))
            ctmp = ps.enter_context(nc.sbuf_tensor("ctmp", [128, D], F32))
            pbrow = ps.enter_context(nc.sbuf_tensor("pbrow", [1, D], F32))
            c_nwb = ps.enter_context(nc.sbuf_tensor("c_nwb", [128, D], F32))
            xld = ps.enter_context(nc.sbuf_tensor("xld", [128, 2 * D], F32))
            strash = ps.enter_context(nc.sbuf_tensor("strash", [128, D], F32))
            wq16 = ps.enter_context(nc.sbuf_tensor("wq16", [128, 2 * 2048], BF16))
            tbi = ps.enter_context(nc.sbuf_tensor("tbi", [128, 2048], I32))
            w2ld = ps.enter_context(nc.sbuf_tensor("w2ld", [128, 2 * 2048], F32))
            xnst = ps.enter_context(nc.sbuf_tensor("xnst", [128, 2 * D], F32))
            xbst = ps.enter_context(nc.sbuf_tensor("xbst", [128, 2 * D], BF16))

            with nc.Block("p0m") as bk:
                def fv(v):
                    v.memset(ones_f[:, :], 1.0)
                    v.memset(ones_b[:, :], 1.0)
                    v.memset(r_qkvb[0:1, :], 0.0)
                    v.memset(scal[:, :], 0.0)
                bk.vector(fv)

            if PHLIM < 0.7:
                return nc
            with nc.Block("p0") as bk:
                def fsy(sy):
                    dput("TX", sy, c_idf[:, :], idf_hbm)
                    dput("TX", sy, r_qkvb[1:2, :], bq_hbm)
                    dput("TX", sy, r_b2[:, :], b2_hbm)
                    dput("TX", sy, nwrow[:, :], nw_hbm)
                    dput("TX", sy, nbrow[:, :], nb_hbm)
                    dput("TX", sy, pbrow[:, :], pb_hbm)
                    dput("TX", sy, bqcol[:, :],
                         bq_hbm.rearrange("o (c p) -> (o p) c", p=128),
                         allow_slow_non_contiguous=True)
                    dput("TX", sy, strash[:, 0:8],
                         nw_hbm.rearrange("o (c p) -> (o p) c", p=128),
                         allow_slow_non_contiguous=True)
                    dput("TX", sy, strash[:, 8:16],
                         nb_hbm.rearrange("o (c p) -> (o p) c", p=128),
                         allow_slow_non_contiguous=True)
                    dput("TX", sy, c_b1c[:, :],
                         b1_hbm.rearrange("o (c p) -> (o p) c", p=128),
                         allow_slow_non_contiguous=True)
                bk.sync(fsy)
                tx_p0 = dtot("TX")

                def ft(t):
                    dwait(t, "TX", tx_p0)
                    mm(p_bc[:, 0:512], ones_f[:, :], nwrow[0:1, 0:512],
                       start=True, stop=True)
                    mm(p_bc[:, 512:1024], ones_f[:, :], nwrow[0:1, 512:1024],
                       start=True, stop=True).then_inc(MM0, 1)
                    cinc("MM0")
                bk.tensor(ft)
                mm0_a = cget("MM0")

            if PHLIM < 0.9:
                return nc
            with nc.Block("p0b") as bk:
                def fv(v):
                    v.wait_ge(MM0, mm0_a)
                    dwait(v, "TX", tx_p0)
                    il = c_nwnb[:, :].rearrange("p (a two) -> p two a", two=2)
                    vi("tensor_copy", il[:, 0, :], strash[:, 0:8])
                    vi("tensor_copy", il[:, 1, :], strash[:, 8:16])
                    vi("tensor_copy", c_nwb[:, :], p_bc[:, :])
                bk.vector(fv)

            with nc.Block("p0c") as bk:
                def ft(t):
                    mm(p_bc[:, 0:512], ones_f[:, :], nbrow[0:1, 0:512],
                       start=True, stop=True)
                    mm(p_bc[:, 512:1024], ones_f[:, :], nbrow[0:1, 512:1024],
                       start=True, stop=True).then_inc(MM0, 1)
                    cinc("MM0")
                bk.tensor(ft)
                mm0_c = cget("MM0")

                def fv(v):
                    v.wait_ge(MM0, mm0_c)
                    vi("tensor_copy", nb_bc[:, :], p_bc[:, :])
                bk.vector(fv)

            if PHLIM < 1:
                return nc
            with nc.Block("p1a") as bk:
                vs1 = cget("VS")
                ss1 = cget("SS")
                toks = []

                def fsy(sy):
                    for i in range(NT):
                        if i >= 2:
                            sy.wait_ge(VS, vs1 + 2 * (i - 1))
                            sy.wait_ge(SS, ss1 + 2 * (i - 1))
                        toks.append(dput(LD[i % 2], sy,
                                         xld[:, (i % 2) * D:(i % 2 + 1) * D],
                                         x_hbm[i * 128:(i + 1) * 128, :]))
                bk.sync(fsy)

                def fv(v):
                    for i in range(NT):
                        dwait(v, LD[i % 2], toks[i])
                        sl = xld[:, (i % 2) * D:(i % 2 + 1) * D]
                        vi("tensor_reduce", st_mnx[:, i:i + 1], sl, AX, ALU.min)
                        vi("reduce_max", st_mxx[:, i:i + 1], sl, AX)
                bk.vector(fv)

                def fs(s):
                    for i in range(NT):
                        dwait(s, LD[i % 2], toks[i])
                        sl = xld[:, (i % 2) * D:(i % 2 + 1) * D]
                        si("activation", strash[:, :], sl, AF.Copy,
                           accum_out=st_sum[:, i:i + 1])
                        si("activation", strash[:, :], sl, AF.Square,
                           accum_out=st_ssq[:, i:i + 1])
                bk.scalar(fs)

            if PHLIM < 2:
                return nc
            with nc.Block("p2a") as bk:
                def fv(v):
                    ts(st_mu, st_sum, INV1024)
                    ts(st_t[:, 0:32], st_ssq, INV1024)
                    vi("tensor_mul", st_t[:, 32:64], st_mu, st_mu)
                    vi("tensor_sub", st_t[:, 0:32], st_t[:, 0:32], st_t[:, 32:64])
                    ts(st_t[:, 32:64], st_t[:, 0:32], EPS, op0=ALU.add)
                bk.vector(fv)

            with nc.Block("p2b") as bk:
                def fs(s):
                    si("activation", st_t[:, 64:96], st_t[:, 32:64], AF.Sqrt)
                bk.scalar(fs)

            with nc.Block("p2c") as bk:
                def fv(v):
                    vi("reciprocal", st_t[:, 96:128], st_t[:, 64:96])
                    vi("tensor_mul", st_t[:, 128:160], st_t[:, 96:128],
                       st_t[:, 96:128])
                    vi("tensor_mul", st_t[:, 128:160], st_t[:, 128:160],
                       st_t[:, 32:64])
                    ts(st_t[:, 128:160], st_t[:, 128:160], HALFNEG, ONEP5,
                       ALU.mult, ALU.add)
                    vi("tensor_mul", st_r1, st_t[:, 96:128], st_t[:, 128:160])
                    ts(st_t[:, 160:192], st_mu, np.float32(-1024.0))
                    vi("tensor_add", st_t[:, 160:192], st_t[:, 160:192], st_sum)
                    ts(st_t[:, 160:192], st_t[:, 160:192], INV1024)
                    vi("tensor_mul", st_mu2, st_t[:, 160:192], st_r1)
                    vi("tensor_mul", st_t[:, 160:192], st_r1, st_r1)
                    vi("tensor_mul", st_t[:, 160:192], st_t[:, 160:192],
                       st_t[:, 0:32])
                    vi("tensor_mul", st_t[:, 192:224], st_mu2, st_mu2)
                    vi("tensor_sub", st_t[:, 160:192], st_t[:, 160:192],
                       st_t[:, 192:224])
                    ts(st_t[:, 160:192], st_t[:, 160:192], EPS, op0=ALU.add)
                bk.vector(fv)

            with nc.Block("p2d") as bk:
                def fs(s):
                    si("activation", st_t[:, 192:224], st_t[:, 160:192], AF.Sqrt)
                bk.scalar(fs)

            with nc.Block("p2e") as bk:
                def fv(v):
                    vi("reciprocal", st_t[:, 224:256], st_t[:, 192:224])
                    vi("tensor_mul", st_t[:, 64:96], st_t[:, 224:256],
                       st_t[:, 224:256])
                    vi("tensor_mul", st_t[:, 64:96], st_t[:, 64:96],
                       st_t[:, 160:192])
                    ts(st_t[:, 64:96], st_t[:, 64:96], HALFNEG, ONEP5,
                       ALU.mult, ALU.add)
                    vi("tensor_mul", st_r2, st_t[:, 224:256], st_t[:, 64:96])
                    vi("tensor_mul", st_al, st_r1, st_r2)
                    vi("reciprocal", st_t[:, 96:128], st_al)
                    vi("tensor_mul", st_t[:, 128:160], st_al, st_mu)
                    vi("tensor_mul", st_t[:, 160:192], st_r2, st_mu2)
                    vi("tensor_add", st_t[:, 128:160], st_t[:, 128:160],
                       st_t[:, 160:192])
                    ts(st_t[:, 128:160], st_t[:, 128:160], NEG1)
                    vi("tensor_mul", st_t[:, 128:160], st_t[:, 128:160],
                       st_t[:, 96:128])
                    il = st_bg[:, :].rearrange("p (a two) -> p two a", two=2)
                    vi("tensor_copy", il[:, 0, :], st_t[:, 128:160])
                    vi("tensor_copy", il[:, 1, :], st_t[:, 96:128])
                    vi("tensor_sub", st_t[:, 160:192], st_mnx[:, :], st_mu[:, :])
                    vi("tensor_mul", st_t[:, 160:192], st_t[:, 160:192],
                       st_r1[:, :])
                    vi("tensor_sub", st_t[:, 192:224], st_mxx[:, :], st_mu[:, :])
                    vi("tensor_mul", st_t[:, 192:224], st_t[:, 192:224],
                       st_r1[:, :])
                    vi("tensor_reduce", st_t[:, 224:225], st_t[:, 160:192], AX,
                       ALU.min)
                    vi("reduce_max", st_t[:, 225:226], st_t[:, 192:224], AX)
                    ts(st_t[:, 224:225], st_t[:, 224:225], NEG1)
                bk.vector(fv)
            vs_p2e = cget("VS")

            with nc.Block("p2f") as bk:
                def ft(t):
                    t.wait_ge(VS, vs_p2e)
                    t.transpose(p_bc[0:64, 0:128], st_bg[:, :], c_idf[:, :])
                    t.transpose(p_bc[0:1, 128:256], st_t[:, 225:226], c_idf[:, :])
                    t.transpose(p_bc[0:1, 256:384], st_t[:, 224:225],
                                c_idf[:, :]).then_inc(MM0, 1)
                    cinc("MM0")
                bk.tensor(ft)
                mm0_f = cget("MM0")

                def fv(v):
                    v.wait_ge(MM0, mm0_f)
                    vi("tensor_copy", bgst16[:, :], p_bc[0:64, 0:128])
                    vi("reduce_max", cc_sb[0:1, 0:1], p_bc[0:1, 128:256], AX)
                    vi("reduce_max", cc_sb[0:1, 1:2], p_bc[0:1, 256:384], AX)
                bk.vector(fv)
            vs_p2f = cget("VS")

            with nc.Block("p2g") as bk:
                def fg(g):
                    g.wait_ge(VS, vs_p2f)
                    dput("CC", g, g_bg, bgst16[:, :])
                    dput("CC", g, cc1_in, cc_sb[0:1, 0:2])
                    dwait(g, "CC", dtot("CC"))
                    g.collective_compute(
                        "AllReduce", ALU.max, replica_groups=[list(range(8))],
                        ins=[cc1_in], outs=[cc1_out]).then_inc(CCS, 1)
                    cinc("CCS")
                bk.gpsimd(fg)
                cc_bg = dtot("CC")

                def fsy(sy):
                    dwait(sy, "CC", cc_bg)
                    bgr = g_bg.rearrange("(a two) j -> two a j", two=2)
                    dput("BG", sy, bg16[:, :], bgr)
                bk.sync(fsy)
            bg_tok = dtot("BG")

            if PHLIM < 3:
                return nc
            def cast_pipe(bk, src, dst, ntiles, c2=False):
                vsb = cget("VS")
                npt = 3 if c2 else 1
                ltoks, stoks = [], []
                sg_base = [dtot(SG[0]), dtot(SG[1])]

                def fsy(sy):
                    for i in range(ntiles):
                        if i >= 2:
                            sy.wait_ge(VS, vsb + npt * (i - 1))
                        ltoks.append(dput(LD[i % 2], sy,
                                          xld[:, (i % 2) * D:(i % 2 + 1) * D],
                                          src[i * 128:(i + 1) * 128, :]))
                bk.sync(fsy)

                def fg(g):
                    for i in range(ntiles):
                        g.wait_ge(VS, vsb + npt * i + 1)
                        stoks.append(dput(SG[i % 2], g,
                                          dst[i * 128:(i + 1) * 128, :],
                                          wq16[:, (i % 2) * 2048:
                                               (i % 2) * 2048 + 1024]))
                bk.gpsimd(fg)

                def fv(v):
                    for i in range(ntiles):
                        dwait(v, LD[i % 2], ltoks[i])
                        if i >= 2:
                            dwait(v, SG[i % 2], stoks[i - 2])
                        else:
                            dwait(v, SG[i % 2], sg_base[i % 2])
                        vi("tensor_copy",
                           wq16[:, (i % 2) * 2048:(i % 2) * 2048 + 1024],
                           xld[:, (i % 2) * D:(i % 2 + 1) * D])
                        if c2:
                            vi("tensor_mul", ctmp[:, :],
                               xld[:, (i % 2) * D:(i % 2 + 1) * D], nb_bc[:, :])
                            vi("reduce_sum", c2bcol[:, i:i + 1], ctmp[:, :], AX)
                    if c2:
                        vi("tensor_add", c2bcol[:, :], c2bcol[:, :], bqcol[:, :])
                bk.vector(fv)

            with nc.Block("p2wq") as bk:
                cast_pipe(bk, wq_hbm, g_wq, O3 // 128, c2=True)
            with nc.Block("p2wp") as bk:
                cast_pipe(bk, pw_hbm, g_pw, D // 128)

            with nc.Block("p2wm") as bk:
                vsb = cget("VS")
                ltoks = []

                def fsy(sy):
                    for i in range(NH):
                        if i >= 2:
                            sy.wait_ge(VS, vsb + (i - 1))
                        ltoks.append(dput(LD[i % 2], sy,
                                          xld[:, (i % 2) * D:(i % 2 + 1) * D],
                                          w1_hbm[i * 128:(i + 1) * 128, :]))
                    for i in range(16):
                        if i >= 2:
                            sy.wait_ge(VS, vsb + NH + (i - 1))
                        else:
                            sy.wait_ge(VS, vsb + NH)
                        ltoks.append(dput(
                            LD[i % 2], sy,
                            w2ld[:, (i % 2) * 2048:(i % 2 + 1) * 2048],
                            w2_hbm[(i // 2) * 128:(i // 2 + 1) * 128,
                                   (i % 2) * 2048:(i % 2 + 1) * 2048]))
                bk.sync(fsy)

                def fv(v):
                    for i in range(NH):
                        dwait(v, LD[i % 2], ltoks[i])
                        vi("reduce_max", st_t[:, 64 + i:65 + i],
                           xld[:, (i % 2) * D:(i % 2 + 1) * D],
                           AX, apply_absolute_value=True)
                    for i in range(16):
                        dwait(v, LD[i % 2], ltoks[NH + i])
                        vi("reduce_max", st_t[:, 96 + i:97 + i],
                           w2ld[:, (i % 2) * 2048:(i % 2 + 1) * 2048],
                           AX, apply_absolute_value=True)
                    vi("reduce_max", st_t[:, 112:113], st_t[:, 64:96], AX)
                    vi("reduce_max", st_t[:, 113:114], st_t[:, 96:112], AX)
                bk.vector(fv)
            vs_wm = cget("VS")

            with nc.Block("p2wn") as bk:
                def ft(t):
                    t.wait_ge(VS, vs_wm)
                    t.transpose(p_bc[0:1, 0:128], st_t[:, 112:113], c_idf[:, :])
                    t.transpose(p_bc[0:1, 128:256], st_t[:, 113:114],
                                c_idf[:, :]).then_inc(MM0, 1)
                    cinc("MM0")
                bk.tensor(ft)
                mm0_w = cget("MM0")

                def fv(v):
                    v.wait_ge(MM0, mm0_w)
                    vi("reduce_max", scal[0:1, 4:5], p_bc[0:1, 0:128], AX)
                    vi("reduce_max", scal[0:1, 5:6], p_bc[0:1, 128:256], AX)
                    ts(scal[0:1, 0:1], scal[0:1, 4:5], INV127)
                    ts(scal[0:1, 0:1], scal[0:1, 0:1], np.float32(1e-8),
                       op0=ALU.max)
                    vi("reciprocal", scal[0:1, 1:2], scal[0:1, 0:1])
                    ts(scal[0:1, 2:3], scal[0:1, 5:6], INV127)
                    ts(scal[0:1, 2:3], scal[0:1, 2:3], np.float32(1e-8),
                       op0=ALU.max)
                    vi("reciprocal", scal[0:1, 3:4], scal[0:1, 2:3])
                bk.vector(fv)
            vs_wn = cget("VS")

            with nc.Block("p2wo") as bk:
                def ft(t):
                    t.wait_ge(VS, vs_wn)
                    mm(p_bc[:, 0:1], ones_f[:, :], scal[0:1, 1:2],
                       start=True, stop=True)
                    mm(p_bc[:, 2:3], ones_f[:, :], scal[0:1, 3:4],
                       start=True, stop=True).then_inc(MM0, 1)
                    cinc("MM0")
                bk.tensor(ft)
                mm0_o = cget("MM0")

                def fv(v):
                    v.wait_ge(MM0, mm0_o)
                    vi("tensor_copy", scl_b[:, 0:1], p_bc[:, 0:1])
                    vi("tensor_copy", scl_b[:, 1:2], p_bc[:, 2:3])
                bk.vector(fv)

            def quant_pipe(bk, src, dst, ntiles, wide, rcol):
                vsb = cget("VS")
                ltoks, stoks = [], []
                sg_base = [dtot(SG[0]), dtot(SG[1])]

                def fsy(sy):
                    for i in range(ntiles):
                        if i >= 2:
                            sy.wait_ge(VS, vsb + 2 * (i - 1))
                        if not wide:
                            ltoks.append(dput(
                                LD[i % 2], sy,
                                xld[:, (i % 2) * D:(i % 2 + 1) * D],
                                src[i * 128:(i + 1) * 128, :]))
                        else:
                            ltoks.append(dput(
                                LD[i % 2], sy,
                                w2ld[:, (i % 2) * 2048:(i % 2 + 1) * 2048],
                                src[(i // 2) * 128:(i // 2 + 1) * 128,
                                    (i % 2) * 2048:(i % 2 + 1) * 2048]))
                bk.sync(fsy)

                def fg(g):
                    for i in range(ntiles):
                        g.wait_ge(VS, vsb + 2 * (i + 1))
                        if not wide:
                            stoks.append(dput(SG[i % 2], g,
                                              dst[i * 128:(i + 1) * 128, :],
                                              wq16[:, (i % 2) * 2048:
                                                   (i % 2) * 2048 + 1024]))
                        else:
                            stoks.append(dput(
                                SG[i % 2], g,
                                dst[(i // 2) * 128:(i // 2 + 1) * 128,
                                    (i % 2) * 2048:(i % 2 + 1) * 2048],
                                wq16[:, (i % 2) * 2048:(i % 2 + 1) * 2048]))
                bk.gpsimd(fg)

                def fv(v):
                    for i in range(ntiles):
                        dwait(v, LD[i % 2], ltoks[i])
                        if i >= 2:
                            dwait(v, SG[i % 2], stoks[i - 2])
                        else:
                            dwait(v, SG[i % 2], sg_base[i % 2])
                        if not wide:
                            srcsl = xld[:, (i % 2) * D:(i % 2 + 1) * D]
                            isl = tbi[:, 0:1024]
                            osl = wq16[:, (i % 2) * 2048:(i % 2) * 2048 + 1024]
                        else:
                            srcsl = w2ld[:, (i % 2) * 2048:(i % 2 + 1) * 2048]
                            isl = tbi[:, 0:2048]
                            osl = wq16[:, (i % 2) * 2048:(i % 2 + 1) * 2048]
                        ts(isl, srcsl, rcol, np.float32(-127.0), ALU.mult,
                           ALU.max)
                        ts(osl, isl, np.float32(127.0), op0=ALU.min)
                bk.vector(fv)

            with nc.Block("p2q1") as bk:
                quant_pipe(bk, w1_hbm, g_w1, NH, False, scl_b[:, 0:1])
            with nc.Block("p2q2") as bk:
                quant_pipe(bk, w2_hbm, g_w2, 16, True, scl_b[:, 1:2])

            if PHLIM < 4:
                return nc
            with nc.Block("p2x") as bk:
                def fg(g):
                    dput("CC", g, g_c2, c2bcol[:, :])
                bk.gpsimd(fg)
                cc_c2 = dtot("CC")

                def fsy(sy):
                    dwait(sy, "CC", cc_c2)
                    dput("BG", sy, c2brow[:, :], g_c2.rearrange("p c -> c p"),
                         allow_slow_non_contiguous=True)
                bk.sync(fsy)
            bg_c2 = dtot("BG")

            with nc.Block("p2y1") as bk:
                def ft(t):
                    dwait(t, "BG", bg_c2)
                    mm(p_bc[:, 0:512], ones_f[:, :], c2brow[0:1, 0:512],
                       start=True, stop=True)
                    mm(p_bc[:, 512:1024], ones_f[:, :], c2brow[0:1, 512:1024],
                       start=True, stop=True).then_inc(MM0, 1)
                    cinc("MM0")
                bk.tensor(ft)
                mm0_y1 = cget("MM0")

                def fv(v):
                    v.wait_ge(MM0, mm0_y1)
                    vi("tensor_copy", c2b_bc[:, 0:1024], p_bc[:, :])
                bk.vector(fv)

            with nc.Block("p2y2") as bk:
                def ft(t):
                    mm(p_bc[:, 0:512], ones_f[:, :], c2brow[0:1, 1024:1536],
                       start=True, stop=True)
                    mm(p_bc[:, 512:1024], ones_f[:, :], c2brow[0:1, 1536:2048],
                       start=True, stop=True).then_inc(MM0, 1)
                    cinc("MM0")
                bk.tensor(ft)
                mm0_y2 = cget("MM0")

                def fv(v):
                    v.wait_ge(MM0, mm0_y2)
                    vi("tensor_copy", c2b_bc[:, 1024:2048], p_bc[:, :])
                bk.vector(fv)

            with nc.Block("p2y3") as bk:
                def ft(t):
                    mm(p_bc[:, 0:512], ones_f[:, :], pbrow[0:1, 0:512],
                       start=True, stop=True)
                    mm(p_bc[:, 512:1024], ones_f[:, :], pbrow[0:1, 512:1024],
                       start=True, stop=True).then_inc(MM0, 1)
                    cinc("MM0")
                bk.tensor(ft)
                mm0_y3 = cget("MM0")

                def fv(v):
                    v.wait_ge(MM0, mm0_y3)
                    vi("tensor_copy", pb_bc[:, :], p_bc[:, :])
                bk.vector(fv)

            with nc.Block("p1b") as bk:
                vsb = cget("VS")
                ltoks, stoks = [], []
                sg_base = [dtot(SG[0]), dtot(SG[1])]

                def fsy(sy):
                    for i in range(NT):
                        if i >= 2:
                            sy.wait_ge(VS, vsb + 2 * (i - 1))
                        ltoks.append(dput(LD[i % 2], sy,
                                          xld[:, (i % 2) * D:(i % 2 + 1) * D],
                                          x_hbm[i * 128:(i + 1) * 128, :]))
                bk.sync(fsy)

                def fg(g):
                    for i in range(NT):
                        g.wait_ge(VS, vsb + 2 * (i + 1))
                        dput(SG[i % 2], g, g_xn[i * 128:(i + 1) * 128, :],
                             xnst[:, (i % 2) * D:(i % 2 + 1) * D])
                        stoks.append(dput(SG[i % 2], g,
                                          g_xb[i * 128:(i + 1) * 128, :],
                                          xbst[:, (i % 2) * D:(i % 2 + 1) * D]))
                bk.gpsimd(fg)

                def fv(v):
                    for i in range(NT):
                        dwait(v, LD[i % 2], ltoks[i])
                        if i >= 2:
                            dwait(v, SG[i % 2], stoks[i - 2])
                        else:
                            dwait(v, SG[i % 2], sg_base[i % 2])
                        sl = xld[:, (i % 2) * D:(i % 2 + 1) * D]
                        ts(xnst[:, (i % 2) * D:(i % 2 + 1) * D], sl,
                           st_mu[:, i:i + 1], st_r1[:, i:i + 1],
                           ALU.subtract, ALU.mult)
                        vi("tensor_mul", xbst[:, (i % 2) * D:(i % 2 + 1) * D],
                           sl, c_nwb[:, :])
                bk.vector(fv)
            sg_p1b = [dtot(SG[0]), dtot(SG[1])]

        if PHLIM < 5:
            return nc
        with ExitStack() as ps:
            xbT = ps.enter_context(nc.sbuf_tensor("xbT", [128, ND * T], BF16))
            wbuf = ps.enter_context(nc.sbuf_tensor("wbuf", [128, 2 * 4096], BF16))
            vwb = ps.enter_context(nc.sbuf_tensor("vwb", [128, 2 * 1024], BF16))
            qkst = ps.enter_context(nc.sbuf_tensor("qkst", [128, 4 * 512], BF16))
            vst = ps.enter_context(nc.sbuf_tensor("vst", [128, 2 * 512], BF16))
            p_qk = ps.enter_context(nc.psum_tensor("p_qk", [128, 2 * 512], F32))
            p_v = ps.enter_context(nc.psum_tensor("p_v", [128, 2 * 512], F32))
            p_c = ps.enter_context(nc.psum_tensor("p_c", [2, 512], F32))

            with nc.Block("p3") as bk:
                vs3, ss3 = cget("VS"), cget("SS")
                bg_all = dtot("BG")
                txk, wtoks, vtoks = [], [], []
                qkstoks, vstoks = [], []

                def qpos(g):
                    return vs3 + 34 * (g // NT) + 2 + (g % NT) + 1

                def fsy(sy):
                    dwait(sy, SG[0], sg_p1b[0])
                    dwait(sy, SG[1], sg_p1b[1])
                    for ic in range(ND):
                        txk.append(dput("TX", sy, xbT[:, ic * T:(ic + 1) * T],
                                        g_xb[0:T, ic * 128:(ic + 1) * 128],
                                        transpose=True))
                    for oc in range(4):
                        if oc >= 2:
                            sy.wait_ge(FR3, oc - 1)
                        for ic in range(ND):
                            wtoks.append(dput(
                                WS[oc % 2], sy,
                                wbuf[:, (oc % 2) * 4096 + ic * 512:
                                     (oc % 2) * 4096 + (ic + 1) * 512],
                                g_wq[oc * 512:(oc + 1) * 512,
                                     ic * 128:(ic + 1) * 128], transpose=True))
                    for ot in range(ND):
                        if ot >= 2:
                            sy.wait_ge(FR3V, ot - 1)
                        for ic in range(ND):
                            vtoks.append(dput(
                                VSL[ot % 2], sy,
                                vwb[:, (ot % 2) * 1024 + ic * 128:
                                    (ot % 2) * 1024 + (ic + 1) * 128],
                                g_wq[2048 + ot * 128:2048 + (ot + 1) * 128,
                                     ic * 128:(ic + 1) * 128], transpose=True))
                bk.sync(fsy)

                def fg(g_):
                    for g in range(128):
                        oc, i = divmod(g, NT)
                        g_.wait_ge(VS, qpos(g))
                        dstt = g_q if oc < 2 else g_k
                        col = (oc % 2) * 512
                        qkstoks.append(dput(
                            S4[g % 4], g_,
                            dstt[i * 128:(i + 1) * 128, col:col + 512],
                            qkst[:, (g % 4) * 512:(g % 4 + 1) * 512]))
                    for g in range(64):
                        ot, tc = divmod(g, ND)
                        g_.wait_ge(VS, vs3 + 136 + 10 * ot + 2 + tc + 1)
                        vstoks.append(dput(
                            S2[g % 2], g_,
                            g_vT[ot * 128:(ot + 1) * 128,
                                 tc * 512:(tc + 1) * 512],
                            vst[:, (g % 2) * 512:(g % 2 + 1) * 512]))
                bk.gpsimd(fg)

                def ft(t):
                    dwait(t, "TX", txk[-1])
                    dwait(t, "BG", bg_all)
                    for oc in range(4):
                        dwait(t, WS[oc % 2], wtoks[oc * ND + ND - 1])
                        if oc >= 1:
                            t.wait_ge(VS, vs3 + 34 * (oc - 1) + 2)
                        inst = None
                        for ic in range(ND):
                            inst = mm(p_c[:, :], c_nwnb[:, 2 * ic:2 * ic + 2],
                                      wbuf[:, (oc % 2) * 4096 + ic * 512:
                                           (oc % 2) * 4096 + (ic + 1) * 512],
                                      start=(ic == 0), stop=(ic == ND - 1))
                        inst.then_inc(MMC, 1)
                        cinc("MMC")
                        for i in range(NT):
                            g = oc * NT + i
                            if g >= 2:
                                t.wait_ge(VS, qpos(g - 2))
                            sl = p_qk[:, (g % 2) * 512:(g % 2 + 1) * 512]
                            for ic in range(ND):
                                wmm = mm(
                                    sl,
                                    xbT[:, ic * T + i * 128:ic * T + (i + 1) * 128],
                                    wbuf[:, (oc % 2) * 4096 + ic * 512:
                                         (oc % 2) * 4096 + (ic + 1) * 512],
                                    start=(ic == 0), stop=False)
                            if i == NT - 1:
                                wmm.then_inc(FR3, 1)
                                cinc("FR3")
                            inst = mm(sl, bg16[0:1, i * 128:(i + 1) * 128],
                                      crow16[0:1, oc * 512:(oc + 1) * 512],
                                      start=False, stop=True)
                            inst._wait_ge(VS, vs3 + 34 * oc + 2)
                            inst.then_inc(MMQ, 1)
                            cinc("MMQ")
                    for ot in range(ND):
                        dwait(t, VSL[ot % 2], vtoks[ot * ND + ND - 1])
                        t.wait_ge(VS, (vs3 + 34 * 3 + 2) if ot == 0
                                  else (vs3 + 136 + 10 * ot))
                        inst = None
                        for ic in range(ND):
                            inst = mm(p_c[:, 0:128], c_nwnb[:, 2 * ic:2 * ic + 2],
                                      vwb[:, (ot % 2) * 1024 + ic * 128:
                                          (ot % 2) * 1024 + (ic + 1) * 128],
                                      start=(ic == 0), stop=(ic == ND - 1))
                        inst.then_inc(MMC, 1)
                        cinc("MMC")
                        for tc in range(ND):
                            g = ot * ND + tc
                            if g >= 2:
                                otp, tcp = divmod(g - 2, ND)
                                t.wait_ge(VS, vs3 + 136 + 10 * otp + 2 + tcp + 1)
                            sl = p_v[:, (g % 2) * 512:(g % 2 + 1) * 512]
                            for ic in range(ND):
                                wmm = mm(
                                    sl,
                                    vwb[:, (ot % 2) * 1024 + ic * 128:
                                        (ot % 2) * 1024 + (ic + 1) * 128],
                                    xbT[:, ic * T + tc * 512:
                                        ic * T + (tc + 1) * 512],
                                    start=(ic == 0), stop=False)
                            if tc == ND - 1:
                                wmm.then_inc(FR3V, 1)
                                cinc("FR3V")
                            inst = mm(sl,
                                      crow16[:, 2048 + ot * 128:
                                             2048 + (ot + 1) * 128],
                                      bg16[:, tc * 512:(tc + 1) * 512],
                                      start=False, stop=True)
                            inst._wait_ge(VS, vs3 + 136 + 10 * ot + 2)
                            inst.then_inc(MMV, 1)
                            cinc("MMV")
                bk.tensor(ft)

                def fv(v):
                    for oc in range(4):
                        v.wait_ge(MMC, oc + 1)
                        vi("tensor_add", crow[:, oc * 512:(oc + 1) * 512],
                           p_c[:, :], r_qkvb[:, oc * 512:(oc + 1) * 512])
                        vi("tensor_copy", crow16[:, oc * 512:(oc + 1) * 512],
                           crow[:, oc * 512:(oc + 1) * 512])
                        for i in range(NT):
                            g = oc * NT + i
                            v.wait_ge(MMQ, g + 1)
                            if g >= 4:
                                dwait(v, S4[g % 4], qkstoks[g - 4])
                            vi("scalar_tensor_tensor",
                               qkst[:, (g % 4) * 512:(g % 4 + 1) * 512],
                               p_qk[:, (g % 2) * 512:(g % 2 + 1) * 512],
                               st_al[:, i:i + 1],
                               c2b_bc[:, oc * 512:(oc + 1) * 512],
                               ALU.mult, ALU.add)
                    for ot in range(ND):
                        v.wait_ge(MMC, 4 + ot + 1)
                        vi("tensor_add", crow[:, 2048 + ot * 128:
                                              2048 + (ot + 1) * 128],
                           p_c[:, 0:128], r_qkvb[:, 2048 + ot * 128:
                                                 2048 + (ot + 1) * 128])
                        vi("tensor_copy", crow16[:, 2048 + ot * 128:
                                                 2048 + (ot + 1) * 128],
                           crow[:, 2048 + ot * 128:2048 + (ot + 1) * 128])
                        for tc in range(ND):
                            g = ot * ND + tc
                            v.wait_ge(MMV, g + 1)
                            if g >= 2:
                                dwait(v, S2[g % 2], vstoks[g - 2])
                            vi("tensor_copy",
                               vst[:, (g % 2) * 512:(g % 2 + 1) * 512],
                               p_v[:, (g % 2) * 512:(g % 2 + 1) * 512])
                bk.vector(fv)
            s4_p3 = [dtot(x) for x in S4]
            s2_p3 = [dtot(x) for x in S2]

        if PHLIM < 6:
            return nc
        with ExitStack() as ps:
            xnld = ps.enter_context(nc.sbuf_tensor("xnld", [128, 2 * D], F32))
            xqst = ps.enter_context(nc.sbuf_tensor("xqst", [128, 2 * D], BF16))
            tbi4 = ps.enter_context(nc.sbuf_tensor("tbi4", [128, 1024], I32))
            tb1 = ps.enter_context(nc.sbuf_tensor("tb1", [1, 1], I32))
            p_b4 = ps.enter_context(nc.psum_tensor("p_b4", [128, 8], F32))

            with nc.Block("p4a") as bk:
                def fg(g):
                    g.wait_ge(CCS, 1)
                    dput("CC", g, cc_sb[0:1, 0:2], cc1_out)
                bk.gpsimd(fg)
                cc_r1 = dtot("CC")

                def fv(v):
                    dwait(v, "CC", cc_r1)
                    vi("tensor_add", scal[0:1, 6:7], cc_sb[0:1, 0:1],
                       cc_sb[0:1, 1:2])
                    ts(scal[0:1, 7:8], scal[0:1, 6:7], INV255)
                    ts(scal[0:1, 7:8], scal[0:1, 7:8], np.float32(1e-8),
                       op0=ALU.max)
                    vi("reciprocal", scal[0:1, 8:9], scal[0:1, 7:8])
                    vi("tensor_mul", scal[0:1, 9:10], cc_sb[0:1, 1:2],
                       scal[0:1, 8:9])
                    vi("tensor_copy", tb1[0:1, 0:1], scal[0:1, 9:10])
                    vi("tensor_copy", scal[0:1, 10:11], tb1[0:1, 0:1])
                    ts(scal[0:1, 10:11], scal[0:1, 10:11], np.float32(-128.0),
                       op0=ALU.add)
                    ts(scal[0:1, 10:11], scal[0:1, 10:11], np.float32(-128.0),
                       op0=ALU.max)
                    ts(scal[0:1, 10:11], scal[0:1, 10:11], np.float32(127.0),
                       op0=ALU.min)
                    ts(scal[0:1, 11:12], scal[0:1, 10:11], NEG1,
                       np.float32(-128.0), ALU.mult, ALU.add)
                    ts(scal[0:1, 12:13], scal[0:1, 10:11], NEG1,
                       np.float32(127.0), ALU.mult, ALU.add)
                    vi("tensor_mul", scal[0:1, 13:14], scal[0:1, 7:8],
                       scal[0:1, 0:1])
                bk.vector(fv)
            vs_4a = cget("VS")

            with nc.Block("p4b") as bk:
                def ft(t):
                    t.wait_ge(VS, vs_4a)
                    mm(p_b4[:, 0:4], ones_f[:, :], scal[0:1, 8:12],
                       start=True, stop=True)
                    mm(p_b4[:, 4:8], ones_f[:, :], scal[0:1, 10:14],
                       start=True, stop=True).then_inc(MM0, 1)
                    cinc("MM0")
                bk.tensor(ft)
                mm0_4 = cget("MM0")

                def fv(v):
                    v.wait_ge(MM0, mm0_4)
                    vi("tensor_copy", scl_b[:, 2:3], p_b4[:, 0:1])
                    vi("tensor_copy", scl_b[:, 3:4], p_b4[:, 5:6])
                    vi("tensor_copy", scl_b[:, 4:5], p_b4[:, 6:7])
                    vi("tensor_copy", scl_b[:, 5:6], p_b4[:, 7:8])
                bk.vector(fv)

            with nc.Block("p4c") as bk:
                vsb = cget("VS")
                ltoks, stoks = [], []
                sg_base = [dtot(SG[0]), dtot(SG[1])]
                s4b = [dtot(x) for x in S4]
                s2b = [dtot(x) for x in S2]

                def fsy(sy):
                    for ix in range(4):
                        dwait(sy, S4[ix], s4b[ix])
                    dwait(sy, S2[0], s2b[0])
                    dwait(sy, S2[1], s2b[1])
                    for i in range(NT):
                        if i >= 2:
                            sy.wait_ge(VS, vsb + 2 * (i - 1))
                        ltoks.append(dput(LD[i % 2], sy,
                                          xnld[:, (i % 2) * D:(i % 2 + 1) * D],
                                          g_xn[i * 128:(i + 1) * 128, :]))
                bk.sync(fsy)

                def fg(g):
                    for i in range(NT):
                        g.wait_ge(VS, vsb + 2 * (i + 1))
                        stoks.append(dput(SG[i % 2], g,
                                          g_xqz[i * 128:(i + 1) * 128, :],
                                          xqst[:, (i % 2) * D:(i % 2 + 1) * D]))
                bk.gpsimd(fg)

                def fv(v):
                    for i in range(NT):
                        dwait(v, LD[i % 2], ltoks[i])
                        if i >= 2:
                            dwait(v, SG[i % 2], stoks[i - 2])
                        else:
                            dwait(v, SG[i % 2], sg_base[i % 2])
                        ts(tbi4[:, :], xnld[:, (i % 2) * D:(i % 2 + 1) * D],
                           scl_b[:, 2:3], scl_b[:, 3:4], ALU.mult, ALU.max)
                        ts(xqst[:, (i % 2) * D:(i % 2 + 1) * D], tbi4[:, :],
                           scl_b[:, 4:5], op0=ALU.min)
                bk.vector(fv)
            sg_p4c = [dtot(SG[0]), dtot(SG[1])]

        if PHLIM < 7:
            return nc
        with ExitStack() as ps:
            xqzT = ps.enter_context(nc.sbuf_tensor("xqzT", [128, ND * T], BF16))
            w1h = ps.enter_context(nc.sbuf_tensor("w1h", [128, 2 * 8192], BF16))
            ghst = ps.enter_context(nc.sbuf_tensor("ghst", [128, 2 * 2048], F32))
            p_f = ps.enter_context(nc.psum_tensor("p_f", [128, 8 * 512], F32))

            with nc.Block("p5") as bk:
                vs5, ss5 = cget("VS"), cget("SS")
                txk, w1t, stoks = [], [], []
                sg_base = [dtot(SG[0]), dtot(SG[1])]

                def fsy(sy):
                    dwait(sy, SG[0], sg_p4c[0])
                    dwait(sy, SG[1], sg_p4c[1])
                    for ix in range(4):
                        dwait(sy, S4[ix], s4_p3[ix])
                    dwait(sy, S2[0], s2_p3[0])
                    dwait(sy, S2[1], s2_p3[1])
                    for ic in range(ND):
                        txk.append(dput("TX", sy, xqzT[:, ic * T:(ic + 1) * T],
                                        g_xqz[0:T, ic * 128:(ic + 1) * 128],
                                        transpose=True))
                    for qq in range(4):
                        if qq >= 2:
                            sy.wait_ge(MM5, 16 * (qq - 1))
                        for ic in range(ND):
                            w1t.append(dput(
                                WS[qq % 2], sy,
                                w1h[:, (qq % 2) * 8192 + ic * 1024:
                                    (qq % 2) * 8192 + (ic + 1) * 1024],
                                g_w1[qq * 1024:(qq + 1) * 1024,
                                     ic * 128:(ic + 1) * 128], transpose=True))
                bk.sync(fsy)

                def fg(g):
                    for n in range(64):
                        ot, gg = divmod(n, 2)
                        g.wait_ge(SS, ss5 + 4 * (n + 1))
                        stoks.append(dput(SG[n % 2], g,
                                          g_gh[ot * 128:(ot + 1) * 128,
                                               gg * 2048:(gg + 1) * 2048],
                                          ghst[:, (n % 2) * 2048:
                                               (n % 2 + 1) * 2048]))
                bk.gpsimd(fg)

                def ft(t):
                    dwait(t, "TX", txk[-1])
                    for ot in range(NH):
                        qq = ot // 8
                        if ot % 8 == 0:
                            dwait(t, WS[qq % 2], w1t[qq * ND + ND - 1])
                        base = (qq % 2) * 8192
                        for gg in range(2):
                            n = 2 * ot + gg
                            if n >= 2:
                                t.wait_ge(SS, ss5 + 4 * (n - 1))
                            inst = None
                            for ic in range(ND):
                                for tc in range(4 * gg, 4 * gg + 4):
                                    inst = mm(
                                        p_f[:, tc * 512:(tc + 1) * 512],
                                        w1h[:, base + ic * 1024 + (ot % 8) * 128:
                                            base + ic * 1024 + (ot % 8) * 128 + 128],
                                        xqzT[:, ic * T + tc * 512:
                                             ic * T + (tc + 1) * 512],
                                        start=(ic == 0), stop=(ic == ND - 1))
                            inst.then_inc(MM5, 1)
                            cinc("MM5")
                bk.tensor(ft)

                def fs(s):
                    for n in range(64):
                        ot, gg = divmod(n, 2)
                        s.wait_ge(MM5, n + 1)
                        if n >= 2:
                            dwait(s, SG[n % 2], stoks[n - 2])
                            s.wait_ge(VS, vs5 + 2 * (n - 1))
                        else:
                            dwait(s, SG[n % 2], sg_base[n % 2])
                        for j in range(4):
                            tc = 4 * gg + j
                            si("activation",
                               ghst[:, (n % 2) * 2048 + j * 512:
                                    (n % 2) * 2048 + (j + 1) * 512],
                               p_f[:, tc * 512:(tc + 1) * 512], AF.Gelu,
                               scale=scl_b[:, 5:6], bias=c_b1c[:, ot:ot + 1])
                bk.scalar(fs)

                def fv(v):
                    for n in range(64):
                        v.wait_ge(SS, ss5 + 4 * (n + 1))
                        sl = ghst[:, (n % 2) * 2048:(n % 2 + 1) * 2048]
                        vi("tensor_reduce", gmn[:, n:n + 1], sl, AX, ALU.min)
                        vi("reduce_max", gmx[:, n:n + 1], sl, AX)
                bk.vector(fv)
            sg_p5 = [dtot(SG[0]), dtot(SG[1])]

        if PHLIM < 8:
            return nc
        with ExitStack() as ps:
            p_b6 = ps.enter_context(nc.psum_tensor("p_b6", [128, 512], F32))

            with nc.Block("p6a") as bk:
                def fv(v):
                    vi("tensor_reduce", st_t[:, 0:1], gmn, AX, ALU.min)
                    vi("reduce_max", st_t[:, 1:2], gmx, AX)
                    ts(st_t[:, 0:1], st_t[:, 0:1], NEG1)
                bk.vector(fv)
            vs6 = cget("VS")

            with nc.Block("p6b") as bk:
                def ft(t):
                    t.wait_ge(VS, vs6)
                    t.transpose(p_b6[0:1, 0:128], st_t[:, 1:2], c_idf[:, :])
                    t.transpose(p_b6[0:1, 128:256], st_t[:, 0:1],
                                c_idf[:, :]).then_inc(MM0, 1)
                    cinc("MM0")
                bk.tensor(ft)
                mm06 = cget("MM0")

                def fv(v):
                    v.wait_ge(MM0, mm06)
                    vi("reduce_max", cc_sb[0:1, 0:1], p_b6[0:1, 0:128], AX)
                    vi("reduce_max", cc_sb[0:1, 1:2], p_b6[0:1, 128:256], AX)
                bk.vector(fv)
            vs6b = cget("VS")

            with nc.Block("p6c") as bk:
                def fg(g):
                    g.wait_ge(VS, vs6b)
                    dput("CC", g, cc2_in, cc_sb[0:1, 0:2])
                    dwait(g, "CC", dtot("CC"))
                    g.collective_compute(
                        "AllReduce", ALU.max, replica_groups=[list(range(8))],
                        ins=[cc2_in], outs=[cc2_out]).then_inc(CCS, 1)
                    cinc("CCS")
                bk.gpsimd(fg)

        if PHLIM < 9:
            return nc
        with ExitStack() as ps:
            qb = ps.enter_context(nc.sbuf_tensor("qb", [128, 2 * D], BF16))
            kb = ps.enter_context(nc.sbuf_tensor("kb", [128, 2 * D], BF16))
            wst = ps.enter_context(nc.sbuf_tensor("wst", [128, 2 * D], BF16))
            p_S = ps.enter_context(nc.psum_tensor("p_S", [128, 4 * 1024], F32))

            with nc.Block("p7") as bk:
                vs7, ss7 = cget("VS"), cget("SS")
                ltoks, stoks = [], []
                sg_base = [dtot(SG[0]), dtot(SG[1])]

                def fsy(sy):
                    for ix in range(4):
                        dwait(sy, S4[ix], s4_p3[ix])
                    dwait(sy, S2[0], s2_p3[0])
                    dwait(sy, S2[1], s2_p3[1])
                    dwait(sy, SG[0], sg_p5[0])
                    dwait(sy, SG[1], sg_p5[1])
                    for grp in range(2):
                        for i_ in range(NT):
                            i = grp * NT + i_
                            if i >= 2:
                                sy.wait_ge(MMS, i - 1)
                            dput(LD[i % 2], sy,
                                 qb[:, (i % 2) * D:(i % 2 + 1) * D],
                                 g_q[i_ * 128:(i_ + 1) * 128, :])
                            ltoks.append(dput(
                                LD[i % 2], sy,
                                kb[:, (i % 2) * D:(i % 2 + 1) * D],
                                g_k[i_ * 128:(i_ + 1) * 128, :]))
                bk.sync(fsy)

                def fg(g):
                    for f1 in range(ND):
                        g.wait_ge(SS, ss7 + f1 + 1)
                        stoks.append(dput(SG[f1 % 2], g,
                                          g_w[f1 * 128:(f1 + 1) * 128, :],
                                          wst[:, (f1 % 2) * D:(f1 % 2 + 1) * D]))
                bk.gpsimd(fg)

                def ft(t):
                    for grp in range(2):
                        if grp == 1:
                            t.wait_ge(SS, ss7 + 4)
                        for i_ in range(NT):
                            i = grp * NT + i_
                            dwait(t, LD[i % 2], ltoks[i])
                            inst = None
                            for j in range(4):
                                f1 = 4 * grp + j
                                for hh in range(2):
                                    inst = mm(
                                        p_S[:, j * 1024 + hh * 512:
                                            j * 1024 + (hh + 1) * 512],
                                        qb[:, (i % 2) * D + f1 * 128:
                                           (i % 2) * D + (f1 + 1) * 128],
                                        kb[:, (i % 2) * D + hh * 512:
                                           (i % 2) * D + (hh + 1) * 512],
                                        start=(i_ == 0), stop=(i_ == NT - 1))
                            inst.then_inc(MMS, 1)
                            cinc("MMS")
                bk.tensor(ft)

                def fs(s):
                    for grp in range(2):
                        for j in range(4):
                            f1 = 4 * grp + j
                            s.wait_ge(VS, vs7 + grp * 12 + 2 * (j + 1))
                            if f1 >= 2:
                                dwait(s, SG[f1 % 2], stoks[f1 - 2])
                            else:
                                dwait(s, SG[f1 % 2], sg_base[f1 % 2])
                            si("activation",
                               wst[:, (f1 % 2) * D:(f1 % 2 + 1) * D],
                               p_S[:, j * 1024:(j + 1) * 1024], AF.Exp,
                               scale=SCL, bias=nm_col[:, f1:f1 + 1],
                               accum_out=st_t[:, 16 + f1:17 + f1])
                bk.scalar(fs)

                def fv(v):
                    for grp in range(2):
                        v.wait_ge(MMS, NT * (grp + 1))
                        for j in range(4):
                            f1 = 4 * grp + j
                            vi("reduce_max", st_t[:, 8 + f1:9 + f1],
                               p_S[:, j * 1024:(j + 1) * 1024], AX)
                            ts(nm_col[:, f1:f1 + 1], st_t[:, 8 + f1:9 + f1],
                               NEGSCL)
                        for j in range(4):
                            f1 = 4 * grp + j
                            v.wait_ge(SS, ss7 + 4 * grp + j + 1)
                            vi("reciprocal", st_rr[:, f1:f1 + 1],
                               st_t[:, 16 + f1:17 + f1])
                bk.vector(fv)
            sg_p7 = [dtot(SG[0]), dtot(SG[1])]

        if PHLIM < 10:
            return nc
        with ExitStack() as ps:
            aT = ps.enter_context(nc.sbuf_tensor("aT", [128, ND * T], BF16))
            wT8 = ps.enter_context(nc.sbuf_tensor("wT8", [128, ND * D], BF16))
            vb = ps.enter_context(nc.sbuf_tensor("vb", [128, 2 * 4096], BF16))
            xn9 = ps.enter_context(nc.sbuf_tensor("xn9", [128, 2 * D], F32))
            sst9 = ps.enter_context(nc.sbuf_tensor("sst9", [128, 2 * D], F32))
            p_a = ps.enter_context(nc.psum_tensor("p_a", [128, 8 * 512], F32))

            with nc.Block("p8") as bk:
                ss8 = cget("SS")
                txk, vbt = [], []

                def fsy(sy):
                    dwait(sy, SG[0], sg_p7[0])
                    dwait(sy, SG[1], sg_p7[1])
                    for ix in range(4):
                        dwait(sy, S4[ix], s4_p3[ix])
                    dwait(sy, S2[0], s2_p3[0])
                    dwait(sy, S2[1], s2_p3[1])
                    for f2 in range(ND):
                        txk.append(dput("TX", sy, wT8[:, f2 * D:(f2 + 1) * D],
                                        g_w[0:D, f2 * 128:(f2 + 1) * 128],
                                        transpose=True))
                    for tc in range(ND):
                        if tc >= 2:
                            sy.wait_ge(MM8, 2 * (tc - 1))
                        for f2 in range(ND):
                            vbt.append(dput(
                                VSL[tc % 2], sy,
                                vb[:, (tc % 2) * 4096 + f2 * 512:
                                   (tc % 2) * 4096 + (f2 + 1) * 512],
                                g_vT[f2 * 128:(f2 + 1) * 128,
                                     tc * 512:(tc + 1) * 512]))
                bk.sync(fsy)

                def ft(t):
                    dwait(t, "TX", txk[-1])
                    for tc in range(ND):
                        dwait(t, VSL[tc % 2], vbt[tc * ND + ND - 1])
                        for gg in range(2):
                            n = 2 * tc + gg
                            if n >= 2:
                                t.wait_ge(SS, ss8 + 4 * (n - 1))
                            inst = None
                            for f2 in range(ND):
                                for jj in range(4):
                                    j = 4 * gg + jj
                                    inst = mm(
                                        p_a[:, j * 512:(j + 1) * 512],
                                        wT8[:, f2 * D + j * 128:
                                            f2 * D + (j + 1) * 128],
                                        vb[:, (tc % 2) * 4096 + f2 * 512:
                                           (tc % 2) * 4096 + (f2 + 1) * 512],
                                        start=(f2 == 0), stop=(f2 == ND - 1))
                            inst.then_inc(MM8, 1)
                            cinc("MM8")
                bk.tensor(ft)

                def fs(s):
                    for n in range(2 * ND):
                        tc, gg = divmod(n, 2)
                        s.wait_ge(MM8, n + 1)
                        for jj in range(4):
                            j = 4 * gg + jj
                            si("activation",
                               aT[:, j * T + tc * 512:j * T + (tc + 1) * 512],
                               p_a[:, j * 512:(j + 1) * 512], AF.Copy,
                               scale=st_rr[:, j:j + 1])
                bk.scalar(fs)

            with nc.Block("p9") as bk:
                vs9 = cget("VS")
                ss9 = cget("SS")
                txk9, ltoks, stoks = [], [], []
                sg_base = [dtot(SG[0]), dtot(SG[1])]

                def fsy(sy):
                    for jc in range(ND):
                        txk9.append(dput("TX", sy, wT8[:, jc * D:(jc + 1) * D],
                                         g_pw[0:D, jc * 128:(jc + 1) * 128],
                                         transpose=True))
                    for i in range(NT):
                        if i >= 2:
                            sy.wait_ge(VS, vs9 + 2 * (i - 1))
                        ltoks.append(dput(LD[i % 2], sy,
                                          xn9[:, (i % 2) * D:(i % 2 + 1) * D],
                                          g_xn[i * 128:(i + 1) * 128, :]))
                bk.sync(fsy)

                def fg(g):
                    for i in range(NT):
                        g.wait_ge(VS, vs9 + 2 * (i + 1))
                        stoks.append(dput(SG[i % 2], g,
                                          g_s[i * 128:(i + 1) * 128, :],
                                          sst9[:, (i % 2) * D:(i % 2 + 1) * D]))
                bk.gpsimd(fg)

                def ft(t):
                    dwait(t, "TX", txk9[-1])
                    t.wait_ge(SS, ss9)
                    for i in range(NT):
                        if i >= 2:
                            t.wait_ge(VS, vs9 + 2 * (i - 2) + 1)
                        sl = p_a[:, (i % 2) * 1024:(i % 2 + 1) * 1024]
                        inst = None
                        for jc in range(ND):
                            for hh in range(2):
                                inst = mm(sl[:, hh * 512:(hh + 1) * 512],
                                          aT[:, jc * T + i * 128:
                                             jc * T + (i + 1) * 128],
                                          wT8[:, jc * D + hh * 512:
                                              jc * D + (hh + 1) * 512],
                                          start=(jc == 0), stop=(jc == ND - 1))
                        inst.then_inc(MM9, 1)
                        cinc("MM9")
                bk.tensor(ft)

                def fv(v):
                    for i in range(NT):
                        v.wait_ge(MM9, i + 1)
                        dwait(v, LD[i % 2], ltoks[i])
                        if i >= 2:
                            dwait(v, SG[i % 2], stoks[i - 2])
                        else:
                            dwait(v, SG[i % 2], sg_base[i % 2])
                        sl = p_a[:, (i % 2) * 1024:(i % 2 + 1) * 1024]
                        vi("scalar_tensor_tensor",
                           sst9[:, (i % 2) * D:(i % 2 + 1) * D], sl,
                           st_al[:, i:i + 1],
                           xn9[:, (i % 2) * D:(i % 2 + 1) * D],
                           ALU.mult, ALU.add)
                        vi("tensor_add",
                           sst9[:, (i % 2) * D:(i % 2 + 1) * D],
                           sst9[:, (i % 2) * D:(i % 2 + 1) * D],
                           pb_bc[:, :])
                bk.vector(fv)
            sg_p9 = [dtot(SG[0]), dtot(SG[1])]

        if PHLIM < 11:
            return nc
        with ExitStack() as ps10a:
            tb1b = ps10a.enter_context(nc.sbuf_tensor("tb1b", [1, 1], I32))
            p_b10 = ps10a.enter_context(nc.psum_tensor("p_b10", [128, 8], F32))

            with nc.Block("p10a") as bk:
                def fg(g):
                    g.wait_ge(CCS, 2)
                    dput("CC", g, cc_sb[0:1, 0:2], cc2_out)
                bk.gpsimd(fg)
                cc_r2 = dtot("CC")

                def fv(v):
                    dwait(v, "CC", cc_r2)
                    vi("tensor_add", scal[0:1, 16:17], cc_sb[0:1, 0:1],
                       cc_sb[0:1, 1:2])
                    ts(scal[0:1, 17:18], scal[0:1, 16:17], INV255)
                    ts(scal[0:1, 17:18], scal[0:1, 17:18], np.float32(1e-8),
                       op0=ALU.max)
                    vi("reciprocal", scal[0:1, 18:19], scal[0:1, 17:18])
                    vi("tensor_mul", scal[0:1, 19:20], cc_sb[0:1, 1:2],
                       scal[0:1, 18:19])
                    vi("tensor_copy", tb1b[0:1, 0:1], scal[0:1, 19:20])
                    vi("tensor_copy", scal[0:1, 20:21], tb1b[0:1, 0:1])
                    ts(scal[0:1, 20:21], scal[0:1, 20:21], np.float32(-128.0),
                       op0=ALU.add)
                    ts(scal[0:1, 20:21], scal[0:1, 20:21], np.float32(-128.0),
                       op0=ALU.max)
                    ts(scal[0:1, 20:21], scal[0:1, 20:21], np.float32(127.0),
                       op0=ALU.min)
                    ts(scal[0:1, 21:22], scal[0:1, 20:21], NEG1,
                       np.float32(-128.0), ALU.mult, ALU.add)
                    ts(scal[0:1, 22:23], scal[0:1, 20:21], NEG1,
                       np.float32(127.0), ALU.mult, ALU.add)
                    vi("tensor_mul", scal[0:1, 23:24], scal[0:1, 17:18],
                       scal[0:1, 2:3])
                    vi("reciprocal", scal[0:1, 24:25], scal[0:1, 23:24])
                    ts(r_b2r[:, :], r_b2[:, :], scal[0:1, 24:25])
                    vi("tensor_copy", r_b2r16[:, :], r_b2r[:, :])
                bk.vector(fv)
            vs_10a = cget("VS")

            with nc.Block("p10c") as bk:
                def ft(t):
                    t.wait_ge(VS, vs_10a)
                    mm(p_b10[:, 0:4], ones_f[:, :], scal[0:1, 18:22],
                       start=True, stop=True)
                    mm(p_b10[:, 4:8], ones_f[:, :], scal[0:1, 22:26],
                       start=True, stop=True).then_inc(MM0, 1)
                    cinc("MM0")
                bk.tensor(ft)
                mm010 = cget("MM0")

                def fv(v):
                    v.wait_ge(MM0, mm010)
                    vi("tensor_copy", scl_b[:, 6:7], p_b10[:, 0:1])
                    vi("tensor_copy", scl_b[:, 7:8], p_b10[:, 3:4])
                    vi("tensor_copy", scl_b[:, 0:1], p_b10[:, 4:5])
                    vi("tensor_copy", scl_b[:, 1:2], p_b10[:, 5:6])
                bk.vector(fv)

        if PHLIM < 12:
            return nc
        with ExitStack() as ps:
            hqT = ps.enter_context(nc.sbuf_tensor("hqT", [128, NH * 512], BF16))
            w2T = ps.enter_context(nc.sbuf_tensor("w2T", [128, NH * D], BF16))
            ghld = ps.enter_context(nc.sbuf_tensor("ghld", [128, 2 * 512], F32))
            tbq = ps.enter_context(nc.sbuf_tensor("tbq", [128, 512], I32))
            xl10 = ps.enter_context(nc.sbuf_tensor("xl10", [128, 2 * D], F32))
            sl10 = ps.enter_context(nc.sbuf_tensor("sl10", [128, 2 * D], F32))
            gst = ps.enter_context(nc.sbuf_tensor("gst", [128, 2 * D], F32))
            ost = ps.enter_context(nc.sbuf_tensor("ost", [128, 2 * D], F32))
            p_o = ps.enter_context(nc.psum_tensor("p_o", [128, 4 * 1024], F32))

            with nc.Block("p10") as bk:
                vs10, ss10 = cget("VS"), cget("SS")
                mm0b = cget("MM0")
                NBLK = 8
                VPB = 72
                txk, lxs, ghtoks, otoks = [], [], [], []

                def q_done(i):
                    b_, o_ = divmod(i, NH)
                    return vs10 + b_ * VPB + 2 * (o_ + 1)

                def comb_done(i):
                    b_, t_ = divmod(i, 4)
                    return vs10 + b_ * VPB + 64 + 2 * (t_ + 1)

                def fsy(sy):
                    for oc in range(NH):
                        txk.append(dput("TX", sy, w2T[:, oc * D:(oc + 1) * D],
                                        g_w2[0:D, oc * 128:(oc + 1) * 128],
                                        transpose=True))
                    dwait(sy, SG[0], sg_p9[0])
                    dwait(sy, SG[1], sg_p9[1])
                    for i in range(NT):
                        if i >= 2:
                            sy.wait_ge(VS, comb_done(i - 2))
                            otoks.append(dput(
                                OUTS[i % 2], sy,
                                out_hbm[(i - 2) * 128:(i - 1) * 128, :],
                                ost[:, (i % 2) * D:(i % 2 + 1) * D]))
                        dput(LD[i % 2], sy, xl10[:, (i % 2) * D:(i % 2 + 1) * D],
                             x_hbm[i * 128:(i + 1) * 128, :])
                        lxs.append(dput(LD[i % 2], sy,
                                        sl10[:, (i % 2) * D:(i % 2 + 1) * D],
                                        g_s[i * 128:(i + 1) * 128, :]))
                    for i in range(NT - 2, NT):
                        sy.wait_ge(VS, comb_done(i))
                        otoks.append(dput(OUTS[i % 2], sy,
                                          out_hbm[i * 128:(i + 1) * 128, :],
                                          ost[:, (i % 2) * D:(i % 2 + 1) * D]))
                    dwait(sy, OUTS[0], dtot(OUTS[0]))
                    dwait(sy, OUTS[1], dtot(OUTS[1]))
                bk.sync(fsy)

                def fg(g):
                    dwait(g, SG[0], sg_p9[0])
                    dwait(g, SG[1], sg_p9[1])
                    for blk_ in range(NBLK):
                        for oc in range(NH):
                            i = blk_ * NH + oc
                            if i >= 2:
                                g.wait_ge(VS, q_done(i - 2))
                            ghtoks.append(dput(
                                GH[i % 2], g,
                                ghld[:, (i % 2) * 512:(i % 2 + 1) * 512],
                                g_gh[oc * 128:(oc + 1) * 128,
                                     blk_ * 512:(blk_ + 1) * 512]))
                bk.gpsimd(fg)

                def fv(v):
                    for blk_ in range(NBLK):
                        for oc in range(NH):
                            i = blk_ * NH + oc
                            dwait(v, GH[i % 2], ghtoks[i])
                            if blk_ >= 1 and oc == 0:
                                v.wait_ge(MM0, mm0b + 4 * blk_)
                            ts(tbq[:, :],
                               ghld[:, (i % 2) * 512:(i % 2 + 1) * 512],
                               scl_b[:, 6:7], scl_b[:, 7:8], ALU.mult, ALU.max)
                            ts(hqT[:, oc * 512:(oc + 1) * 512], tbq[:, :],
                               scl_b[:, 0:1], op0=ALU.min)
                        for tt in range(4):
                            i = blk_ * 4 + tt
                            v.wait_ge(SS, ss10 + i + 1)
                            dwait(v, LD[i % 2], lxs[i])
                            if i >= 2:
                                dwait(v, OUTS[i % 2], otoks[i - 2])
                            vi("tensor_mul",
                               gst[:, (i % 2) * D:(i % 2 + 1) * D],
                               gst[:, (i % 2) * D:(i % 2 + 1) * D],
                               sl10[:, (i % 2) * D:(i % 2 + 1) * D])
                            vi("tensor_add",
                               ost[:, (i % 2) * D:(i % 2 + 1) * D],
                               gst[:, (i % 2) * D:(i % 2 + 1) * D],
                               xl10[:, (i % 2) * D:(i % 2 + 1) * D])
                bk.vector(fv)

                def ft(t):
                    dwait(t, "TX", txk[-1])
                    for blk_ in range(NBLK):
                        if blk_ >= 1:
                            t.wait_ge(SS, ss10 + 4 * blk_)
                        for oc in range(NH):
                            first = True
                            for tt in range(4):
                                for hh in range(2):
                                    inst = mm(
                                        p_o[:, tt * 1024 + hh * 512:
                                            tt * 1024 + (hh + 1) * 512],
                                        hqT[:, oc * 512 + tt * 128:
                                            oc * 512 + (tt + 1) * 128],
                                        w2T[:, oc * D + hh * 512:
                                            oc * D + (hh + 1) * 512],
                                        start=(oc == 0), stop=False)
                                    if first:
                                        inst._wait_ge(VS,
                                                      q_done(blk_ * NH + oc))
                                        first = False
                        for tt in range(4):
                            inst = None
                            for hh in range(2):
                                inst = mm(
                                    p_o[:, tt * 1024 + hh * 512:
                                        tt * 1024 + (hh + 1) * 512],
                                    ones_b[:, :],
                                    r_b2r16[:, hh * 512:(hh + 1) * 512],
                                    start=False, stop=True)
                            inst.then_inc(MM0, 1)
                            cinc("MM0")
                bk.tensor(ft)

                def fs(s):
                    for blk_ in range(NBLK):
                        for tt in range(4):
                            i = blk_ * 4 + tt
                            s.wait_ge(MM0, mm0b + i + 1)
                            if i >= 2:
                                s.wait_ge(VS, comb_done(i - 2))
                            si("activation",
                               gst[:, (i % 2) * D:(i % 2 + 1) * D],
                               p_o[:, tt * 1024:(tt + 1) * 1024], AF.Silu,
                               scale=scl_b[:, 1:2])
                bk.scalar(fs)

    return nc


IDENT = np.eye(128, dtype=np.float32)


def kernel(L0_attn_x, L0_attn_norm_w, L0_attn_norm_b, L0_attn_qkv_w, L0_attn_qkv_b,
           L0_attn_proj_w, L0_attn_proj_b, L0_ffn_W1, L0_ffn_b1, L0_ffn_W2,
           L0_ffn_b2):
    x = np.ascontiguousarray(np.asarray(L0_attn_x, dtype=np.float32))
    common = {
        "norm_w": np.asarray(L0_attn_norm_w, np.float32).reshape(1, D),
        "norm_b": np.asarray(L0_attn_norm_b, np.float32).reshape(1, D),
        "qkv_w": np.ascontiguousarray(np.asarray(L0_attn_qkv_w, np.float32)),
        "qkv_b": np.asarray(L0_attn_qkv_b, np.float32).reshape(1, O3),
        "proj_w": np.ascontiguousarray(np.asarray(L0_attn_proj_w, np.float32)),
        "proj_b": np.asarray(L0_attn_proj_b, np.float32).reshape(1, D),
        "w1": np.ascontiguousarray(np.asarray(L0_ffn_W1, np.float32)),
        "b1": np.asarray(L0_ffn_b1, np.float32).reshape(1, H),
        "w2": np.ascontiguousarray(np.asarray(L0_ffn_W2, np.float32)),
        "b2": np.asarray(L0_ffn_b2, np.float32).reshape(1, D),
        "identf": IDENT,
    }
    in_maps = [dict(common, x=x[b]) for b in range(8)]
    nc = build_nc()
    res = run_bass_kernel_spmd(nc, in_maps, core_ids=list(range(8)))
    out = np.stack([np.asarray(res.results[b]["out"]) for b in range(8)], axis=0)
    return out.astype(np.float32)


if __name__ == "__main__":
    import reference as R
    ins = {k: np.asarray(v) for k, v in R.setup_inputs(0).items()}
    out = kernel(**ins)
    print(out.shape, out.dtype)


# version 04 (final)
# speedup vs baseline: 1.0457x; 1.0202x over previous
import os
import numpy as np
import ml_dtypes
from contextlib import ExitStack

PHLIM = float(os.environ.get("PHLIM", "99"))

import concourse.bass as bass
import concourse.mybir as mybir
from concourse.bass_utils import run_bass_kernel_spmd

F32 = mybir.dt.float32
BF16 = mybir.dt.bfloat16
I32 = mybir.dt.int32
AF = mybir.ActivationFunctionType
ALU = mybir.AluOpType
AX = mybir.AxisListType.X

T, D, O3, H = 4096, 1024, 3072, 4096
NT, ND, NH = 32, 8, 32
EPS = np.float32(1e-5)
INV1024 = np.float32(1.0 / 1024.0)
SCL = np.float32(0.015625)
NEGSCL = np.float32(-0.015625)
INV255 = np.float32(1.0 / 255.0)
INV127 = np.float32(1.0 / 127.0)
NEG1 = np.float32(-1.0)
HALFNEG = np.float32(-0.5)
ONEP5 = np.float32(1.5)


def build_nc():
    nc = bass.Bass("TRN2", target_bir_lowering=False)

    x_hbm = nc.dram_tensor("x", [T, D], F32, kind="ExternalInput").ap()
    nw_hbm = nc.dram_tensor("norm_w", [1, D], F32, kind="ExternalInput").ap()
    nb_hbm = nc.dram_tensor("norm_b", [1, D], F32, kind="ExternalInput").ap()
    wq_hbm = nc.dram_tensor("qkv_w", [O3, D], F32, kind="ExternalInput").ap()
    bq_hbm = nc.dram_tensor("qkv_b", [1, O3], F32, kind="ExternalInput").ap()
    pw_hbm = nc.dram_tensor("proj_w", [D, D], F32, kind="ExternalInput").ap()
    pb_hbm = nc.dram_tensor("proj_b", [1, D], F32, kind="ExternalInput").ap()
    w1_hbm = nc.dram_tensor("w1", [H, D], F32, kind="ExternalInput").ap()
    b1_hbm = nc.dram_tensor("b1", [1, H], F32, kind="ExternalInput").ap()
    w2_hbm = nc.dram_tensor("w2", [D, H], F32, kind="ExternalInput").ap()
    b2_hbm = nc.dram_tensor("b2", [1, D], F32, kind="ExternalInput").ap()
    idf_hbm = nc.dram_tensor("identf", [128, 128], F32, kind="ExternalInput").ap()
    out_hbm = nc.dram_tensor("out", [T, D], F32, kind="ExternalOutput").ap()

    g_xb = nc.dram_tensor("g_xb", [T, D], BF16).ap()
    g_xn = nc.dram_tensor("g_xn", [T, D], F32).ap()
    g_xqz = nc.dram_tensor("g_xqz", [T, D], BF16).ap()
    g_q = nc.dram_tensor("g_q", [T, D], BF16).ap()
    g_k = nc.dram_tensor("g_k", [T, D], BF16).ap()
    g_vT = nc.dram_tensor("g_vT", [D, T], BF16).ap()
    g_w = nc.dram_tensor("g_w", [D, D], BF16).ap()
    g_s = nc.dram_tensor("g_s", [T, D], F32).ap()
    g_gh = nc.dram_tensor("g_gh", [H, T], F32).ap()
    g_wq = nc.dram_tensor("g_wq", [O3, D], BF16).ap()
    g_pw = nc.dram_tensor("g_pw", [D, D], BF16).ap()
    g_w1 = nc.dram_tensor("g_w1", [H, D], BF16).ap()
    g_w2 = nc.dram_tensor("g_w2", [D, H], BF16).ap()
    g_bg = nc.dram_tensor("g_bg", [64, 128], BF16).ap()
    g_c2 = nc.dram_tensor("g_c2", [128, 24], F32).ap()
    cc1_in = nc.dram_tensor("cc1_in", [1, 2], F32).ap()
    cc1_out = nc.dram_tensor("cc1_out", [1, 2], F32).ap()
    cc2_in = nc.dram_tensor("cc2_in", [1, 2], F32).ap()
    cc2_out = nc.dram_tensor("cc2_out", [1, 2], F32).ap()

    C = {}

    def cinc(n, k=1):
        C[n] = C.get(n, 0) + k
        return C[n]

    def cget(n):
        return C.get(n, 0)

    with ExitStack() as top:
        sb = lambda n, s, d: top.enter_context(nc.sbuf_tensor(n, s, d))
        sem = lambda n: top.enter_context(nc.semaphore(n))

        c_idf = sb("c_idf", [128, 128], F32)
        ones_f = sb("ones_f", [1, 128], F32)
        ones_b = sb("ones_b", [1, 128], BF16)
        c_nwnb = sb("c_nwnb", [128, 16], BF16)
        c_b1c = sb("c_b1c", [128, NH], F32)
        c2bcol = sb("c2bcol", [128, 24], F32)
        c2b_bc = sb("c2b_bc", [128, 2048], F32)
        pb_bc = sb("pb_bc", [128, D], F32)
        r_b2 = sb("r_b2", [1, D], F32)
        r_b2r = sb("r_b2r", [1, D], F32)
        r_b2r16 = sb("r_b2r16", [1, D], BF16)
        bg16 = sb("bg16", [2, T], BF16)
        st_sum = sb("st_sum", [128, NT], F32)
        st_ssq = sb("st_ssq", [128, NT], F32)
        st_mnx = sb("st_mnx", [128, NT], F32)
        st_mxx = sb("st_mxx", [128, NT], F32)
        st_mu = sb("st_mu", [128, NT], F32)
        st_mu2 = sb("st_mu2", [128, NT], F32)
        st_r1 = sb("st_r1", [128, NT], F32)
        st_r2 = sb("st_r2", [128, NT], F32)
        st_al = sb("st_al", [128, NT], F32)
        st_bg = sb("st_bg", [128, 2 * NT], F32)
        st_t = sb("st_t", [128, 256], F32)
        st_rr = sb("st_rr", [128, ND], F32)
        nm_col = sb("nm_col", [128, ND], F32)
        gmn = sb("gmn", [128, 64], F32)
        gmx = sb("gmx", [128, 64], F32)
        scal = sb("scal", [1, 32], F32)
        scl_b = sb("scl_b", [128, 8], F32)
        cc_sb = sb("cc_sb", [1, 2], F32)
        crow = sb("crow", [2, O3], F32)
        crow16 = sb("crow16", [2, O3], BF16)
        r_qkvb = sb("r_qkvb", [2, O3], F32)
        bgst16 = sb("bgst16", [64, 128], BF16)

        VS = sem("VS")
        SS = sem("SS")
        MMC = sem("MMC")
        MMQ = sem("MMQ")
        MMV = sem("MMV")
        MMS = sem("MMS")
        MM5 = sem("MM5")
        MM8 = sem("MM8")
        MM9 = sem("MM9")
        MM0 = sem("MM0")
        FR3 = sem("FR3")
        FR3V = sem("FR3V")
        CCS = sem("CCS")

        _streams = {}

        def _stream(name):
            if name not in _streams:
                _streams[name] = [top.enter_context(nc.semaphore(name)), 0]
            return _streams[name]

        def dput(name, eng, out, in_, **k):
            s = _stream(name)
            eng.dma_start(out=out, in_=in_, **k).then_inc(s[0], 16)
            s[1] += 16
            return s[1]

        def dtot(name):
            return _stream(name)[1]

        def dwait(eng, name, tok):
            if tok:
                eng.wait_ge(_stream(name)[0], tok)

        from concourse.bass_types import SBTensorHandle, PSumTensorHandle

        def _ap(x):
            if isinstance(x, (SBTensorHandle, PSumTensorHandle)):
                return x[:, :]
            return x

        def vi(op, *a, **k):
            a = tuple(_ap(x) for x in a)
            inst = getattr(nc.vector, op)(*a, **k)
            prev = cget("VS")
            if prev:
                inst._wait_ge(VS, prev)
            inst.then_inc(VS, 1)
            cinc("VS")
            return inst

        def si(op, *a, **k):
            a = tuple(_ap(x) for x in a)
            inst = getattr(nc.scalar, op)(*a, **k)
            prev = cget("SS")
            if prev:
                inst._wait_ge(SS, prev)
            inst.then_inc(SS, 1)
            cinc("SS")
            return inst

        def ts(out, in0, s1, s2=None, op0=ALU.mult, op1=None):
            if isinstance(s1, np.floating):
                s1 = float(s1)
            if isinstance(s2, np.floating):
                s2 = float(s2)
            if op1 is None:
                return vi("tensor_scalar", out, in0, s1, None, op0)
            return vi("tensor_scalar", out, in0, s1, s2, op0, op1)

        def mm(*a, **k):
            return nc.tensor.matmul(*a, **k)

        LD = ["LA", "LB"]
        LD4 = ["LA", "LB", "LC", "LE"]
        SG4 = ["SA", "SB", "SC", "SE"]
        GH6 = ["GA", "GB", "GC", "GD", "GE", "GF"]
        WS = ["WA", "WB"]
        VSL = ["VA", "VB"]
        SG = ["SA", "SB"]
        S4 = ["S4A", "S4B", "S4C", "S4D"]
        S2 = ["S2A", "S2B"]
        GH = ["GA", "GB"]
        OUTS = ["OA", "OB"]

        if PHLIM < 0.5:
            with nc.Block("pnull") as bk:
                def fv(v):
                    v.memset(scal[:, :], 0.0)
                bk.vector(fv)
            return nc
        with ExitStack() as ps:
            p_tr = ps.enter_context(nc.psum_tensor("p_tr", [128, 64], F32))
            p_bc = ps.enter_context(nc.psum_tensor("p_bc", [128, 1024], F32))
            nwrow = ps.enter_context(nc.sbuf_tensor("nwrow", [1, D], F32))
            nbrow = ps.enter_context(nc.sbuf_tensor("nbrow", [1, D], F32))
            nb_bc = ps.enter_context(nc.sbuf_tensor("nb_bc", [128, D], F32))
            bqcol = ps.enter_context(nc.sbuf_tensor("bqcol", [128, 24], F32))
            c2brow = ps.enter_context(nc.sbuf_tensor("c2brow", [1, O3], F32))
            ctmp = ps.enter_context(nc.sbuf_tensor("ctmp", [128, D], F32))
            pbrow = ps.enter_context(nc.sbuf_tensor("pbrow", [1, D], F32))
            c_nwb = ps.enter_context(nc.sbuf_tensor("c_nwb", [128, D], F32))
            xld = ps.enter_context(nc.sbuf_tensor("xld", [128, 4 * D], F32))
            strash = ps.enter_context(nc.sbuf_tensor("strash", [128, D], F32))
            wq16 = ps.enter_context(nc.sbuf_tensor("wq16", [128, 4 * 2048], BF16))
            tbi = ps.enter_context(nc.sbuf_tensor("tbi", [128, 2048], I32))
            w2ld = ps.enter_context(nc.sbuf_tensor("w2ld", [128, 4 * 2048], F32))
            xnst = ps.enter_context(nc.sbuf_tensor("xnst", [128, 4 * D], F32))
            xbst = ps.enter_context(nc.sbuf_tensor("xbst", [128, 4 * D], BF16))

            with nc.Block("p0m") as bk:
                def fv(v):
                    v.memset(ones_f[:, :], 1.0)
                    v.memset(ones_b[:, :], 1.0)
                    v.memset(r_qkvb[0:1, :], 0.0)
                    v.memset(scal[:, :], 0.0)
                bk.vector(fv)

            if PHLIM < 0.7:
                return nc
            with nc.Block("p0") as bk:
                def fsy(sy):
                    dput("TX", sy, c_idf[:, :], idf_hbm)
                    dput("TX", sy, r_qkvb[1:2, :], bq_hbm)
                    dput("TX", sy, r_b2[:, :], b2_hbm)
                    dput("TX", sy, nwrow[:, :], nw_hbm)
                    dput("TX", sy, nbrow[:, :], nb_hbm)
                    dput("TX", sy, pbrow[:, :], pb_hbm)
                    dput("TX", sy, bqcol[:, :],
                         bq_hbm.rearrange("o (c p) -> (o p) c", p=128),
                         allow_slow_non_contiguous=True)
                    dput("TX", sy, strash[:, 0:8],
                         nw_hbm.rearrange("o (c p) -> (o p) c", p=128),
                         allow_slow_non_contiguous=True)
                    dput("TX", sy, strash[:, 8:16],
                         nb_hbm.rearrange("o (c p) -> (o p) c", p=128),
                         allow_slow_non_contiguous=True)
                    dput("TX", sy, c_b1c[:, :],
                         b1_hbm.rearrange("o (c p) -> (o p) c", p=128),
                         allow_slow_non_contiguous=True)
                bk.sync(fsy)
                tx_p0 = dtot("TX")

                def ft(t):
                    dwait(t, "TX", tx_p0)
                    mm(p_bc[:, 0:512], ones_f[:, :], nwrow[0:1, 0:512],
                       start=True, stop=True)
                    mm(p_bc[:, 512:1024], ones_f[:, :], nwrow[0:1, 512:1024],
                       start=True, stop=True).then_inc(MM0, 1)
                    cinc("MM0")
                bk.tensor(ft)
                mm0_a = cget("MM0")

            if PHLIM < 0.9:
                return nc
            with nc.Block("p0b") as bk:
                def fv(v):
                    v.wait_ge(MM0, mm0_a)
                    dwait(v, "TX", tx_p0)
                    il = c_nwnb[:, :].rearrange("p (a two) -> p two a", two=2)
                    vi("tensor_copy", il[:, 0, :], strash[:, 0:8])
                    vi("tensor_copy", il[:, 1, :], strash[:, 8:16])
                    vi("tensor_copy", c_nwb[:, :], p_bc[:, :])
                bk.vector(fv)

            with nc.Block("p0c") as bk:
                def ft(t):
                    mm(p_bc[:, 0:512], ones_f[:, :], nbrow[0:1, 0:512],
                       start=True, stop=True)
                    mm(p_bc[:, 512:1024], ones_f[:, :], nbrow[0:1, 512:1024],
                       start=True, stop=True).then_inc(MM0, 1)
                    cinc("MM0")
                bk.tensor(ft)
                mm0_c = cget("MM0")

                def fv(v):
                    v.wait_ge(MM0, mm0_c)
                    vi("tensor_copy", nb_bc[:, :], p_bc[:, :])
                bk.vector(fv)

            if PHLIM < 1:
                return nc
            with nc.Block("p1a") as bk:
                vs1 = cget("VS")
                ss1 = cget("SS")
                toks = []

                def fsy(sy):
                    for i in range(NT):
                        if i >= 4:
                            sy.wait_ge(VS, vs1 + 2 * (i - 3))
                            sy.wait_ge(SS, ss1 + 2 * (i - 3))
                        toks.append(dput(LD4[i % 4], sy,
                                         xld[:, (i % 4) * D:(i % 4 + 1) * D],
                                         x_hbm[i * 128:(i + 1) * 128, :]))
                bk.sync(fsy)

                def fv(v):
                    for i in range(NT):
                        dwait(v, LD4[i % 4], toks[i])
                        sl = xld[:, (i % 4) * D:(i % 4 + 1) * D]
                        vi("tensor_reduce", st_mnx[:, i:i + 1], sl, AX, ALU.min)
                        vi("reduce_max", st_mxx[:, i:i + 1], sl, AX)
                bk.vector(fv)

                def fs(s):
                    for i in range(NT):
                        dwait(s, LD4[i % 4], toks[i])
                        sl = xld[:, (i % 4) * D:(i % 4 + 1) * D]
                        si("activation", strash[:, :], sl, AF.Copy,
                           accum_out=st_sum[:, i:i + 1])
                        si("activation", strash[:, :], sl, AF.Square,
                           accum_out=st_ssq[:, i:i + 1])
                bk.scalar(fs)

            if PHLIM < 2:
                return nc
            with nc.Block("p2a") as bk:
                def fv(v):
                    ts(st_mu, st_sum, INV1024)
                    ts(st_t[:, 0:32], st_ssq, INV1024)
                    vi("tensor_mul", st_t[:, 32:64], st_mu, st_mu)
                    vi("tensor_sub", st_t[:, 0:32], st_t[:, 0:32], st_t[:, 32:64])
                    ts(st_t[:, 32:64], st_t[:, 0:32], EPS, op0=ALU.add)
                bk.vector(fv)

            with nc.Block("p2b") as bk:
                def fs(s):
                    si("activation", st_t[:, 64:96], st_t[:, 32:64], AF.Sqrt)
                bk.scalar(fs)

            with nc.Block("p2c") as bk:
                def fv(v):
                    vi("reciprocal", st_t[:, 96:128], st_t[:, 64:96])
                    vi("tensor_mul", st_t[:, 128:160], st_t[:, 96:128],
                       st_t[:, 96:128])
                    vi("tensor_mul", st_t[:, 128:160], st_t[:, 128:160],
                       st_t[:, 32:64])
                    ts(st_t[:, 128:160], st_t[:, 128:160], HALFNEG, ONEP5,
                       ALU.mult, ALU.add)
                    vi("tensor_mul", st_r1, st_t[:, 96:128], st_t[:, 128:160])
                    ts(st_t[:, 160:192], st_mu, np.float32(-1024.0))
                    vi("tensor_add", st_t[:, 160:192], st_t[:, 160:192], st_sum)
                    ts(st_t[:, 160:192], st_t[:, 160:192], INV1024)
                    vi("tensor_mul", st_mu2, st_t[:, 160:192], st_r1)
                    vi("tensor_mul", st_t[:, 160:192], st_r1, st_r1)
                    vi("tensor_mul", st_t[:, 160:192], st_t[:, 160:192],
                       st_t[:, 0:32])
                    vi("tensor_mul", st_t[:, 192:224], st_mu2, st_mu2)
                    vi("tensor_sub", st_t[:, 160:192], st_t[:, 160:192],
                       st_t[:, 192:224])
                    ts(st_t[:, 160:192], st_t[:, 160:192], EPS, op0=ALU.add)
                bk.vector(fv)

            with nc.Block("p2d") as bk:
                def fs(s):
                    si("activation", st_t[:, 192:224], st_t[:, 160:192], AF.Sqrt)
                bk.scalar(fs)

            with nc.Block("p2e") as bk:
                def fv(v):
                    vi("reciprocal", st_t[:, 224:256], st_t[:, 192:224])
                    vi("tensor_mul", st_t[:, 64:96], st_t[:, 224:256],
                       st_t[:, 224:256])
                    vi("tensor_mul", st_t[:, 64:96], st_t[:, 64:96],
                       st_t[:, 160:192])
                    ts(st_t[:, 64:96], st_t[:, 64:96], HALFNEG, ONEP5,
                       ALU.mult, ALU.add)
                    vi("tensor_mul", st_r2, st_t[:, 224:256], st_t[:, 64:96])
                    vi("tensor_mul", st_al, st_r1, st_r2)
                    vi("reciprocal", st_t[:, 96:128], st_al)
                    vi("tensor_mul", st_t[:, 128:160], st_al, st_mu)
                    vi("tensor_mul", st_t[:, 160:192], st_r2, st_mu2)
                    vi("tensor_add", st_t[:, 128:160], st_t[:, 128:160],
                       st_t[:, 160:192])
                    ts(st_t[:, 128:160], st_t[:, 128:160], NEG1)
                    vi("tensor_mul", st_t[:, 128:160], st_t[:, 128:160],
                       st_t[:, 96:128])
                    il = st_bg[:, :].rearrange("p (a two) -> p two a", two=2)
                    vi("tensor_copy", il[:, 0, :], st_t[:, 128:160])
                    vi("tensor_copy", il[:, 1, :], st_t[:, 96:128])
                    vi("tensor_sub", st_t[:, 160:192], st_mnx[:, :], st_mu[:, :])
                    vi("tensor_mul", st_t[:, 160:192], st_t[:, 160:192],
                       st_r1[:, :])
                    vi("tensor_sub", st_t[:, 192:224], st_mxx[:, :], st_mu[:, :])
                    vi("tensor_mul", st_t[:, 192:224], st_t[:, 192:224],
                       st_r1[:, :])
                    vi("tensor_reduce", st_t[:, 224:225], st_t[:, 160:192], AX,
                       ALU.min)
                    vi("reduce_max", st_t[:, 225:226], st_t[:, 192:224], AX)
                    ts(st_t[:, 224:225], st_t[:, 224:225], NEG1)
                bk.vector(fv)
            vs_p2e = cget("VS")

            with nc.Block("p2f") as bk:
                def ft(t):
                    t.wait_ge(VS, vs_p2e)
                    t.transpose(p_bc[0:64, 0:128], st_bg[:, :], c_idf[:, :])
                    t.transpose(p_bc[0:1, 128:256], st_t[:, 225:226], c_idf[:, :])
                    t.transpose(p_bc[0:1, 256:384], st_t[:, 224:225],
                                c_idf[:, :]).then_inc(MM0, 1)
                    cinc("MM0")
                bk.tensor(ft)
                mm0_f = cget("MM0")

                def fv(v):
                    v.wait_ge(MM0, mm0_f)
                    vi("tensor_copy", bgst16[:, :], p_bc[0:64, 0:128])
                    vi("reduce_max", cc_sb[0:1, 0:1], p_bc[0:1, 128:256], AX)
                    vi("reduce_max", cc_sb[0:1, 1:2], p_bc[0:1, 256:384], AX)
                bk.vector(fv)
            vs_p2f = cget("VS")

            with nc.Block("p2g") as bk:
                def fg(g):
                    g.wait_ge(VS, vs_p2f)
                    dput("CC", g, g_bg, bgst16[:, :])
                    dput("CC", g, cc1_in, cc_sb[0:1, 0:2])
                    dwait(g, "CC", dtot("CC"))
                    g.collective_compute(
                        "AllReduce", ALU.max, replica_groups=[list(range(8))],
                        ins=[cc1_in], outs=[cc1_out]).then_inc(CCS, 1)
                    cinc("CCS")
                bk.gpsimd(fg)
                cc_bg = dtot("CC")

                def fsy(sy):
                    dwait(sy, "CC", cc_bg)
                    bgr = g_bg.rearrange("(a two) j -> two a j", two=2)
                    dput("BG", sy, bg16[:, :], bgr)
                bk.sync(fsy)
            bg_tok = dtot("BG")

            if PHLIM < 3:
                return nc
            def cast_pipe(bk, src, dst, ntiles, c2=False):
                vsb = cget("VS")
                npt = 3 if c2 else 1
                ltoks, stoks = [], []
                sg_base = [dtot(s_) for s_ in SG4]

                def fsy(sy):
                    for i in range(ntiles):
                        if i >= 4:
                            sy.wait_ge(VS, vsb + npt * (i - 3))
                        ltoks.append(dput(LD4[i % 4], sy,
                                          xld[:, (i % 4) * D:(i % 4 + 1) * D],
                                          src[i * 128:(i + 1) * 128, :]))
                bk.sync(fsy)

                def fg(g):
                    for i in range(ntiles):
                        g.wait_ge(VS, vsb + npt * i + 1)
                        stoks.append(dput(SG4[i % 4], g,
                                          dst[i * 128:(i + 1) * 128, :],
                                          wq16[:, (i % 4) * 2048:
                                               (i % 4) * 2048 + 1024]))
                bk.gpsimd(fg)

                def fv(v):
                    for i in range(ntiles):
                        dwait(v, LD4[i % 4], ltoks[i])
                        if i >= 4:
                            dwait(v, SG4[i % 4], stoks[i - 4])
                        else:
                            dwait(v, SG4[i % 4], sg_base[i % 4])
                        vi("tensor_copy",
                           wq16[:, (i % 4) * 2048:(i % 4) * 2048 + 1024],
                           xld[:, (i % 4) * D:(i % 4 + 1) * D])
                        if c2:
                            vi("tensor_mul", ctmp[:, :],
                               xld[:, (i % 4) * D:(i % 4 + 1) * D], nb_bc[:, :])
                            vi("reduce_sum", c2bcol[:, i:i + 1], ctmp[:, :], AX)
                    if c2:
                        vi("tensor_add", c2bcol[:, :], c2bcol[:, :], bqcol[:, :])
                bk.vector(fv)

            with nc.Block("p2wq") as bk:
                cast_pipe(bk, wq_hbm, g_wq, O3 // 128, c2=True)
            with nc.Block("p2wp") as bk:
                cast_pipe(bk, pw_hbm, g_pw, D // 128)

            with nc.Block("p2wm") as bk:
                vsb = cget("VS")
                ltoks = []

                def fsy(sy):
                    for i in range(NH):
                        if i >= 4:
                            sy.wait_ge(VS, vsb + (i - 3))
                        ltoks.append(dput(LD4[i % 4], sy,
                                          xld[:, (i % 4) * D:(i % 4 + 1) * D],
                                          w1_hbm[i * 128:(i + 1) * 128, :]))
                    for i in range(16):
                        if i >= 4:
                            sy.wait_ge(VS, vsb + NH + (i - 3))
                        else:
                            sy.wait_ge(VS, vsb + NH)
                        ltoks.append(dput(
                            LD4[i % 4], sy,
                            w2ld[:, (i % 4) * 2048:(i % 4 + 1) * 2048],
                            w2_hbm[(i // 2) * 128:(i // 2 + 1) * 128,
                                   (i % 2) * 2048:(i % 2 + 1) * 2048]))
                bk.sync(fsy)

                def fv(v):
                    for i in range(NH):
                        dwait(v, LD4[i % 4], ltoks[i])
                        vi("reduce_max", st_t[:, 64 + i:65 + i],
                           xld[:, (i % 4) * D:(i % 4 + 1) * D],
                           AX, apply_absolute_value=True)
                    for i in range(16):
                        dwait(v, LD4[i % 4], ltoks[NH + i])
                        vi("reduce_max", st_t[:, 96 + i:97 + i],
                           w2ld[:, (i % 4) * 2048:(i % 4 + 1) * 2048],
                           AX, apply_absolute_value=True)
                    vi("reduce_max", st_t[:, 112:113], st_t[:, 64:96], AX)
                    vi("reduce_max", st_t[:, 113:114], st_t[:, 96:112], AX)
                bk.vector(fv)
            vs_wm = cget("VS")

            with nc.Block("p2wn") as bk:
                def ft(t):
                    t.wait_ge(VS, vs_wm)
                    t.transpose(p_bc[0:1, 0:128], st_t[:, 112:113], c_idf[:, :])
                    t.transpose(p_bc[0:1, 128:256], st_t[:, 113:114],
                                c_idf[:, :]).then_inc(MM0, 1)
                    cinc("MM0")
                bk.tensor(ft)
                mm0_w = cget("MM0")

                def fv(v):
                    v.wait_ge(MM0, mm0_w)
                    vi("reduce_max", scal[0:1, 4:5], p_bc[0:1, 0:128], AX)
                    vi("reduce_max", scal[0:1, 5:6], p_bc[0:1, 128:256], AX)
                    ts(scal[0:1, 0:1], scal[0:1, 4:5], INV127)
                    ts(scal[0:1, 0:1], scal[0:1, 0:1], np.float32(1e-8),
                       op0=ALU.max)
                    vi("reciprocal", scal[0:1, 1:2], scal[0:1, 0:1])
                    ts(scal[0:1, 2:3], scal[0:1, 5:6], INV127)
                    ts(scal[0:1, 2:3], scal[0:1, 2:3], np.float32(1e-8),
                       op0=ALU.max)
                    vi("reciprocal", scal[0:1, 3:4], scal[0:1, 2:3])
                bk.vector(fv)
            vs_wn = cget("VS")

            with nc.Block("p2wo") as bk:
                def ft(t):
                    t.wait_ge(VS, vs_wn)
                    mm(p_bc[:, 0:1], ones_f[:, :], scal[0:1, 1:2],
                       start=True, stop=True)
                    mm(p_bc[:, 2:3], ones_f[:, :], scal[0:1, 3:4],
                       start=True, stop=True).then_inc(MM0, 1)
                    cinc("MM0")
                bk.tensor(ft)
                mm0_o = cget("MM0")

                def fv(v):
                    v.wait_ge(MM0, mm0_o)
                    vi("tensor_copy", scl_b[:, 0:1], p_bc[:, 0:1])
                    vi("tensor_copy", scl_b[:, 1:2], p_bc[:, 2:3])
                bk.vector(fv)

            def quant_pipe(bk, src, dst, ntiles, wide, rcol):
                vsb = cget("VS")
                ltoks, stoks = [], []
                sg_base = [dtot(s_) for s_ in SG4]

                def fsy(sy):
                    for i in range(ntiles):
                        if i >= 4:
                            sy.wait_ge(VS, vsb + 2 * (i - 3))
                        if not wide:
                            ltoks.append(dput(
                                LD4[i % 4], sy,
                                xld[:, (i % 4) * D:(i % 4 + 1) * D],
                                src[i * 128:(i + 1) * 128, :]))
                        else:
                            ltoks.append(dput(
                                LD4[i % 4], sy,
                                w2ld[:, (i % 4) * 2048:(i % 4 + 1) * 2048],
                                src[(i // 2) * 128:(i // 2 + 1) * 128,
                                    (i % 2) * 2048:(i % 2 + 1) * 2048]))
                bk.sync(fsy)

                def fg(g):
                    for i in range(ntiles):
                        g.wait_ge(VS, vsb + 2 * (i + 1))
                        if not wide:
                            stoks.append(dput(SG4[i % 4], g,
                                              dst[i * 128:(i + 1) * 128, :],
                                              wq16[:, (i % 4) * 2048:
                                                   (i % 4) * 2048 + 1024]))
                        else:
                            stoks.append(dput(
                                SG4[i % 4], g,
                                dst[(i // 2) * 128:(i // 2 + 1) * 128,
                                    (i % 2) * 2048:(i % 2 + 1) * 2048],
                                wq16[:, (i % 4) * 2048:(i % 4 + 1) * 2048]))
                bk.gpsimd(fg)

                def fv(v):
                    for i in range(ntiles):
                        dwait(v, LD4[i % 4], ltoks[i])
                        if i >= 4:
                            dwait(v, SG4[i % 4], stoks[i - 4])
                        else:
                            dwait(v, SG4[i % 4], sg_base[i % 4])
                        if not wide:
                            srcsl = xld[:, (i % 4) * D:(i % 4 + 1) * D]
                            isl = tbi[:, 0:1024]
                            osl = wq16[:, (i % 4) * 2048:(i % 4) * 2048 + 1024]
                        else:
                            srcsl = w2ld[:, (i % 4) * 2048:(i % 4 + 1) * 2048]
                            isl = tbi[:, 0:2048]
                            osl = wq16[:, (i % 4) * 2048:(i % 4 + 1) * 2048]
                        ts(isl, srcsl, rcol, np.float32(-127.0), ALU.mult,
                           ALU.max)
                        ts(osl, isl, np.float32(127.0), op0=ALU.min)
                bk.vector(fv)

            with nc.Block("p2q1") as bk:
                quant_pipe(bk, w1_hbm, g_w1, NH, False, scl_b[:, 0:1])
            with nc.Block("p2q2") as bk:
                quant_pipe(bk, w2_hbm, g_w2, 16, True, scl_b[:, 1:2])

            if PHLIM < 4:
                return nc
            with nc.Block("p2x") as bk:
                def fg(g):
                    dput("CC", g, g_c2, c2bcol[:, :])
                bk.gpsimd(fg)
                cc_c2 = dtot("CC")

                def fsy(sy):
                    dwait(sy, "CC", cc_c2)
                    dput("BG", sy, c2brow[:, :], g_c2.rearrange("p c -> c p"),
                         allow_slow_non_contiguous=True)
                bk.sync(fsy)
            bg_c2 = dtot("BG")

            with nc.Block("p2y1") as bk:
                def ft(t):
                    dwait(t, "BG", bg_c2)
                    mm(p_bc[:, 0:512], ones_f[:, :], c2brow[0:1, 0:512],
                       start=True, stop=True)
                    mm(p_bc[:, 512:1024], ones_f[:, :], c2brow[0:1, 512:1024],
                       start=True, stop=True).then_inc(MM0, 1)
                    cinc("MM0")
                bk.tensor(ft)
                mm0_y1 = cget("MM0")

                def fv(v):
                    v.wait_ge(MM0, mm0_y1)
                    vi("tensor_copy", c2b_bc[:, 0:1024], p_bc[:, :])
                bk.vector(fv)

            with nc.Block("p2y2") as bk:
                def ft(t):
                    mm(p_bc[:, 0:512], ones_f[:, :], c2brow[0:1, 1024:1536],
                       start=True, stop=True)
                    mm(p_bc[:, 512:1024], ones_f[:, :], c2brow[0:1, 1536:2048],
                       start=True, stop=True).then_inc(MM0, 1)
                    cinc("MM0")
                bk.tensor(ft)
                mm0_y2 = cget("MM0")

                def fv(v):
                    v.wait_ge(MM0, mm0_y2)
                    vi("tensor_copy", c2b_bc[:, 1024:2048], p_bc[:, :])
                bk.vector(fv)

            with nc.Block("p2y3") as bk:
                def ft(t):
                    mm(p_bc[:, 0:512], ones_f[:, :], pbrow[0:1, 0:512],
                       start=True, stop=True)
                    mm(p_bc[:, 512:1024], ones_f[:, :], pbrow[0:1, 512:1024],
                       start=True, stop=True).then_inc(MM0, 1)
                    cinc("MM0")
                bk.tensor(ft)
                mm0_y3 = cget("MM0")

                def fv(v):
                    v.wait_ge(MM0, mm0_y3)
                    vi("tensor_copy", pb_bc[:, :], p_bc[:, :])
                bk.vector(fv)

            with nc.Block("p1b") as bk:
                vsb = cget("VS")
                ltoks, stoks = [], []
                sg_base = [dtot(s_) for s_ in SG4]

                def fsy(sy):
                    for i in range(NT):
                        if i >= 4:
                            sy.wait_ge(VS, vsb + 2 * (i - 3))
                        ltoks.append(dput(LD4[i % 4], sy,
                                          xld[:, (i % 4) * D:(i % 4 + 1) * D],
                                          x_hbm[i * 128:(i + 1) * 128, :]))
                bk.sync(fsy)

                def fg(g):
                    for i in range(NT):
                        g.wait_ge(VS, vsb + 2 * (i + 1))
                        dput(SG4[i % 4], g, g_xn[i * 128:(i + 1) * 128, :],
                             xnst[:, (i % 4) * D:(i % 4 + 1) * D])
                        stoks.append(dput(SG4[i % 4], g,
                                          g_xb[i * 128:(i + 1) * 128, :],
                                          xbst[:, (i % 4) * D:(i % 4 + 1) * D]))
                bk.gpsimd(fg)

                def fv(v):
                    for i in range(NT):
                        dwait(v, LD4[i % 4], ltoks[i])
                        if i >= 4:
                            dwait(v, SG4[i % 4], stoks[i - 4])
                        else:
                            dwait(v, SG4[i % 4], sg_base[i % 4])
                        sl = xld[:, (i % 4) * D:(i % 4 + 1) * D]
                        ts(xnst[:, (i % 4) * D:(i % 4 + 1) * D], sl,
                           st_mu[:, i:i + 1], st_r1[:, i:i + 1],
                           ALU.subtract, ALU.mult)
                        vi("tensor_mul", xbst[:, (i % 4) * D:(i % 4 + 1) * D],
                           sl, c_nwb[:, :])
                bk.vector(fv)
            sg_p1b = [dtot(s_) for s_ in SG4]

        if PHLIM < 5:
            return nc
        with ExitStack() as ps:
            xbT = ps.enter_context(nc.sbuf_tensor("xbT", [128, ND * T], BF16))
            wbuf = ps.enter_context(nc.sbuf_tensor("wbuf", [128, 2 * 4096], BF16))
            vwb = ps.enter_context(nc.sbuf_tensor("vwb", [128, 2 * 1024], BF16))
            qkst = ps.enter_context(nc.sbuf_tensor("qkst", [128, 4 * 512], BF16))
            vst = ps.enter_context(nc.sbuf_tensor("vst", [128, 2 * 512], BF16))
            p_qk = ps.enter_context(nc.psum_tensor("p_qk", [128, 2 * 512], F32))
            p_v = ps.enter_context(nc.psum_tensor("p_v", [128, 2 * 512], F32))
            p_c = ps.enter_context(nc.psum_tensor("p_c", [2, 512], F32))

            with nc.Block("p3") as bk:
                vs3, ss3 = cget("VS"), cget("SS")
                bg_all = dtot("BG")
                txk, wtoks, vtoks = [], [], []
                qkstoks, vstoks = [], []

                def qpos(g):
                    return vs3 + 34 * (g // NT) + 2 + (g % NT) + 1

                def fsy(sy):
                    for ix in range(4):
                        dwait(sy, SG4[ix], sg_p1b[ix])
                    for ic in range(ND):
                        txk.append(dput("TX", sy, xbT[:, ic * T:(ic + 1) * T],
                                        g_xb[0:T, ic * 128:(ic + 1) * 128],
                                        transpose=True))
                    for oc in range(4):
                        if oc >= 2:
                            sy.wait_ge(FR3, oc - 1)
                        for ic in range(ND):
                            wtoks.append(dput(
                                WS[oc % 2], sy,
                                wbuf[:, (oc % 2) * 4096 + ic * 512:
                                     (oc % 2) * 4096 + (ic + 1) * 512],
                                g_wq[oc * 512:(oc + 1) * 512,
                                     ic * 128:(ic + 1) * 128], transpose=True))
                    for ot in range(ND):
                        if ot >= 2:
                            sy.wait_ge(FR3V, ot - 1)
                        for ic in range(ND):
                            vtoks.append(dput(
                                VSL[ot % 2], sy,
                                vwb[:, (ot % 2) * 1024 + ic * 128:
                                    (ot % 2) * 1024 + (ic + 1) * 128],
                                g_wq[2048 + ot * 128:2048 + (ot + 1) * 128,
                                     ic * 128:(ic + 1) * 128], transpose=True))
                bk.sync(fsy)

                def fg(g_):
                    for g in range(128):
                        oc, i = divmod(g, NT)
                        g_.wait_ge(VS, qpos(g))
                        dstt = g_q if oc < 2 else g_k
                        col = (oc % 2) * 512
                        qkstoks.append(dput(
                            S4[g % 4], g_,
                            dstt[i * 128:(i + 1) * 128, col:col + 512],
                            qkst[:, (g % 4) * 512:(g % 4 + 1) * 512]))
                    for g in range(64):
                        ot, tc = divmod(g, ND)
                        g_.wait_ge(VS, vs3 + 136 + 10 * ot + 2 + tc + 1)
                        vstoks.append(dput(
                            S2[g % 2], g_,
                            g_vT[ot * 128:(ot + 1) * 128,
                                 tc * 512:(tc + 1) * 512],
                            vst[:, (g % 2) * 512:(g % 2 + 1) * 512]))
                bk.gpsimd(fg)

                def ft(t):
                    dwait(t, "TX", txk[-1])
                    dwait(t, "BG", bg_all)
                    for oc in range(4):
                        dwait(t, WS[oc % 2], wtoks[oc * ND + ND - 1])
                        if oc >= 1:
                            t.wait_ge(VS, vs3 + 34 * (oc - 1) + 2)
                        inst = None
                        for ic in range(ND):
                            inst = mm(p_c[:, :], c_nwnb[:, 2 * ic:2 * ic + 2],
                                      wbuf[:, (oc % 2) * 4096 + ic * 512:
                                           (oc % 2) * 4096 + (ic + 1) * 512],
                                      start=(ic == 0), stop=(ic == ND - 1))
                        inst.then_inc(MMC, 1)
                        cinc("MMC")
                        for i in range(NT):
                            g = oc * NT + i
                            if g >= 2:
                                t.wait_ge(VS, qpos(g - 2))
                            sl = p_qk[:, (g % 2) * 512:(g % 2 + 1) * 512]
                            for ic in range(ND):
                                wmm = mm(
                                    sl,
                                    xbT[:, ic * T + i * 128:ic * T + (i + 1) * 128],
                                    wbuf[:, (oc % 2) * 4096 + ic * 512:
                                         (oc % 2) * 4096 + (ic + 1) * 512],
                                    start=(ic == 0), stop=False)
                            if i == NT - 1:
                                wmm.then_inc(FR3, 1)
                                cinc("FR3")
                            inst = mm(sl, bg16[0:1, i * 128:(i + 1) * 128],
                                      crow16[0:1, oc * 512:(oc + 1) * 512],
                                      start=False, stop=True)
                            inst._wait_ge(VS, vs3 + 34 * oc + 2)
                            inst.then_inc(MMQ, 1)
                            cinc("MMQ")
                    for ot in range(ND):
                        dwait(t, VSL[ot % 2], vtoks[ot * ND + ND - 1])
                        t.wait_ge(VS, (vs3 + 34 * 3 + 2) if ot == 0
                                  else (vs3 + 136 + 10 * ot))
                        inst = None
                        for ic in range(ND):
                            inst = mm(p_c[:, 0:128], c_nwnb[:, 2 * ic:2 * ic + 2],
                                      vwb[:, (ot % 2) * 1024 + ic * 128:
                                          (ot % 2) * 1024 + (ic + 1) * 128],
                                      start=(ic == 0), stop=(ic == ND - 1))
                        inst.then_inc(MMC, 1)
                        cinc("MMC")
                        for tc in range(ND):
                            g = ot * ND + tc
                            if g >= 2:
                                otp, tcp = divmod(g - 2, ND)
                                t.wait_ge(VS, vs3 + 136 + 10 * otp + 2 + tcp + 1)
                            sl = p_v[:, (g % 2) * 512:(g % 2 + 1) * 512]
                            for ic in range(ND):
                                wmm = mm(
                                    sl,
                                    vwb[:, (ot % 2) * 1024 + ic * 128:
                                        (ot % 2) * 1024 + (ic + 1) * 128],
                                    xbT[:, ic * T + tc * 512:
                                        ic * T + (tc + 1) * 512],
                                    start=(ic == 0), stop=False)
                            if tc == ND - 1:
                                wmm.then_inc(FR3V, 1)
                                cinc("FR3V")
                            inst = mm(sl,
                                      crow16[:, 2048 + ot * 128:
                                             2048 + (ot + 1) * 128],
                                      bg16[:, tc * 512:(tc + 1) * 512],
                                      start=False, stop=True)
                            inst._wait_ge(VS, vs3 + 136 + 10 * ot + 2)
                            inst.then_inc(MMV, 1)
                            cinc("MMV")
                bk.tensor(ft)

                def fv(v):
                    for oc in range(4):
                        v.wait_ge(MMC, oc + 1)
                        vi("tensor_add", crow[:, oc * 512:(oc + 1) * 512],
                           p_c[:, :], r_qkvb[:, oc * 512:(oc + 1) * 512])
                        vi("tensor_copy", crow16[:, oc * 512:(oc + 1) * 512],
                           crow[:, oc * 512:(oc + 1) * 512])
                        for i in range(NT):
                            g = oc * NT + i
                            v.wait_ge(MMQ, g + 1)
                            if g >= 4:
                                dwait(v, S4[g % 4], qkstoks[g - 4])
                            vi("scalar_tensor_tensor",
                               qkst[:, (g % 4) * 512:(g % 4 + 1) * 512],
                               p_qk[:, (g % 2) * 512:(g % 2 + 1) * 512],
                               st_al[:, i:i + 1],
                               c2b_bc[:, oc * 512:(oc + 1) * 512],
                               ALU.mult, ALU.add)
                    for ot in range(ND):
                        v.wait_ge(MMC, 4 + ot + 1)
                        vi("tensor_add", crow[:, 2048 + ot * 128:
                                              2048 + (ot + 1) * 128],
                           p_c[:, 0:128], r_qkvb[:, 2048 + ot * 128:
                                                 2048 + (ot + 1) * 128])
                        vi("tensor_copy", crow16[:, 2048 + ot * 128:
                                                 2048 + (ot + 1) * 128],
                           crow[:, 2048 + ot * 128:2048 + (ot + 1) * 128])
                        for tc in range(ND):
                            g = ot * ND + tc
                            v.wait_ge(MMV, g + 1)
                            if g >= 2:
                                dwait(v, S2[g % 2], vstoks[g - 2])
                            vi("tensor_copy",
                               vst[:, (g % 2) * 512:(g % 2 + 1) * 512],
                               p_v[:, (g % 2) * 512:(g % 2 + 1) * 512])
                bk.vector(fv)
            s4_p3 = [dtot(x) for x in S4]
            s2_p3 = [dtot(x) for x in S2]

        if PHLIM < 6:
            return nc
        with ExitStack() as ps:
            xnld = ps.enter_context(nc.sbuf_tensor("xnld", [128, 4 * D], F32))
            xqst = ps.enter_context(nc.sbuf_tensor("xqst", [128, 4 * D], BF16))
            tbi4 = ps.enter_context(nc.sbuf_tensor("tbi4", [128, 1024], I32))
            tb1 = ps.enter_context(nc.sbuf_tensor("tb1", [1, 1], I32))
            p_b4 = ps.enter_context(nc.psum_tensor("p_b4", [128, 8], F32))

            with nc.Block("p4a") as bk:
                def fg(g):
                    g.wait_ge(CCS, 1)
                    dput("CC", g, cc_sb[0:1, 0:2], cc1_out)
                bk.gpsimd(fg)
                cc_r1 = dtot("CC")

                def fv(v):
                    dwait(v, "CC", cc_r1)
                    vi("tensor_add", scal[0:1, 6:7], cc_sb[0:1, 0:1],
                       cc_sb[0:1, 1:2])
                    ts(scal[0:1, 7:8], scal[0:1, 6:7], INV255)
                    ts(scal[0:1, 7:8], scal[0:1, 7:8], np.float32(1e-8),
                       op0=ALU.max)
                    vi("reciprocal", scal[0:1, 8:9], scal[0:1, 7:8])
                    vi("tensor_mul", scal[0:1, 9:10], cc_sb[0:1, 1:2],
                       scal[0:1, 8:9])
                    vi("tensor_copy", tb1[0:1, 0:1], scal[0:1, 9:10])
                    vi("tensor_copy", scal[0:1, 10:11], tb1[0:1, 0:1])
                    ts(scal[0:1, 10:11], scal[0:1, 10:11], np.float32(-128.0),
                       op0=ALU.add)
                    ts(scal[0:1, 10:11], scal[0:1, 10:11], np.float32(-128.0),
                       op0=ALU.max)
                    ts(scal[0:1, 10:11], scal[0:1, 10:11], np.float32(127.0),
                       op0=ALU.min)
                    ts(scal[0:1, 11:12], scal[0:1, 10:11], NEG1,
                       np.float32(-128.0), ALU.mult, ALU.add)
                    ts(scal[0:1, 12:13], scal[0:1, 10:11], NEG1,
                       np.float32(127.0), ALU.mult, ALU.add)
                    vi("tensor_mul", scal[0:1, 13:14], scal[0:1, 7:8],
                       scal[0:1, 0:1])
                bk.vector(fv)
            vs_4a = cget("VS")

            with nc.Block("p4b") as bk:
                def ft(t):
                    t.wait_ge(VS, vs_4a)
                    mm(p_b4[:, 0:4], ones_f[:, :], scal[0:1, 8:12],
                       start=True, stop=True)
                    mm(p_b4[:, 4:8], ones_f[:, :], scal[0:1, 10:14],
                       start=True, stop=True).then_inc(MM0, 1)
                    cinc("MM0")
                bk.tensor(ft)
                mm0_4 = cget("MM0")

                def fv(v):
                    v.wait_ge(MM0, mm0_4)
                    vi("tensor_copy", scl_b[:, 2:3], p_b4[:, 0:1])
                    vi("tensor_copy", scl_b[:, 3:4], p_b4[:, 5:6])
                    vi("tensor_copy", scl_b[:, 4:5], p_b4[:, 6:7])
                    vi("tensor_copy", scl_b[:, 5:6], p_b4[:, 7:8])
                bk.vector(fv)

            with nc.Block("p4c") as bk:
                vsb = cget("VS")
                ltoks, stoks = [], []
                sg_base = [dtot(s_) for s_ in SG4]
                s4b = [dtot(x) for x in S4]
                s2b = [dtot(x) for x in S2]

                def fsy(sy):
                    for ix in range(4):
                        dwait(sy, S4[ix], s4b[ix])
                    dwait(sy, S2[0], s2b[0])
                    dwait(sy, S2[1], s2b[1])
                    for i in range(NT):
                        if i >= 4:
                            sy.wait_ge(VS, vsb + 2 * (i - 3))
                        ltoks.append(dput(LD4[i % 4], sy,
                                          xnld[:, (i % 4) * D:(i % 4 + 1) * D],
                                          g_xn[i * 128:(i + 1) * 128, :]))
                bk.sync(fsy)

                def fg(g):
                    for i in range(NT):
                        g.wait_ge(VS, vsb + 2 * (i + 1))
                        stoks.append(dput(SG4[i % 4], g,
                                          g_xqz[i * 128:(i + 1) * 128, :],
                                          xqst[:, (i % 4) * D:(i % 4 + 1) * D]))
                bk.gpsimd(fg)

                def fv(v):
                    for i in range(NT):
                        dwait(v, LD4[i % 4], ltoks[i])
                        if i >= 4:
                            dwait(v, SG4[i % 4], stoks[i - 4])
                        else:
                            dwait(v, SG4[i % 4], sg_base[i % 4])
                        ts(tbi4[:, :], xnld[:, (i % 4) * D:(i % 4 + 1) * D],
                           scl_b[:, 2:3], scl_b[:, 3:4], ALU.mult, ALU.max)
                        ts(xqst[:, (i % 4) * D:(i % 4 + 1) * D], tbi4[:, :],
                           scl_b[:, 4:5], op0=ALU.min)
                bk.vector(fv)
            sg_p4c = [dtot(s_) for s_ in SG4]

        if PHLIM < 7:
            return nc
        with ExitStack() as ps:
            xqzT = ps.enter_context(nc.sbuf_tensor("xqzT", [128, ND * T], BF16))
            w1h = ps.enter_context(nc.sbuf_tensor("w1h", [128, 2 * 8192], BF16))
            ghst = ps.enter_context(nc.sbuf_tensor("ghst", [128, 2 * 2048], F32))
            p_f = ps.enter_context(nc.psum_tensor("p_f", [128, 8 * 512], F32))

            with nc.Block("p5") as bk:
                vs5, ss5 = cget("VS"), cget("SS")
                txk, w1t, stoks = [], [], []
                sg_base = [dtot(SG[0]), dtot(SG[1])]

                def fsy(sy):
                    for ix in range(4):
                        dwait(sy, SG4[ix], sg_p4c[ix])
                    for ix in range(4):
                        dwait(sy, S4[ix], s4_p3[ix])
                    dwait(sy, S2[0], s2_p3[0])
                    dwait(sy, S2[1], s2_p3[1])
                    for ic in range(ND):
                        txk.append(dput("TX", sy, xqzT[:, ic * T:(ic + 1) * T],
                                        g_xqz[0:T, ic * 128:(ic + 1) * 128],
                                        transpose=True))
                    for qq in range(4):
                        if qq >= 2:
                            sy.wait_ge(MM5, 16 * (qq - 1))
                        for ic in range(ND):
                            w1t.append(dput(
                                WS[qq % 2], sy,
                                w1h[:, (qq % 2) * 8192 + ic * 1024:
                                    (qq % 2) * 8192 + (ic + 1) * 1024],
                                g_w1[qq * 1024:(qq + 1) * 1024,
                                     ic * 128:(ic + 1) * 128], transpose=True))
                bk.sync(fsy)

                def fg(g):
                    for n in range(64):
                        ot, gg = divmod(n, 2)
                        g.wait_ge(SS, ss5 + 4 * (n + 1))
                        stoks.append(dput(SG[n % 2], g,
                                          g_gh[ot * 128:(ot + 1) * 128,
                                               gg * 2048:(gg + 1) * 2048],
                                          ghst[:, (n % 2) * 2048:
                                               (n % 2 + 1) * 2048]))
                bk.gpsimd(fg)

                def ft(t):
                    dwait(t, "TX", txk[-1])
                    for ot in range(NH):
                        qq = ot // 8
                        if ot % 8 == 0:
                            dwait(t, WS[qq % 2], w1t[qq * ND + ND - 1])
                        base = (qq % 2) * 8192
                        for gg in range(2):
                            n = 2 * ot + gg
                            if n >= 2:
                                t.wait_ge(SS, ss5 + 4 * (n - 1))
                            inst = None
                            for ic in range(ND):
                                for tc in range(4 * gg, 4 * gg + 4):
                                    inst = mm(
                                        p_f[:, tc * 512:(tc + 1) * 512],
                                        w1h[:, base + ic * 1024 + (ot % 8) * 128:
                                            base + ic * 1024 + (ot % 8) * 128 + 128],
                                        xqzT[:, ic * T + tc * 512:
                                             ic * T + (tc + 1) * 512],
                                        start=(ic == 0), stop=(ic == ND - 1))
                            inst.then_inc(MM5, 1)
                            cinc("MM5")
                bk.tensor(ft)

                def fs(s):
                    for n in range(64):
                        ot, gg = divmod(n, 2)
                        s.wait_ge(MM5, n + 1)
                        if n >= 2:
                            dwait(s, SG[n % 2], stoks[n - 2])
                            s.wait_ge(VS, vs5 + 2 * (n - 1))
                        else:
                            dwait(s, SG[n % 2], sg_base[n % 2])
                        for j in range(4):
                            tc = 4 * gg + j
                            si("activation",
                               ghst[:, (n % 2) * 2048 + j * 512:
                                    (n % 2) * 2048 + (j + 1) * 512],
                               p_f[:, tc * 512:(tc + 1) * 512], AF.Gelu,
                               scale=scl_b[:, 5:6], bias=c_b1c[:, ot:ot + 1])
                bk.scalar(fs)

                def fv(v):
                    for n in range(64):
                        v.wait_ge(SS, ss5 + 4 * (n + 1))
                        sl = ghst[:, (n % 2) * 2048:(n % 2 + 1) * 2048]
                        vi("tensor_reduce", gmn[:, n:n + 1], sl, AX, ALU.min)
                        vi("reduce_max", gmx[:, n:n + 1], sl, AX)
                bk.vector(fv)
            sg_p5 = [dtot(SG[0]), dtot(SG[1])]

        if PHLIM < 8:
            return nc
        with ExitStack() as ps:
            p_b6 = ps.enter_context(nc.psum_tensor("p_b6", [128, 512], F32))

            with nc.Block("p6a") as bk:
                def fv(v):
                    vi("tensor_reduce", st_t[:, 0:1], gmn, AX, ALU.min)
                    vi("reduce_max", st_t[:, 1:2], gmx, AX)
                    ts(st_t[:, 0:1], st_t[:, 0:1], NEG1)
                bk.vector(fv)
            vs6 = cget("VS")

            with nc.Block("p6b") as bk:
                def ft(t):
                    t.wait_ge(VS, vs6)
                    t.transpose(p_b6[0:1, 0:128], st_t[:, 1:2], c_idf[:, :])
                    t.transpose(p_b6[0:1, 128:256], st_t[:, 0:1],
                                c_idf[:, :]).then_inc(MM0, 1)
                    cinc("MM0")
                bk.tensor(ft)
                mm06 = cget("MM0")

                def fv(v):
                    v.wait_ge(MM0, mm06)
                    vi("reduce_max", cc_sb[0:1, 0:1], p_b6[0:1, 0:128], AX)
                    vi("reduce_max", cc_sb[0:1, 1:2], p_b6[0:1, 128:256], AX)
                bk.vector(fv)
            vs6b = cget("VS")

            with nc.Block("p6c") as bk:
                def fg(g):
                    g.wait_ge(VS, vs6b)
                    dput("CC", g, cc2_in, cc_sb[0:1, 0:2])
                    dwait(g, "CC", dtot("CC"))
                    g.collective_compute(
                        "AllReduce", ALU.max, replica_groups=[list(range(8))],
                        ins=[cc2_in], outs=[cc2_out]).then_inc(CCS, 1)
                    cinc("CCS")
                bk.gpsimd(fg)

        if PHLIM < 9:
            return nc
        with ExitStack() as ps:
            qb = ps.enter_context(nc.sbuf_tensor("qb", [128, 4 * D], BF16))
            kb = ps.enter_context(nc.sbuf_tensor("kb", [128, 4 * D], BF16))
            wst = ps.enter_context(nc.sbuf_tensor("wst", [128, 2 * D], BF16))
            p_S = ps.enter_context(nc.psum_tensor("p_S", [128, 4 * 1024], F32))

            with nc.Block("p7") as bk:
                vs7, ss7 = cget("VS"), cget("SS")
                ltoks, stoks = [], []
                sg_base = [dtot(SG[0]), dtot(SG[1])]

                def fsy(sy):
                    for ix in range(4):
                        dwait(sy, S4[ix], s4_p3[ix])
                    dwait(sy, S2[0], s2_p3[0])
                    dwait(sy, S2[1], s2_p3[1])
                    dwait(sy, SG[0], sg_p5[0])
                    dwait(sy, SG[1], sg_p5[1])
                    for grp in range(2):
                        for i_ in range(NT):
                            i = grp * NT + i_
                            if i >= 4:
                                sy.wait_ge(MMS, i - 3)
                            dput(LD4[i % 4], sy,
                                 qb[:, (i % 4) * D:(i % 4 + 1) * D],
                                 g_q[i_ * 128:(i_ + 1) * 128, :])
                            ltoks.append(dput(
                                LD4[i % 4], sy,
                                kb[:, (i % 4) * D:(i % 4 + 1) * D],
                                g_k[i_ * 128:(i_ + 1) * 128, :]))
                bk.sync(fsy)

                def fg(g):
                    for f1 in range(ND):
                        g.wait_ge(SS, ss7 + f1 + 1)
                        stoks.append(dput(SG[f1 % 2], g,
                                          g_w[f1 * 128:(f1 + 1) * 128, :],
                                          wst[:, (f1 % 2) * D:(f1 % 2 + 1) * D]))
                bk.gpsimd(fg)

                def ft(t):
                    for grp in range(2):
                        if grp == 1:
                            t.wait_ge(SS, ss7 + 4)
                        for i_ in range(NT):
                            i = grp * NT + i_
                            dwait(t, LD4[i % 4], ltoks[i])
                            inst = None
                            for j in range(4):
                                f1 = 4 * grp + j
                                for hh in range(2):
                                    inst = mm(
                                        p_S[:, j * 1024 + hh * 512:
                                            j * 1024 + (hh + 1) * 512],
                                        qb[:, (i % 4) * D + f1 * 128:
                                           (i % 4) * D + (f1 + 1) * 128],
                                        kb[:, (i % 4) * D + hh * 512:
                                           (i % 4) * D + (hh + 1) * 512],
                                        start=(i_ == 0), stop=(i_ == NT - 1))
                            inst.then_inc(MMS, 1)
                            cinc("MMS")
                bk.tensor(ft)

                def fs(s):
                    for grp in range(2):
                        for j in range(4):
                            f1 = 4 * grp + j
                            s.wait_ge(VS, vs7 + grp * 12 + 2 * (j + 1))
                            if f1 >= 2:
                                dwait(s, SG[f1 % 2], stoks[f1 - 2])
                            else:
                                dwait(s, SG[f1 % 2], sg_base[f1 % 2])
                            si("activation",
                               wst[:, (f1 % 2) * D:(f1 % 2 + 1) * D],
                               p_S[:, j * 1024:(j + 1) * 1024], AF.Exp,
                               scale=SCL, bias=nm_col[:, f1:f1 + 1],
                               accum_out=st_t[:, 16 + f1:17 + f1])
                bk.scalar(fs)

                def fv(v):
                    for grp in range(2):
                        v.wait_ge(MMS, NT * (grp + 1))
                        for j in range(4):
                            f1 = 4 * grp + j
                            vi("reduce_max", st_t[:, 8 + f1:9 + f1],
                               p_S[:, j * 1024:(j + 1) * 1024], AX)
                            ts(nm_col[:, f1:f1 + 1], st_t[:, 8 + f1:9 + f1],
                               NEGSCL)
                        for j in range(4):
                            f1 = 4 * grp + j
                            v.wait_ge(SS, ss7 + 4 * grp + j + 1)
                            vi("reciprocal", st_rr[:, f1:f1 + 1],
                               st_t[:, 16 + f1:17 + f1])
                bk.vector(fv)
            sg_p7 = [dtot(SG[0]), dtot(SG[1])]

        if PHLIM < 10:
            return nc
        with ExitStack() as ps:
            aT = ps.enter_context(nc.sbuf_tensor("aT", [128, ND * T], BF16))
            wT8 = ps.enter_context(nc.sbuf_tensor("wT8", [128, ND * D], BF16))
            vb = ps.enter_context(nc.sbuf_tensor("vb", [128, 2 * 4096], BF16))
            xn9 = ps.enter_context(nc.sbuf_tensor("xn9", [128, 4 * D], F32))
            sst9 = ps.enter_context(nc.sbuf_tensor("sst9", [128, 4 * D], F32))
            p_a = ps.enter_context(nc.psum_tensor("p_a", [128, 8 * 512], F32))

            with nc.Block("p8") as bk:
                ss8 = cget("SS")
                txk, vbt = [], []

                def fsy(sy):
                    dwait(sy, SG[0], sg_p7[0])
                    dwait(sy, SG[1], sg_p7[1])
                    for ix in range(4):
                        dwait(sy, S4[ix], s4_p3[ix])
                    dwait(sy, S2[0], s2_p3[0])
                    dwait(sy, S2[1], s2_p3[1])
                    for f2 in range(ND):
                        txk.append(dput("TX", sy, wT8[:, f2 * D:(f2 + 1) * D],
                                        g_w[0:D, f2 * 128:(f2 + 1) * 128],
                                        transpose=True))
                    for tc in range(ND):
                        if tc >= 2:
                            sy.wait_ge(MM8, 2 * (tc - 1))
                        for f2 in range(ND):
                            vbt.append(dput(
                                VSL[tc % 2], sy,
                                vb[:, (tc % 2) * 4096 + f2 * 512:
                                   (tc % 2) * 4096 + (f2 + 1) * 512],
                                g_vT[f2 * 128:(f2 + 1) * 128,
                                     tc * 512:(tc + 1) * 512]))
                bk.sync(fsy)

                def ft(t):
                    dwait(t, "TX", txk[-1])
                    for tc in range(ND):
                        dwait(t, VSL[tc % 2], vbt[tc * ND + ND - 1])
                        for gg in range(2):
                            n = 2 * tc + gg
                            if n >= 2:
                                t.wait_ge(SS, ss8 + 4 * (n - 1))
                            inst = None
                            for f2 in range(ND):
                                for jj in range(4):
                                    j = 4 * gg + jj
                                    inst = mm(
                                        p_a[:, j * 512:(j + 1) * 512],
                                        wT8[:, f2 * D + j * 128:
                                            f2 * D + (j + 1) * 128],
                                        vb[:, (tc % 2) * 4096 + f2 * 512:
                                           (tc % 2) * 4096 + (f2 + 1) * 512],
                                        start=(f2 == 0), stop=(f2 == ND - 1))
                            inst.then_inc(MM8, 1)
                            cinc("MM8")
                bk.tensor(ft)

                def fs(s):
                    for n in range(2 * ND):
                        tc, gg = divmod(n, 2)
                        s.wait_ge(MM8, n + 1)
                        for jj in range(4):
                            j = 4 * gg + jj
                            si("activation",
                               aT[:, j * T + tc * 512:j * T + (tc + 1) * 512],
                               p_a[:, j * 512:(j + 1) * 512], AF.Copy,
                               scale=st_rr[:, j:j + 1])
                bk.scalar(fs)

            with nc.Block("p9") as bk:
                vs9 = cget("VS")
                ss9 = cget("SS")
                txk9, ltoks, stoks = [], [], []
                sg_base9 = [dtot(s_) for s_ in SG4]

                def fsy(sy):
                    for jc in range(ND):
                        txk9.append(dput("TX", sy, wT8[:, jc * D:(jc + 1) * D],
                                         g_pw[0:D, jc * 128:(jc + 1) * 128],
                                         transpose=True))
                    for i in range(NT):
                        if i >= 4:
                            sy.wait_ge(VS, vs9 + 2 * (i - 3))
                        ltoks.append(dput(LD4[i % 4], sy,
                                          xn9[:, (i % 4) * D:(i % 4 + 1) * D],
                                          g_xn[i * 128:(i + 1) * 128, :]))
                bk.sync(fsy)

                def fg(g):
                    for i in range(NT):
                        g.wait_ge(VS, vs9 + 2 * (i + 1))
                        stoks.append(dput(SG4[i % 4], g,
                                          g_s[i * 128:(i + 1) * 128, :],
                                          sst9[:, (i % 4) * D:(i % 4 + 1) * D]))
                bk.gpsimd(fg)

                def ft(t):
                    dwait(t, "TX", txk9[-1])
                    t.wait_ge(SS, ss9)
                    for i in range(NT):
                        if i >= 2:
                            t.wait_ge(VS, vs9 + 2 * (i - 2) + 1)
                        sl = p_a[:, (i % 2) * 1024:(i % 2 + 1) * 1024]
                        inst = None
                        for jc in range(ND):
                            for hh in range(2):
                                inst = mm(sl[:, hh * 512:(hh + 1) * 512],
                                          aT[:, jc * T + i * 128:
                                             jc * T + (i + 1) * 128],
                                          wT8[:, jc * D + hh * 512:
                                              jc * D + (hh + 1) * 512],
                                          start=(jc == 0), stop=(jc == ND - 1))
                        inst.then_inc(MM9, 1)
                        cinc("MM9")
                bk.tensor(ft)

                def fv(v):
                    for i in range(NT):
                        v.wait_ge(MM9, i + 1)
                        dwait(v, LD4[i % 4], ltoks[i])
                        if i >= 4:
                            dwait(v, SG4[i % 4], stoks[i - 4])
                        else:
                            dwait(v, SG4[i % 4], sg_base9[i % 4])
                        sl = p_a[:, (i % 2) * 1024:(i % 2 + 1) * 1024]
                        vi("scalar_tensor_tensor",
                           sst9[:, (i % 4) * D:(i % 4 + 1) * D], sl,
                           st_al[:, i:i + 1],
                           xn9[:, (i % 4) * D:(i % 4 + 1) * D],
                           ALU.mult, ALU.add)
                        vi("tensor_add",
                           sst9[:, (i % 4) * D:(i % 4 + 1) * D],
                           sst9[:, (i % 4) * D:(i % 4 + 1) * D],
                           pb_bc[:, :])
                bk.vector(fv)
            sg_p9 = [dtot(s_) for s_ in SG4]

        if PHLIM < 11:
            return nc
        with ExitStack() as ps10a:
            tb1b = ps10a.enter_context(nc.sbuf_tensor("tb1b", [1, 1], I32))
            p_b10 = ps10a.enter_context(nc.psum_tensor("p_b10", [128, 8], F32))

            with nc.Block("p10a") as bk:
                def fg(g):
                    g.wait_ge(CCS, 2)
                    dput("CC", g, cc_sb[0:1, 0:2], cc2_out)
                bk.gpsimd(fg)
                cc_r2 = dtot("CC")

                def fv(v):
                    dwait(v, "CC", cc_r2)
                    vi("tensor_add", scal[0:1, 16:17], cc_sb[0:1, 0:1],
                       cc_sb[0:1, 1:2])
                    ts(scal[0:1, 17:18], scal[0:1, 16:17], INV255)
                    ts(scal[0:1, 17:18], scal[0:1, 17:18], np.float32(1e-8),
                       op0=ALU.max)
                    vi("reciprocal", scal[0:1, 18:19], scal[0:1, 17:18])
                    vi("tensor_mul", scal[0:1, 19:20], cc_sb[0:1, 1:2],
                       scal[0:1, 18:19])
                    vi("tensor_copy", tb1b[0:1, 0:1], scal[0:1, 19:20])
                    vi("tensor_copy", scal[0:1, 20:21], tb1b[0:1, 0:1])
                    ts(scal[0:1, 20:21], scal[0:1, 20:21], np.float32(-128.0),
                       op0=ALU.add)
                    ts(scal[0:1, 20:21], scal[0:1, 20:21], np.float32(-128.0),
                       op0=ALU.max)
                    ts(scal[0:1, 20:21], scal[0:1, 20:21], np.float32(127.0),
                       op0=ALU.min)
                    ts(scal[0:1, 21:22], scal[0:1, 20:21], NEG1,
                       np.float32(-128.0), ALU.mult, ALU.add)
                    ts(scal[0:1, 22:23], scal[0:1, 20:21], NEG1,
                       np.float32(127.0), ALU.mult, ALU.add)
                    vi("tensor_mul", scal[0:1, 23:24], scal[0:1, 17:18],
                       scal[0:1, 2:3])
                    vi("reciprocal", scal[0:1, 24:25], scal[0:1, 23:24])
                    ts(r_b2r[:, :], r_b2[:, :], scal[0:1, 24:25])
                    vi("tensor_copy", r_b2r16[:, :], r_b2r[:, :])
                bk.vector(fv)
            vs_10a = cget("VS")

            with nc.Block("p10c") as bk:
                def ft(t):
                    t.wait_ge(VS, vs_10a)
                    mm(p_b10[:, 0:4], ones_f[:, :], scal[0:1, 18:22],
                       start=True, stop=True)
                    mm(p_b10[:, 4:8], ones_f[:, :], scal[0:1, 22:26],
                       start=True, stop=True).then_inc(MM0, 1)
                    cinc("MM0")
                bk.tensor(ft)
                mm010 = cget("MM0")

                def fv(v):
                    v.wait_ge(MM0, mm010)
                    vi("tensor_copy", scl_b[:, 6:7], p_b10[:, 0:1])
                    vi("tensor_copy", scl_b[:, 7:8], p_b10[:, 3:4])
                    vi("tensor_copy", scl_b[:, 0:1], p_b10[:, 4:5])
                    vi("tensor_copy", scl_b[:, 1:2], p_b10[:, 5:6])
                bk.vector(fv)

        if PHLIM < 12:
            return nc
        with ExitStack() as ps:
            hqT = ps.enter_context(nc.sbuf_tensor("hqT", [128, NH * 512], BF16))
            w2T = ps.enter_context(nc.sbuf_tensor("w2T", [128, NH * D], BF16))
            ghld = ps.enter_context(nc.sbuf_tensor("ghld", [128, 6 * 512], F32))
            tbq = ps.enter_context(nc.sbuf_tensor("tbq", [128, 512], I32))
            xl10 = ps.enter_context(nc.sbuf_tensor("xl10", [128, 2 * D], F32))
            sl10 = ps.enter_context(nc.sbuf_tensor("sl10", [128, 2 * D], F32))
            gst = ps.enter_context(nc.sbuf_tensor("gst", [128, 2 * D], F32))
            ost = ps.enter_context(nc.sbuf_tensor("ost", [128, 2 * D], F32))
            p_o = ps.enter_context(nc.psum_tensor("p_o", [128, 4 * 1024], F32))

            with nc.Block("p10") as bk:
                vs10, ss10 = cget("VS"), cget("SS")
                mm0b = cget("MM0")
                NBLK = 8
                VPB = 72
                txk, lxs, ghtoks, otoks = [], [], [], []

                def q_done(i):
                    b_, o_ = divmod(i, NH)
                    return vs10 + b_ * VPB + 2 * (o_ + 1)

                def comb_done(i):
                    b_, t_ = divmod(i, 4)
                    return vs10 + b_ * VPB + 64 + 2 * (t_ + 1)

                def fsy(sy):
                    for oc in range(NH):
                        txk.append(dput("TX", sy, w2T[:, oc * D:(oc + 1) * D],
                                        g_w2[0:D, oc * 128:(oc + 1) * 128],
                                        transpose=True))
                    for ix in range(4):
                        dwait(sy, SG4[ix], sg_p9[ix])
                    for i in range(NT):
                        if i >= 2:
                            sy.wait_ge(VS, comb_done(i - 2))
                            otoks.append(dput(
                                OUTS[i % 2], sy,
                                out_hbm[(i - 2) * 128:(i - 1) * 128, :],
                                ost[:, (i % 2) * D:(i % 2 + 1) * D]))
                        dput(LD[i % 2], sy, xl10[:, (i % 2) * D:(i % 2 + 1) * D],
                             x_hbm[i * 128:(i + 1) * 128, :])
                        lxs.append(dput(LD[i % 2], sy,
                                        sl10[:, (i % 2) * D:(i % 2 + 1) * D],
                                        g_s[i * 128:(i + 1) * 128, :]))
                    for i in range(NT - 2, NT):
                        sy.wait_ge(VS, comb_done(i))
                        otoks.append(dput(OUTS[i % 2], sy,
                                          out_hbm[i * 128:(i + 1) * 128, :],
                                          ost[:, (i % 2) * D:(i % 2 + 1) * D]))
                    dwait(sy, OUTS[0], dtot(OUTS[0]))
                    dwait(sy, OUTS[1], dtot(OUTS[1]))
                bk.sync(fsy)

                def fg(g):
                    for ix in range(4):
                        dwait(g, SG4[ix], sg_p9[ix])
                    for blk_ in range(NBLK):
                        for oc in range(NH):
                            i = blk_ * NH + oc
                            if i >= 6:
                                g.wait_ge(VS, q_done(i - 6))
                            ghtoks.append(dput(
                                GH6[i % 6], g,
                                ghld[:, (i % 6) * 512:(i % 6 + 1) * 512],
                                g_gh[oc * 128:(oc + 1) * 128,
                                     blk_ * 512:(blk_ + 1) * 512]))
                bk.gpsimd(fg)

                def fv(v):
                    for blk_ in range(NBLK):
                        for oc in range(NH):
                            i = blk_ * NH + oc
                            dwait(v, GH6[i % 6], ghtoks[i])
                            if blk_ >= 1 and oc == 0:
                                v.wait_ge(MM0, mm0b + 4 * blk_)
                            ts(tbq[:, :],
                               ghld[:, (i % 6) * 512:(i % 6 + 1) * 512],
                               scl_b[:, 6:7], scl_b[:, 7:8], ALU.mult, ALU.max)
                            ts(hqT[:, oc * 512:(oc + 1) * 512], tbq[:, :],
                               scl_b[:, 0:1], op0=ALU.min)
                        for tt in range(4):
                            i = blk_ * 4 + tt
                            v.wait_ge(SS, ss10 + i + 1)
                            dwait(v, LD[i % 2], lxs[i])
                            if i >= 2:
                                dwait(v, OUTS[i % 2], otoks[i - 2])
                            vi("tensor_mul",
                               gst[:, (i % 2) * D:(i % 2 + 1) * D],
                               gst[:, (i % 2) * D:(i % 2 + 1) * D],
                               sl10[:, (i % 2) * D:(i % 2 + 1) * D])
                            vi("tensor_add",
                               ost[:, (i % 2) * D:(i % 2 + 1) * D],
                               gst[:, (i % 2) * D:(i % 2 + 1) * D],
                               xl10[:, (i % 2) * D:(i % 2 + 1) * D])
                bk.vector(fv)

                def ft(t):
                    dwait(t, "TX", txk[-1])
                    for blk_ in range(NBLK):
                        if blk_ >= 1:
                            t.wait_ge(SS, ss10 + 4 * blk_)
                        for oc in range(NH):
                            first = True
                            for tt in range(4):
                                for hh in range(2):
                                    inst = mm(
                                        p_o[:, tt * 1024 + hh * 512:
                                            tt * 1024 + (hh + 1) * 512],
                                        hqT[:, oc * 512 + tt * 128:
                                            oc * 512 + (tt + 1) * 128],
                                        w2T[:, oc * D + hh * 512:
                                            oc * D + (hh + 1) * 512],
                                        start=(oc == 0), stop=False)
                                    if first:
                                        inst._wait_ge(VS,
                                                      q_done(blk_ * NH + oc))
                                        first = False
                        for tt in range(4):
                            inst = None
                            for hh in range(2):
                                inst = mm(
                                    p_o[:, tt * 1024 + hh * 512:
                                        tt * 1024 + (hh + 1) * 512],
                                    ones_b[:, :],
                                    r_b2r16[:, hh * 512:(hh + 1) * 512],
                                    start=False, stop=True)
                            inst.then_inc(MM0, 1)
                            cinc("MM0")
                bk.tensor(ft)

                def fs(s):
                    for blk_ in range(NBLK):
                        for tt in range(4):
                            i = blk_ * 4 + tt
                            s.wait_ge(MM0, mm0b + i + 1)
                            if i >= 2:
                                s.wait_ge(VS, comb_done(i - 2))
                            si("activation",
                               gst[:, (i % 2) * D:(i % 2 + 1) * D],
                               p_o[:, tt * 1024:(tt + 1) * 1024], AF.Silu,
                               scale=scl_b[:, 1:2])
                bk.scalar(fs)

    return nc


IDENT = np.eye(128, dtype=np.float32)


def kernel(L0_attn_x, L0_attn_norm_w, L0_attn_norm_b, L0_attn_qkv_w, L0_attn_qkv_b,
           L0_attn_proj_w, L0_attn_proj_b, L0_ffn_W1, L0_ffn_b1, L0_ffn_W2,
           L0_ffn_b2):
    x = np.ascontiguousarray(np.asarray(L0_attn_x, dtype=np.float32))
    common = {
        "norm_w": np.asarray(L0_attn_norm_w, np.float32).reshape(1, D),
        "norm_b": np.asarray(L0_attn_norm_b, np.float32).reshape(1, D),
        "qkv_w": np.ascontiguousarray(np.asarray(L0_attn_qkv_w, np.float32)),
        "qkv_b": np.asarray(L0_attn_qkv_b, np.float32).reshape(1, O3),
        "proj_w": np.ascontiguousarray(np.asarray(L0_attn_proj_w, np.float32)),
        "proj_b": np.asarray(L0_attn_proj_b, np.float32).reshape(1, D),
        "w1": np.ascontiguousarray(np.asarray(L0_ffn_W1, np.float32)),
        "b1": np.asarray(L0_ffn_b1, np.float32).reshape(1, H),
        "w2": np.ascontiguousarray(np.asarray(L0_ffn_W2, np.float32)),
        "b2": np.asarray(L0_ffn_b2, np.float32).reshape(1, D),
        "identf": IDENT,
    }
    in_maps = [dict(common, x=x[b]) for b in range(8)]
    nc = build_nc()
    res = run_bass_kernel_spmd(nc, in_maps, core_ids=list(range(8)))
    out = np.stack([np.asarray(res.results[b]["out"]) for b in range(8)], axis=0)
    return out.astype(np.float32)


if __name__ == "__main__":
    import reference as R
    ins = {k: np.asarray(v) for k, v in R.setup_inputs(0).items()}
    out = kernel(**ins)
    print(out.shape, out.dtype)
